# version 03
# baseline (speedup 1.0000x reference)
import numpy as np
import concourse.bass as bass
import concourse.mybir as mybir
from concourse.bass_utils import run_bass_kernel_spmd

F32 = mybir.dt.float32
BF16 = mybir.dt.bfloat16
F8E5 = mybir.dt.float8e5
AF = mybir.ActivationFunctionType
ALU = mybir.AluOpType
AX = mybir.AxisListType


class P:

    def __init__(self, nc, ktile=None):
        self.nc = nc
        self.ops = []
        self.ktile = ktile
        self.ksrc = None
        self.kmap = {}
        self.kvals = []

    def op(self, eng, fn, inc=1):
        self.ops.append((eng, fn, inc))

    def konst(self, v):
        v = float(v)
        if v not in self.kmap:
            i = len(self.kmap)
            assert i < 16 and self.ktile.shape[1] >= 16
            if i == 0:
                self.dma('sync', self.ktile[:, 0:16], self.ksrc[:, 0:16])
            self.kmap[v] = self.ktile[:, i:i + 1]
            self.kvals.append(v)
        return self.kmap[v]

    def dma(self, queue, out, in_):
        self.op(queue, lambda e: e.dma_start(out=out, in_=in_), 16)

    def emit(self):
        nc = self.nc
        nc._kvals = list(self.kvals)
        befores = []
        tot = 0
        for (_, _, inc) in self.ops:
            befores.append(tot)
            tot += inc
        self.total = tot
        with nc.semaphore("tk") as tk, nc.Block() as block:
            def emit_for(name):
                def inner(eng):
                    last_wait = -1
                    for (en, fn, inc), before in zip(self.ops, befores):
                        if en != name:
                            continue
                        if before > 0 and before > last_wait:
                            eng.wait_ge(tk, before)
                            last_wait = before
                        inst = fn(eng)
                        inst.then_inc(tk, inc)
                    if name == 'sync':
                        eng.wait_ge(tk, tot)
                    return None
                return inner
            used = {en for (en, _, _) in self.ops} | {'sync'}
            if 'sync' in used:
                block.sync(emit_for('sync'))
            if 'tensor' in used:
                block.tensor(emit_for('tensor'))
            if 'vector' in used:
                block.vector(emit_for('vector'))
            if 'scalar' in used:
                block.scalar(emit_for('scalar'))
            if 'gpsimd' in used:
                block.gpsimd(emit_for('gpsimd'))
        return tot


def ceil_div(a, b):
    return (a + b - 1) // b


def run8(nc, in_maps, **kw):
    return run_bass_kernel_spmd(nc, in_maps, core_ids=list(range(8)), **kw)


def build_matmul(K, M, N, act='none', bias=False, mulT=False, addT=False,
                 f8=False, out_bf16_round=False, clip50=False):
    nc = bass.Bass("TRN2", target_bir_lowering=False)
    KON = nc.dram_tensor("KONST", [128, 16], F32, kind="ExternalInput").ap()
    W = nc.dram_tensor("W", [K, M], F32, kind="ExternalInput").ap()
    R = nc.dram_tensor("R", [K, N], F32, kind="ExternalInput").ap()
    OUT = nc.dram_tensor("OUT", [M, N], F32, kind="ExternalOutput").ap()
    Bv = nc.dram_tensor("B", [M, 1], F32, kind="ExternalInput").ap() if bias else None
    MT = nc.dram_tensor("MT", [M, N], F32, kind="ExternalInput").ap() if mulT else None
    AT = nc.dram_tensor("AT", [M, N], F32, kind="ExternalInput").ap() if addT else None

    NC = 512
    kch = ceil_div(K, 128)
    mch = ceil_div(M, 128)
    nch = ceil_div(N, NC)

    p = P(nc)
    with (
        nc.sbuf_tensor("Wt", [128, kch * 128], F32) as Wt,
        nc.sbuf_tensor("Rt", [128, kch * NC], F32) as Rt,
        nc.sbuf_tensor("Wq", [128, kch * 128], F8E5) as Wq,
        nc.sbuf_tensor("Rq", [128, kch * NC], F8E5) as Rq,
        nc.sbuf_tensor("bt", [128, 4], F32) as bt,
        nc.sbuf_tensor("bq", [128, 1], F8E5) as bq,
        nc.sbuf_tensor("ot", [128, NC], F32) as ot,
        nc.sbuf_tensor("t1", [128, NC], F32) as t1,
        nc.sbuf_tensor("t2", [128, NC], F32) as t2,
        nc.sbuf_tensor("mt", [128, NC], F32) as mt,
        nc.sbuf_tensor("rb", [128, NC], BF16) as rb,
        nc.sbuf_tensor("kt", [128, 16], F32) as kt,
        nc.psum_tensor("ps", [128, NC], F32) as ps,
    ):
        p.ktile = kt
        p.ksrc = KON
        for ni in range(nch):
            n0 = ni * NC
            nw = min(NC, N - n0)
            for j in range(kch):
                k0 = j * 128
                kw = min(128, K - k0)
                p.dma('sync', Rt[0:kw, j * NC:j * NC + nw], R[k0:k0 + kw, n0:n0 + nw])
                if f8:
                    p.op('vector', lambda e, a=Rq[0:kw, j * NC:j * NC + nw],
                         b=Rt[0:kw, j * NC:j * NC + nw]: e.tensor_copy(a, b))
            for mi in range(mch):
                m0 = mi * 128
                mw = min(128, M - m0)
                for j in range(kch):
                    k0 = j * 128
                    kw = min(128, K - k0)
                    p.dma('sync', Wt[0:kw, j * 128:j * 128 + mw], W[k0:k0 + kw, m0:m0 + mw])
                    if f8:
                        p.op('vector', lambda e, a=Wq[0:kw, j * 128:j * 128 + mw],
                             b=Wt[0:kw, j * 128:j * 128 + mw]: e.tensor_copy(a, b))
                if bias:
                    p.dma('sync', bt[0:mw, 0:1], Bv[m0:m0 + mw, :])
                    if f8:
                        p.op('vector', lambda e, a=bq[0:mw, 0:1], b=bt[0:mw, 0:1]:
                             e.tensor_copy(a, b))
                        p.op('vector', lambda e, a=bt[0:mw, 0:1], b=bq[0:mw, 0:1]:
                             e.tensor_copy(a, b))
                for j in range(kch):
                    k0 = j * 128
                    kw = min(128, K - k0)
                    lhs = (Wq if f8 else Wt)[0:kw, j * 128:j * 128 + mw]
                    rhs = (Rq if f8 else Rt)[0:kw, j * NC:j * NC + nw]
                    p.op('tensor', lambda e, o=ps[0:mw, 0:nw], l=lhs, r=rhs,
                         s=(j == 0), t=(j == kch - 1):
                         e.matmul(o, l, r, start=s, stop=t))
                src = ps[0:mw, 0:nw]
                dst = ot[0:mw, 0:nw]
                bap = bt[0:mw, 0:1] if bias else p.konst(0.0)[0:mw, 0:1]
                if clip50:
                    p.op('vector', lambda e, o=t1[0:mw, 0:nw], i=src, b=bap:
                         e.tensor_scalar(o, i, b, -50.0, ALU.add, ALU.max))
                    p.op('vector', lambda e, o=t1[0:mw, 0:nw], i=t1[0:mw, 0:nw]:
                         e.tensor_scalar(o, i, 50.0, None, ALU.min))
                    src = t1[0:mw, 0:nw]
                    bap = p.konst(0.0)[0:mw, 0:1]
                if act == 'none':
                    p.op('vector', lambda e, o=dst, i=src, b=bap:
                         e.tensor_scalar(o, i, b, None, ALU.add))
                elif act in ('relu', 'silu', 'sigmoid'):
                    fmap = {'relu': AF.Relu, 'silu': AF.Silu, 'sigmoid': AF.Sigmoid}
                    p.op('scalar', lambda e, o=dst, i=src, b=bap, f=fmap[act]:
                         e.activation(o, i, f, bias=b))
                elif act == 'softplus':
                    p.op('vector', lambda e, o=t1[0:mw, 0:nw], i=src, b=bap:
                         e.tensor_scalar(o, i, b, None, ALU.add))
                    p.op('scalar', lambda e, o=t2[0:mw, 0:nw], i=t1[0:mw, 0:nw],
                         z=p.konst(0.0)[0:mw, 0:1]:
                         e.activation(o, i, AF.Abs, bias=z))
                    p.op('scalar', lambda e, o=t2[0:mw, 0:nw], i=t2[0:mw, 0:nw],
                         s=p.konst(-1.0)[0:mw, 0:1], z=p.konst(0.0)[0:mw, 0:1]:
                         e.activation(o, i, AF.Exp, bias=z, scale=s))
                    p.op('scalar', lambda e, o=t2[0:mw, 0:nw], i=t2[0:mw, 0:nw],
                         b=p.konst(1.0)[0:mw, 0:1]:
                         e.activation(o, i, AF.Ln, bias=b))
                    p.op('vector', lambda e, o=t1[0:mw, 0:nw], i=t1[0:mw, 0:nw]:
                         e.tensor_scalar(o, i, 0.0, None, ALU.max))
                    p.op('vector', lambda e, o=dst, a=t1[0:mw, 0:nw], b=t2[0:mw, 0:nw]:
                         e.tensor_add(o, a, b))
                else:
                    raise ValueError(act)
                if out_bf16_round:
                    p.op('vector', lambda e, o=rb[0:mw, 0:nw], i=dst:
                         e.tensor_copy(o, i))
                    p.op('vector', lambda e, o=dst, i=rb[0:mw, 0:nw]:
                         e.tensor_copy(o, i))
                if mulT:
                    p.dma('sync', mt[0:mw, 0:nw], MT[m0:m0 + mw, n0:n0 + nw])
                    p.op('vector', lambda e, o=dst, a=dst, b=mt[0:mw, 0:nw]:
                         e.tensor_mul(o, a, b))
                if addT:
                    p.dma('sync', mt[0:mw, 0:nw], AT[m0:m0 + mw, n0:n0 + nw])
                    p.op('vector', lambda e, o=dst, a=dst, b=mt[0:mw, 0:nw]:
                         e.tensor_add(o, a, b))
                p.dma('sync', OUT[m0:m0 + mw, n0:n0 + nw], dst)
        p.emit()
    return nc


def build_lnorm0(T, D):
    nc = bass.Bass("TRN2", target_bir_lowering=False)
    KON = nc.dram_tensor("KONST", [128, 16], F32, kind="ExternalInput").ap()
    X = nc.dram_tensor("X", [T, D], F32, kind="ExternalInput").ap()
    GB = nc.dram_tensor("GB", [128, 2 * D], F32, kind="ExternalInput").ap()
    XN0 = nc.dram_tensor("XN0", [T, D], F32, kind="ExternalOutput").ap()
    XIN = nc.dram_tensor("XIN", [T, D], F32, kind="ExternalOutput").ap()
    tch = ceil_div(T, 128)
    p = P(nc)
    with (
        nc.sbuf_tensor("xt", [128, D], F32) as xt,
        nc.sbuf_tensor("gb", [128, 2 * D], F32) as gb,
        nc.sbuf_tensor("sq", [128, D], F32) as sq,
        nc.sbuf_tensor("xn", [128, D], F32) as xn,
        nc.sbuf_tensor("xa", [128, D], F32) as xa,
        nc.sbuf_tensor("c1", [128, 8], F32) as c1,
        nc.sbuf_tensor("kt", [128, 16], F32) as kt,
    ):
        p.ktile = kt
        p.ksrc = KON
        p.dma('sync', gb[:, :], GB[:, :])
        for ti in range(tch):
            t0 = ti * 128
            p.dma('sync', xt[:, :], X[t0:t0 + 128, :])
            for (src, dst) in ((xt, xn), (xn, xa)):
                p.op('vector', lambda e, o=c1[:, 0:1], i=src[:, :]:
                     e.reduce_sum(o, i, axis=AX.X))
                p.op('vector', lambda e, o=c1[:, 0:1], i=c1[:, 0:1]:
                     e.tensor_scalar(o, i, 1.0 / D, None, ALU.mult))
                p.op('scalar', lambda e, o=sq[:, :], i=src[:, :], z=p.konst(0.0):
                     e.activation(o, i, AF.Square, bias=z))
                p.op('vector', lambda e, o=c1[:, 1:2], i=sq[:, :]:
                     e.reduce_sum(o, i, axis=AX.X))
                p.op('vector', lambda e, o=c1[:, 1:2], i=c1[:, 1:2]:
                     e.tensor_scalar(o, i, 1.0 / D, None, ALU.mult))
                p.op('vector', lambda e, o=c1[:, 2:3], a=c1[:, 0:1], b=c1[:, 0:1]:
                     e.tensor_mul(o, a, b))
                p.op('vector', lambda e, o=c1[:, 1:2], a=c1[:, 1:2], b=c1[:, 2:3]:
                     e.tensor_sub(o, a, b))
                p.op('scalar', lambda e, o=c1[:, 3:4], i=c1[:, 1:2], b5=p.konst(1e-5):
                     e.activation(o, i, AF.Ln, bias=b5))
                p.op('scalar', lambda e, o=c1[:, 4:5], i=c1[:, 3:4], s=p.konst(-0.5),
                     z=p.konst(0.0):
                     e.activation(o, i, AF.Exp, bias=z, scale=s))
                p.op('vector', lambda e, o=dst[:, :], i=src[:, :], m=c1[:, 0:1],
                     r=c1[:, 4:5]:
                     e.tensor_scalar(o, i, m, r, ALU.subtract, ALU.mult))
            p.op('vector', lambda e, o=xa[:, :], a=xa[:, :], b=gb[:, 0:D]:
                 e.tensor_mul(o, a, b))
            p.op('vector', lambda e, o=xa[:, :], a=xa[:, :], b=gb[:, D:2 * D]:
                 e.tensor_add(o, a, b))
            p.dma('sync', XN0[t0:t0 + 128, :], xn[:, :])
            p.dma('sync', XIN[t0:t0 + 128, :], xa[:, :])
        p.emit()
    return nc


def build_lnorm(T, D):
    nc = bass.Bass("TRN2", target_bir_lowering=False)
    KON = nc.dram_tensor("KONST", [128, 16], F32, kind="ExternalInput").ap()
    X = nc.dram_tensor("X", [T, D], F32, kind="ExternalInput").ap()
    XN = nc.dram_tensor("XN", [T, D], F32, kind="ExternalOutput").ap()
    tch = ceil_div(T, 128)
    p = P(nc)
    with (
        nc.sbuf_tensor("xt", [128, D], F32) as xt,
        nc.sbuf_tensor("sq", [128, D], F32) as sq,
        nc.sbuf_tensor("xn", [128, D], F32) as xn,
        nc.sbuf_tensor("c1", [128, 8], F32) as c1,
        nc.sbuf_tensor("kt", [128, 16], F32) as kt,
    ):
        p.ktile = kt
        p.ksrc = KON
        for ti in range(tch):
            t0 = ti * 128
            p.dma('sync', xt[:, :], X[t0:t0 + 128, :])
            p.op('vector', lambda e, o=c1[:, 0:1], i=xt[:, :]:
                 e.reduce_sum(o, i, axis=AX.X))
            p.op('vector', lambda e, o=c1[:, 0:1], i=c1[:, 0:1]:
                 e.tensor_scalar(o, i, 1.0 / D, None, ALU.mult))
            p.op('scalar', lambda e, o=sq[:, :], i=xt[:, :], z=p.konst(0.0):
                 e.activation(o, i, AF.Square, bias=z))
            p.op('vector', lambda e, o=c1[:, 1:2], i=sq[:, :]:
                 e.reduce_sum(o, i, axis=AX.X))
            p.op('vector', lambda e, o=c1[:, 1:2], i=c1[:, 1:2]:
                 e.tensor_scalar(o, i, 1.0 / D, None, ALU.mult))
            p.op('vector', lambda e, o=c1[:, 2:3], a=c1[:, 0:1], b=c1[:, 0:1]:
                 e.tensor_mul(o, a, b))
            p.op('vector', lambda e, o=c1[:, 1:2], a=c1[:, 1:2], b=c1[:, 2:3]:
                 e.tensor_sub(o, a, b))
            p.op('scalar', lambda e, o=c1[:, 3:4], i=c1[:, 1:2], b5=p.konst(1e-5):
                 e.activation(o, i, AF.Ln, bias=b5))
            p.op('scalar', lambda e, o=c1[:, 4:5], i=c1[:, 3:4], s=p.konst(-0.5),
                 z=p.konst(0.0):
                 e.activation(o, i, AF.Exp, bias=z, scale=s))
            p.op('vector', lambda e, o=xn[:, :], i=xt[:, :], m=c1[:, 0:1],
                 r=c1[:, 4:5]:
                 e.tensor_scalar(o, i, m, r, ALU.subtract, ALU.mult))
            p.dma('sync', XN[t0:t0 + 128, :], xn[:, :])
        p.emit()
    return nc


def build_conv(C, T, KK, act='none', bias=False):
    nc = bass.Bass("TRN2", target_bir_lowering=False)
    KON = nc.dram_tensor("KONST", [128, 16], F32, kind="ExternalInput").ap()
    X = nc.dram_tensor("X", [C, T], F32, kind="ExternalInput").ap()
    W = nc.dram_tensor("W", [C, KK], F32, kind="ExternalInput").ap()
    Bv = nc.dram_tensor("B", [C, 1], F32, kind="ExternalInput").ap() if bias else None
    OUT = nc.dram_tensor("OUT", [C, T], F32, kind="ExternalOutput").ap()
    cch = ceil_div(C, 128)
    p = P(nc)
    with (
        nc.sbuf_tensor("xt", [128, KK - 1 + T], F32) as xt,
        nc.sbuf_tensor("wt", [128, KK], F32) as wt,
        nc.sbuf_tensor("bt", [128, 1], F32) as bt,
        nc.sbuf_tensor("a1", [128, T], F32) as a1,
        nc.sbuf_tensor("a2", [128, T], F32) as a2,
        nc.sbuf_tensor("kt", [128, 16], F32) as kt,
    ):
        p.ktile = kt
        p.ksrc = KON
        for ci in range(cch):
            c0 = ci * 128
            p.op('vector', lambda e, o=xt[:, 0:KK - 1]: e.memset(o, 0.0))
            p.dma('sync', xt[:, KK - 1:], X[c0:c0 + 128, :])
            p.dma('sync', wt[:, :], W[c0:c0 + 128, :])
            if bias:
                p.dma('sync', bt[:, :], Bv[c0:c0 + 128, :])
            acc = [a1, a2]
            p.op('vector', lambda e, o=a1[:, :], i=xt[:, 0:T], w=wt[:, 0:1]:
                 e.tensor_scalar(o, i, w, None, ALU.mult))
            cur = 0
            for k in range(1, KK):
                src, dst = acc[cur], acc[1 - cur]
                p.op('vector', lambda e, o=dst[:, :], i=xt[:, k:k + T],
                     w=wt[:, k:k + 1], ac=src[:, :]:
                     e.scalar_tensor_tensor(o, i, w, ac, ALU.mult, ALU.add))
                cur = 1 - cur
            fin = acc[cur]
            bap = bt[:, 0:1] if bias else p.konst(0.0)
            if act == 'silu':
                p.op('scalar', lambda e, o=fin[:, :], i=fin[:, :], b=bap:
                     e.activation(o, i, AF.Silu, bias=b))
            elif bias:
                p.op('vector', lambda e, o=fin[:, :], i=fin[:, :], b=bap:
                     e.tensor_scalar(o, i, b, None, ALU.add))
            p.dma('sync', OUT[c0:c0 + 128, :], fin[:, :])
        p.emit()
    return nc


def build_scan(D, T, N):
    nc = bass.Bass("TRN2", target_bir_lowering=False)
    KON = nc.dram_tensor("KONST", [128, 16], F32, kind="ExternalInput").ap()
    DL = nc.dram_tensor("DL", [D, T], F32, kind="ExternalInput").ap()
    U = nc.dram_tensor("U", [D, T], F32, kind="ExternalInput").ap()
    AL = nc.dram_tensor("AL", [D, N], F32, kind="ExternalInput").ap()
    BB = nc.dram_tensor("BB", [N, 128, T], F32, kind="ExternalInput").ap()
    CC = nc.dram_tensor("CC", [N, 128, T], F32, kind="ExternalInput").ap()
    Y = nc.dram_tensor("Y", [D, T], F32, kind="ExternalOutput").ap()
    dch = ceil_div(D, 128)
    TC = 1024
    tch = ceil_div(T, TC)
    p = P(nc)
    with (
        nc.sbuf_tensor("bb", [128, N * TC], F32) as bb,
        nc.sbuf_tensor("cc", [128, N * TC], F32) as cc,
        nc.sbuf_tensor("dl", [128, TC], F32) as dl,
        nc.sbuf_tensor("ut", [128, TC], F32) as ut,
        nc.sbuf_tensor("du", [128, TC], F32) as du,
        nc.sbuf_tensor("da", [128, TC], F32) as da,
        nc.sbuf_tensor("bt", [128, TC], F32) as bt,
        nc.sbuf_tensor("ht", [128, TC], F32) as ht,
        nc.sbuf_tensor("zt", [128, TC], F32) as zt,
        nc.sbuf_tensor("y1", [128, TC], F32) as y1,
        nc.sbuf_tensor("y2", [128, TC], F32) as y2,
        nc.sbuf_tensor("at", [128, N], F32) as at,
        nc.sbuf_tensor("st", [128, N * dch], F32) as st,
        nc.sbuf_tensor("kt", [128, 16], F32) as kt,
    ):
        p.ktile = kt
        p.ksrc = KON
        p.op('vector', lambda e, o=st[:, :]: e.memset(o, 0.0))
        for tc in range(tch):
            t0 = tc * TC
            for n in range(N):
                p.dma('sync', bb[:, n * TC:(n + 1) * TC], BB[n, :, t0:t0 + TC])
                p.dma('sync', cc[:, n * TC:(n + 1) * TC], CC[n, :, t0:t0 + TC])
            for di in range(dch):
                d0 = di * 128
                p.dma('sync', dl[:, :], DL[d0:d0 + 128, t0:t0 + TC])
                p.dma('sync', ut[:, :], U[d0:d0 + 128, t0:t0 + TC])
                p.dma('sync', at[:, :], AL[d0:d0 + 128, :])
                p.op('scalar', lambda e, o=at[:, :], i=at[:, :], z=p.konst(0.0):
                     e.activation(o, i, AF.Exp, bias=z))
                p.op('vector', lambda e, o=at[:, :], i=at[:, :]:
                     e.tensor_scalar(o, i, -1.0, None, ALU.mult))
                p.op('vector', lambda e, o=du[:, :], a=dl[:, :], b=ut[:, :]:
                     e.tensor_mul(o, a, b))
                ys = [y1, y2]
                for n in range(N):
                    p.op('scalar', lambda e, o=da[:, :], i=dl[:, :], s=at[:, n:n + 1],
                         z=p.konst(0.0):
                         e.activation(o, i, AF.Exp, bias=z, scale=s))
                    p.op('vector', lambda e, o=bt[:, :], a=du[:, :],
                         b=bb[:, n * TC:(n + 1) * TC]:
                         e.tensor_mul(o, a, b))
                    p.op('vector', lambda e, o=ht[:, :], d0_=da[:, :], d1=bt[:, :],
                         ini=st[:, di * N + n:di * N + n + 1]:
                         e.tensor_tensor_scan(o, d0_, d1, ini, ALU.mult, ALU.add))
                    p.op('vector', lambda e, o=st[:, di * N + n:di * N + n + 1],
                         i=ht[:, TC - 1:TC]: e.tensor_copy(o, i))
                    p.op('vector', lambda e, o=zt[:, :], a=ht[:, :],
                         b=cc[:, n * TC:(n + 1) * TC]:
                         e.tensor_mul(o, a, b))
                    if n == 0:
                        p.op('vector', lambda e, o=ys[0][:, :], i=zt[:, :]:
                             e.tensor_copy(o, i))
                    else:
                        src, dst = ys[(n + 1) % 2], ys[n % 2]
                        p.op('vector', lambda e, o=dst[:, :], a=src[:, :], b=zt[:, :]:
                             e.tensor_add(o, a, b))
                fin = ys[(N - 1) % 2]
                p.dma('sync', Y[d0:d0 + 128, t0:t0 + TC], fin[:, :])
        p.emit()
    return nc


def build_mamba_out(D, T):
    nc = bass.Bass("TRN2", target_bir_lowering=False)
    KON = nc.dram_tensor("KONST", [128, 16], F32, kind="ExternalInput").ap()
    Yi = nc.dram_tensor("Y", [D, T], F32, kind="ExternalInput").ap()
    U = nc.dram_tensor("U", [D, T], F32, kind="ExternalInput").ap()
    Dv = nc.dram_tensor("DV", [D, 1], F32, kind="ExternalInput").ap()
    Z = nc.dram_tensor("Z", [D, T], F32, kind="ExternalInput").ap()
    OUT = nc.dram_tensor("OUT", [D, T], F32, kind="ExternalOutput").ap()
    dch = ceil_div(D, 128)
    p = P(nc)
    with (
        nc.sbuf_tensor("yt", [128, T], F32) as yt,
        nc.sbuf_tensor("ut", [128, T], F32) as ut,
        nc.sbuf_tensor("zt", [128, T], F32) as zt,
        nc.sbuf_tensor("dv", [128, 1], F32) as dv,
        nc.sbuf_tensor("kt", [128, 16], F32) as kt,
    ):
        p.ktile = kt
        p.ksrc = KON
        for di in range(dch):
            d0 = di * 128
            p.dma('sync', yt[:, :], Yi[d0:d0 + 128, :])
            p.dma('sync', ut[:, :], U[d0:d0 + 128, :])
            p.dma('sync', zt[:, :], Z[d0:d0 + 128, :])
            p.dma('sync', dv[:, :], Dv[d0:d0 + 128, :])
            p.op('vector', lambda e, o=yt[:, :], i=ut[:, :], w=dv[:, 0:1],
                 ac=yt[:, :]:
                 e.scalar_tensor_tensor(o, i, w, ac, ALU.mult, ALU.add))
            p.op('scalar', lambda e, o=zt[:, :], i=zt[:, :], z=p.konst(0.0):
                 e.activation(o, i, AF.Silu, bias=z))
            p.op('vector', lambda e, o=yt[:, :], a=yt[:, :], b=zt[:, :]:
                 e.tensor_mul(o, a, b))
            p.dma('sync', OUT[d0:d0 + 128, :], yt[:, :])
        p.emit()
    return nc


def build_combine(M, N, k):
    nc = bass.Bass("TRN2", target_bir_lowering=False)
    KON = nc.dram_tensor("KONST", [128, 16], F32, kind="ExternalInput").ap()
    Xs = [nc.dram_tensor(f"X{i}", [M, N], F32, kind="ExternalInput").ap()
          for i in range(k)]
    OUT = nc.dram_tensor("OUT", [M, N], F32, kind="ExternalOutput").ap()
    mch = ceil_div(M, 128)
    p = P(nc)
    with (
        nc.sbuf_tensor("t0", [128, N], F32) as t0_,
        nc.sbuf_tensor("t1", [128, N], F32) as t1_,
    ):
        for mi in range(mch):
            m0 = mi * 128
            p.dma('sync', t0_[:, :], Xs[0][m0:m0 + 128, :])
            for i in range(1, k):
                p.dma('sync', t1_[:, :], Xs[i][m0:m0 + 128, :])
                p.op('vector', lambda e, o=t0_[:, :], a=t0_[:, :], b=t1_[:, :]:
                     e.tensor_add(o, a, b))
            p.dma('sync', OUT[m0:m0 + 128, :], t0_[:, :])
        p.emit()
    return nc


def kinput(nc):
    vals = getattr(nc, '_kvals', [])
    a = np.zeros(16, np.float32)
    a[:len(vals)] = vals
    return np.tile(a[None, :], (128, 1))


def build_l2scale(T, NH, DK):
    D = NH * DK
    nc = bass.Bass("TRN2", target_bir_lowering=False)
    KON = nc.dram_tensor("KONST", [128, 16], F32, kind="ExternalInput").ap()
    QT = nc.dram_tensor("QT", [T, D], F32, kind="ExternalInput").ap()
    KT = nc.dram_tensor("KT", [T, D], F32, kind="ExternalInput").ap()
    VT = nc.dram_tensor("VT", [T, D], F32, kind="ExternalInput").ap()
    BETA = nc.dram_tensor("BETA", [T, NH], F32, kind="ExternalInput").ap()
    QN = nc.dram_tensor("QN", [T, D], F32, kind="ExternalOutput").ap()
    KN = nc.dram_tensor("KN", [T, D], F32, kind="ExternalOutput").ap()
    VB = nc.dram_tensor("VB", [T, D], F32, kind="ExternalOutput").ap()
    KB = nc.dram_tensor("KB", [T, D], F32, kind="ExternalOutput").ap()
    tch = ceil_div(T, 128)
    p = P(nc)
    with (
        nc.sbuf_tensor("qt", [128, D], F32) as qt,
        nc.sbuf_tensor("kt2", [128, D], F32) as kt2,
        nc.sbuf_tensor("vt", [128, D], F32) as vt,
        nc.sbuf_tensor("bt", [128, NH], F32) as bt,
        nc.sbuf_tensor("sq", [128, DK], F32) as sq,
        nc.sbuf_tensor("ob", [128, D], F32) as ob,
        nc.sbuf_tensor("c1", [128, 8], F32) as c1,
        nc.sbuf_tensor("kt", [128, 16], F32) as kt,
    ):
        p.ktile = kt
        p.ksrc = KON
        for ti in range(tch):
            t0 = ti * 128
            p.dma('sync', qt[:, :], QT[t0:t0 + 128, :])
            p.dma('sync', kt2[:, :], KT[t0:t0 + 128, :])
            p.dma('sync', vt[:, :], VT[t0:t0 + 128, :])
            p.dma('sync', bt[:, :], BETA[t0:t0 + 128, :])
            for (src, out_, scaled, sout) in ((qt, QN, None, None), (kt2, KN, bt, KB)):
                for h in range(NH):
                    sl = src[:, h * DK:(h + 1) * DK]
                    p.op('scalar', lambda e, o=sq[:, :], i=sl, z=p.konst(0.0):
                         e.activation(o, i, AF.Square, bias=z))
                    p.op('vector', lambda e, o=c1[:, 0:1], i=sq[:, :]:
                         e.reduce_sum(o, i, axis=AX.X))
                    p.op('scalar', lambda e, o=c1[:, 1:2], i=c1[:, 0:1], b=p.konst(1e-6):
                         e.activation(o, i, AF.Ln, bias=b))
                    p.op('scalar', lambda e, o=c1[:, 2:3], i=c1[:, 1:2],
                         s=p.konst(-0.5), z=p.konst(0.0):
                         e.activation(o, i, AF.Exp, bias=z, scale=s))
                    p.op('vector', lambda e, o=ob[:, h * DK:(h + 1) * DK], i=sl,
                         r=c1[:, 2:3]:
                         e.tensor_scalar(o, i, r, None, ALU.mult))
                p.dma('sync', out_[t0:t0 + 128, :], ob[:, :])
                if scaled is not None:
                    for h in range(NH):
                        p.op('vector', lambda e, o=ob[:, h * DK:(h + 1) * DK],
                             i=ob[:, h * DK:(h + 1) * DK], b=scaled[:, h:h + 1]:
                             e.tensor_scalar(o, i, b, None, ALU.mult))
                    p.dma('sync', sout[t0:t0 + 128, :], ob[:, :])
            for h in range(NH):
                p.op('vector', lambda e, o=ob[:, h * DK:(h + 1) * DK],
                     i=vt[:, h * DK:(h + 1) * DK], b=bt[:, h:h + 1]:
                     e.tensor_scalar(o, i, b, None, ALU.mult))
            p.dma('sync', VB[t0:t0 + 128, :], ob[:, :])
        p.emit()
    return nc


def build_delta(T, DK, DV, CH):
    nch = T // CH
    nc = bass.Bass("TRN2", target_bir_lowering=False)
    KON = nc.dram_tensor("KONST", [128, 16], F32, kind="ExternalInput").ap()
    QF = nc.dram_tensor("QF", [DK, T], F32, kind="ExternalInput").ap()
    KF = nc.dram_tensor("KF", [DK, T], F32, kind="ExternalInput").ap()
    KBF = nc.dram_tensor("KBF", [DK, T], F32, kind="ExternalInput").ap()
    KTC = nc.dram_tensor("KTC", [nch, CH, DK], F32, kind="ExternalInput").ap()
    KBTC = nc.dram_tensor("KBTC", [nch, CH, DK], F32, kind="ExternalInput").ap()
    VTC = nc.dram_tensor("VTC", [nch, CH, DV], F32, kind="ExternalInput").ap()
    MSK = nc.dram_tensor("MSK", [CH, 4 * CH], F32, kind="ExternalInput").ap()
    O = nc.dram_tensor("O", [nch, CH, DV], F32, kind="ExternalOutput").ap()
    dkc = DK // 128
    p = P(nc)
    with (
        nc.sbuf_tensor("qf", [128, dkc * T], F32) as qf,
        nc.sbuf_tensor("kf", [128, dkc * T], F32) as kf,
        nc.sbuf_tensor("kbf", [128, dkc * T], F32) as kbf,
        nc.sbuf_tensor("ktc", [CH, DK], F32) as ktc,
        nc.sbuf_tensor("kbtc", [CH, DK], F32) as kbtc,
        nc.sbuf_tensor("vtc", [CH, DV], F32) as vtc,
        nc.sbuf_tensor("msk", [CH, 4 * CH], F32) as msk,
        nc.sbuf_tensor("S0", [128, DV], F32) as S0,
        nc.sbuf_tensor("S1", [128, DV], F32) as S1,
        nc.sbuf_tensor("tw", [CH, 32 * CH], F32) as tw,
        nc.sbuf_tensor("wf", [128, 2 * CH], F32) as wf,
        nc.sbuf_tensor("usb", [CH, DV], F32) as usb,
        nc.sbuf_tensor("u2", [CH, DV], F32) as u2,
        nc.sbuf_tensor("osb", [CH, DV], F32) as osb,
        nc.sbuf_tensor("kt", [128, 16], F32) as kt,
        nc.psum_tensor("ps", [CH, 512], F32) as ps,
        nc.psum_tensor("psS", [128, DV], F32) as psS,
    ):
        p.ktile = kt
        p.ksrc = KON
        p.dma('sync', msk[:, :], MSK[:, :])
        for j in range(dkc):
            p.dma('sync', qf[:, j * T:(j + 1) * T], QF[j * 128:(j + 1) * 128, :])
            p.dma('sync', kf[:, j * T:(j + 1) * T], KF[j * 128:(j + 1) * 128, :])
            p.dma('sync', kbf[:, j * T:(j + 1) * T], KBF[j * 128:(j + 1) * 128, :])
        p.op('vector', lambda e: e.memset(S0[:, :], 0.0))
        p.op('vector', lambda e: e.memset(S1[:, :], 0.0))
        sl = lambda i: tw[:, i * CH:(i + 1) * CH]
        M_ST, M_STT, M_LOWT, M_EYE = (msk[:, 0:CH], msk[:, CH:2 * CH],
                                      msk[:, 2 * CH:3 * CH], msk[:, 3 * CH:4 * CH])
        for ci in range(nch):
            c0 = ci * CH
            p.dma('sync', ktc[:, :], KTC[ci])
            p.dma('sync', kbtc[:, :], KBTC[ci])
            p.dma('sync', vtc[:, :], VTC[ci])
            for (x, y, slot, m) in ((kbf, kf, 0, M_ST), (kf, kbf, 1, M_STT)):
                for j in range(dkc):
                    p.op('tensor', lambda e, o=ps[0:CH, 0:CH],
                         l=x[:, j * T + c0:j * T + c0 + CH],
                         r=y[:, j * T + c0:j * T + c0 + CH],
                         s=(j == 0), t=(j == dkc - 1):
                         e.matmul(o, l, r, start=s, stop=t))
                p.op('vector', lambda e, o=sl(slot), i=ps[0:CH, 0:CH]:
                     e.tensor_scalar(o, i, -1.0, None, ALU.mult))
                p.op('vector', lambda e, o=sl(slot), a=sl(slot), b=m:
                     e.tensor_mul(o, a, b))
            p.op('vector', lambda e, o=sl(4), i=M_EYE: e.tensor_copy(o, i))
            p.op('vector', lambda e, o=sl(5), i=M_EYE: e.tensor_copy(o, i))
            p.op('vector', lambda e, o=sl(2), i=sl(0): e.tensor_copy(o, i))
            p.op('vector', lambda e, o=sl(3), i=sl(1): e.tensor_copy(o, i))
            for lev in range(5):
                p.op('tensor', lambda e, o=ps[0:CH, 0:CH], l=sl(3), r=sl(4):
                     e.matmul(o, l, r, start=True, stop=True))
                p.op('tensor', lambda e, o=ps[0:CH, CH:2 * CH], l=sl(4), r=sl(3):
                     e.matmul(o, l, r, start=True, stop=True))
                if lev < 4:
                    p.op('tensor', lambda e, o=ps[0:CH, 2 * CH:3 * CH], l=sl(3), r=sl(2):
                         e.matmul(o, l, r, start=True, stop=True))
                    p.op('tensor', lambda e, o=ps[0:CH, 3 * CH:4 * CH], l=sl(2), r=sl(3):
                         e.matmul(o, l, r, start=True, stop=True))
                p.op('vector', lambda e, o=sl(4), a=sl(4), b=ps[0:CH, 0:CH]:
                     e.tensor_add(o, a, b))
                p.op('vector', lambda e, o=sl(5), a=sl(5), b=ps[0:CH, CH:2 * CH]:
                     e.tensor_add(o, a, b))
                if lev < 4:
                    p.op('vector', lambda e, o=sl(2), i=ps[0:CH, 2 * CH:3 * CH]:
                         e.tensor_copy(o, i))
                    p.op('vector', lambda e, o=sl(3), i=ps[0:CH, 3 * CH:4 * CH]:
                         e.tensor_copy(o, i))
            p.op('tensor', lambda e, o=ps[0:CH, 0:DV], l=sl(5), r=vtc[:, :]:
                 e.matmul(o, l, r, start=True, stop=True))
            p.op('vector', lambda e, o=usb[:, :], i=ps[0:CH, 0:DV]:
                 e.tensor_copy(o, i))
            for j in range(dkc):
                p.op('tensor', lambda e, o=psS[0:128, 0:CH],
                     l=kbtc[:, j * 128:(j + 1) * 128], r=sl(5):
                     e.matmul(o, l, r, start=True, stop=True))
                p.op('vector', lambda e, o=wf[:, j * CH:(j + 1) * CH],
                     i=psS[0:128, 0:CH]: e.tensor_copy(o, i))
            p.op('tensor', lambda e, o=ps[0:CH, 0:DV], l=wf[:, 0:CH], r=S0[:, :]:
                 e.matmul(o, l, r, start=True, stop=False))
            p.op('tensor', lambda e, o=ps[0:CH, 0:DV], l=wf[:, CH:2 * CH], r=S1[:, :]:
                 e.matmul(o, l, r, start=False, stop=True))
            p.op('vector', lambda e, o=u2[:, :], a=usb[:, :], b=ps[0:CH, 0:DV]:
                 e.tensor_sub(o, a, b))
            for j in range(dkc):
                p.op('tensor', lambda e, o=ps[0:CH, 0:CH],
                     l=kf[:, j * T + c0:j * T + c0 + CH],
                     r=qf[:, j * T + c0:j * T + c0 + CH],
                     s=(j == 0), t=(j == dkc - 1):
                     e.matmul(o, l, r, start=s, stop=t))
            p.op('vector', lambda e, o=sl(6), a=ps[0:CH, 0:CH], b=M_LOWT:
                 e.tensor_mul(o, a, b))
            p.op('tensor', lambda e, o=ps[0:CH, 0:DV],
                 l=qf[:, 0 * T + c0:0 * T + c0 + CH], r=S0[:, :]:
                 e.matmul(o, l, r, start=True, stop=False))
            p.op('tensor', lambda e, o=ps[0:CH, 0:DV],
                 l=qf[:, 1 * T + c0:1 * T + c0 + CH], r=S1[:, :]:
                 e.matmul(o, l, r, start=False, stop=False))
            p.op('tensor', lambda e, o=ps[0:CH, 0:DV], l=sl(6), r=u2[:, :]:
                 e.matmul(o, l, r, start=False, stop=True))
            p.op('vector', lambda e, o=osb[:, :], i=ps[0:CH, 0:DV]:
                 e.tensor_copy(o, i))
            p.dma('sync', O[ci], osb[:, :])
            for (j, S) in ((0, S0), (1, S1)):
                p.op('tensor', lambda e, o=psS[0:128, 0:DV],
                     l=ktc[:, j * 128:(j + 1) * 128], r=u2[:, :]:
                     e.matmul(o, l, r, start=True, stop=True))
                p.op('vector', lambda e, o=S[:, :], a=S[:, :], b=psS[0:128, 0:DV]:
                     e.tensor_add(o, a, b))
        p.emit()
    return nc


def build_stats(T, NH, DV):
    D = NH * DV
    nc = bass.Bass("TRN2", target_bir_lowering=False)
    KON = nc.dram_tensor("KONST", [128, 16], F32, kind="ExternalInput").ap()
    ins = {nm: nc.dram_tensor(nm, [T, D], F32, kind="ExternalInput").ap()
           for nm in ("FS", "FL", "OD", "VD")}
    ST = nc.dram_tensor("ST", [T, NH * 16], F32, kind="ExternalOutput").ap()
    tch = ceil_div(T, 128)
    p = P(nc)
    with (
        nc.sbuf_tensor("xt", [128, D], F32) as xt,
        nc.sbuf_tensor("sq", [128, DV], F32) as sq,
        nc.sbuf_tensor("st", [128, NH * 16], F32) as st,
        nc.sbuf_tensor("c1", [128, 8], F32) as c1,
        nc.sbuf_tensor("kt", [128, 16], F32) as kt,
    ):
        p.ktile = kt
        p.ksrc = KON
        for ti in range(tch):
            t0 = ti * 128
            for w, nm in enumerate(("FS", "FL", "OD", "VD")):
                p.dma('sync', xt[:, :], ins[nm][t0:t0 + 128, :])
                for h in range(NH):
                    slx = xt[:, h * DV:(h + 1) * DV]
                    base = h * 16 + w * 4
                    p.op('vector', lambda e, o=c1[:, 0:1], i=slx:
                         e.reduce_sum(o, i, axis=AX.X))
                    p.op('vector', lambda e, o=st[:, base:base + 1], i=c1[:, 0:1]:
                         e.tensor_scalar(o, i, 1.0 / DV, None, ALU.mult))
                    p.op('scalar', lambda e, o=sq[:, :], i=slx, z=p.konst(0.0):
                         e.activation(o, i, AF.Square, bias=z))
                    p.op('vector', lambda e, o=c1[:, 1:2], i=sq[:, :]:
                         e.reduce_sum(o, i, axis=AX.X))
                    p.op('vector', lambda e, o=c1[:, 2:3], i=c1[:, 1:2]:
                         e.tensor_scalar(o, i, 1.0 / DV, None, ALU.mult))
                    p.op('vector', lambda e, o=c1[:, 3:4], a=st[:, base:base + 1],
                         b=st[:, base:base + 1]: e.tensor_mul(o, a, b))
                    p.op('vector', lambda e, o=st[:, base + 1:base + 2],
                         a=c1[:, 2:3], b=c1[:, 3:4]: e.tensor_sub(o, a, b))
                    p.op('scalar', lambda e, o=sq[:, :], i=slx, z=p.konst(0.0):
                         e.activation(o, i, AF.Abs, bias=z))
                    p.op('vector', lambda e, o=c1[:, 4:5], i=sq[:, :]:
                         e.reduce_sum(o, i, axis=AX.X))
                    p.op('vector', lambda e, o=st[:, base + 2:base + 3], i=c1[:, 4:5]:
                         e.tensor_scalar(o, i, 1.0 / DV, None, ALU.mult))
                    p.op('scalar', lambda e, o=c1[:, 5:6], i=c1[:, 1:2], z=p.konst(0.0):
                         e.activation(o, i, AF.Ln, bias=z))
                    p.op('scalar', lambda e, o=st[:, base + 3:base + 4], i=c1[:, 5:6],
                         s=p.konst(0.5), z=p.konst(0.0):
                         e.activation(o, i, AF.Exp, bias=z, scale=s))
            p.dma('sync', ST[t0:t0 + 128, :], st[:, :])
        p.emit()
    return nc


def build_gelu(M, N):
    nc = bass.Bass("TRN2", target_bir_lowering=False)
    KON = nc.dram_tensor("KONST", [128, 16], F32, kind="ExternalInput").ap()
    X = nc.dram_tensor("X", [M, N], F32, kind="ExternalInput").ap()
    OUT = nc.dram_tensor("OUT", [M, N], F32, kind="ExternalOutput").ap()
    mch = ceil_div(M, 128)
    C = float(np.sqrt(2.0 / np.pi))
    p = P(nc)
    with (
        nc.sbuf_tensor("xt", [128, N], F32) as xt,
        nc.sbuf_tensor("x2", [128, N], F32) as x2,
        nc.sbuf_tensor("tt", [128, N], F32) as tt,
        nc.sbuf_tensor("kt", [128, 16], F32) as kt,
    ):
        p.ktile = kt
        p.ksrc = KON
        for mi in range(mch):
            m0 = mi * 128
            p.dma('sync', xt[:, :], X[m0:m0 + 128, :])
            p.op('scalar', lambda e, o=x2[:, :], i=xt[:, :], z=p.konst(0.0):
                 e.activation(o, i, AF.Square, bias=z))
            p.op('vector', lambda e, o=x2[:, :], i=x2[:, :]:
                 e.tensor_scalar(o, i, C * 0.044715, C, ALU.mult, ALU.add))
            p.op('vector', lambda e, o=tt[:, :], a=x2[:, :], b=xt[:, :]:
                 e.tensor_mul(o, a, b))
            p.op('scalar', lambda e, o=tt[:, :], i=tt[:, :], z=p.konst(0.0):
                 e.activation(o, i, AF.Tanh, bias=z))
            p.op('vector', lambda e, o=tt[:, :], i=tt[:, :]:
                 e.tensor_scalar(o, i, 1.0, 0.5, ALU.add, ALU.mult))
            p.op('vector', lambda e, o=tt[:, :], a=tt[:, :], b=xt[:, :]:
                 e.tensor_mul(o, a, b))
            p.dma('sync', OUT[m0:m0 + 128, :], tt[:, :])
        p.emit()
    return nc


def build_gate_softmax(T, NH):
    DECAY = 1.0 - 1.0 / 3000.0
    EF = 0.08 * DECAY
    nc = bass.Bass("TRN2", target_bir_lowering=False)
    KON = nc.dram_tensor("KONST", [128, 16], F32, kind="ExternalInput").ap()
    LG = nc.dram_tensor("LG", [T, NH * 4], F32, kind="ExternalInput").ap()
    CB = nc.dram_tensor("CB", [128, NH], F32, kind="ExternalInput").ap()
    LTt = nc.dram_tensor("LT", [128, NH], F32, kind="ExternalInput").ap()
    WGT = nc.dram_tensor("WGT", [T, NH * 4], F32, kind="ExternalOutput").ap()
    tch = ceil_div(T, 128)
    p = P(nc)
    with (
        nc.sbuf_tensor("lg", [128, NH * 4], F32) as lg,
        nc.sbuf_tensor("cb", [128, NH], F32) as cb,
        nc.sbuf_tensor("lt", [128, NH], F32) as lt,
        nc.sbuf_tensor("rt", [128, NH], F32) as rt,
        nc.sbuf_tensor("c1", [128, 8], F32) as c1,
        nc.sbuf_tensor("ex", [128, 4], F32) as ex,
        nc.sbuf_tensor("kt", [128, 16], F32) as kt,
    ):
        p.ktile = kt
        p.ksrc = KON
        p.dma('sync', cb[:, :], CB[:, :])
        p.dma('sync', lt[:, :], LTt[:, :])
        p.op('vector', lambda e: e.tensor_scalar(cb[:, :], cb[:, :], DECAY, None, ALU.mult))
        p.op('scalar', lambda e, z=p.konst(0.0):
             e.activation(lt[:, :], lt[:, :], AF.Exp, bias=z))
        p.op('vector', lambda e: e.reciprocal(rt[:, :], lt[:, :]))
        for ti in range(tch):
            t0 = ti * 128
            p.dma('sync', lg[:, :], LG[t0:t0 + 128, :])
            for h in range(NH):
                s4 = lg[:, h * 4:h * 4 + 4]
                p.op('vector', lambda e, o=lg[:, h * 4 + 3:h * 4 + 4], b=cb[:, h:h + 1]:
                     e.tensor_scalar(o, o, b, None, ALU.add))
                p.op('vector', lambda e, o=s4, r=rt[:, h:h + 1]:
                     e.tensor_scalar(o, o, r, None, ALU.mult))
                p.op('vector', lambda e, o=c1[:, 0:1], i=s4:
                     e.reduce_max(o, i, axis=AX.X))
                p.op('vector', lambda e, o=c1[:, 1:2], i=c1[:, 0:1]:
                     e.tensor_scalar(o, i, -1.0, None, ALU.mult))
                p.op('scalar', lambda e, o=ex[:, :], i=s4, b=c1[:, 1:2]:
                     e.activation(o, i, AF.Exp, bias=b))
                p.op('vector', lambda e, o=c1[:, 2:3], i=ex[:, :]:
                     e.reduce_sum(o, i, axis=AX.X))
                p.op('vector', lambda e, o=c1[:, 3:4], i=c1[:, 2:3]:
                     e.reciprocal(o, i))
                p.op('vector', lambda e, o=ex[:, :], r=c1[:, 3:4]:
                     e.tensor_scalar(o, o, r, None, ALU.mult))
                p.op('vector', lambda e, o=s4, i=ex[:, :]:
                     e.tensor_scalar(o, i, 1.0 - 4.0 * EF, EF, ALU.mult, ALU.add))
            p.dma('sync', WGT[t0:t0 + 128, :], lg[:, :])
        p.emit()
    return nc


def build_mix(T, NH, DV):
    D = NH * DV
    nc = bass.Bass("TRN2", target_bir_lowering=False)
    KON = nc.dram_tensor("KONST", [128, 16], F32, kind="ExternalInput").ap()
    ins = {nm: nc.dram_tensor(nm, [T, D], F32, kind="ExternalInput").ap()
           for nm in ("FS", "FL", "OD", "VD")}
    WGT = nc.dram_tensor("WGT", [T, NH * 4], F32, kind="ExternalInput").ap()
    OWB = nc.dram_tensor("OWB", [128, D], F32, kind="ExternalInput").ap()
    OMIX = nc.dram_tensor("OMIX", [T, D], F32, kind="ExternalOutput").ap()
    tch = ceil_div(T, 128)
    p = P(nc)
    with (
        nc.sbuf_tensor("fs", [128, D], F32) as fs,
        nc.sbuf_tensor("fl", [128, D], F32) as fl,
        nc.sbuf_tensor("od", [128, D], F32) as od,
        nc.sbuf_tensor("vd", [128, D], F32) as vd,
        nc.sbuf_tensor("wg", [128, NH * 4], F32) as wg,
        nc.sbuf_tensor("ow", [128, D], F32) as ow,
        nc.sbuf_tensor("ot", [128, D], F32) as ot,
        nc.sbuf_tensor("sq", [128, DV], F32) as sq,
        nc.sbuf_tensor("c1", [128, 8], F32) as c1,
        nc.sbuf_tensor("kt", [128, 16], F32) as kt,
    ):
        p.ktile = kt
        p.ksrc = KON
        p.dma('sync', ow[:, :], OWB[:, :])
        for ti in range(tch):
            t0 = ti * 128
            for tile, nm in ((fs, "FS"), (fl, "FL"), (od, "OD"), (vd, "VD")):
                p.dma('sync', tile[:, :], ins[nm][t0:t0 + 128, :])
            p.dma('sync', wg[:, :], WGT[t0:t0 + 128, :])
            for h in range(NH):
                hd = slice(h * DV, (h + 1) * DV)
                p.op('vector', lambda e, o=ot[:, hd], i=fs[:, hd], w=wg[:, h * 4:h * 4 + 1]:
                     e.tensor_scalar(o, i, w, None, ALU.mult))
                for (tile, ci) in ((fl, 1), (od, 2), (vd, 3)):
                    p.op('vector', lambda e, o=ot[:, hd], i=tile[:, hd],
                         w=wg[:, h * 4 + ci:h * 4 + ci + 1], ac=ot[:, hd]:
                         e.scalar_tensor_tensor(o, i, w, ac, ALU.mult, ALU.add))
                p.op('scalar', lambda e, o=sq[:, :], i=ot[:, hd], z=p.konst(0.0):
                     e.activation(o, i, AF.Square, bias=z))
                p.op('vector', lambda e, o=c1[:, 0:1], i=sq[:, :]:
                     e.reduce_sum(o, i, axis=AX.X))
                p.op('vector', lambda e, o=c1[:, 1:2], i=c1[:, 0:1]:
                     e.tensor_scalar(o, i, 1.0 / DV, None, ALU.mult))
                p.op('scalar', lambda e, o=c1[:, 2:3], i=c1[:, 1:2], b=p.konst(1e-5):
                     e.activation(o, i, AF.Ln, bias=b))
                p.op('scalar', lambda e, o=c1[:, 3:4], i=c1[:, 2:3], s=p.konst(-0.5),
                     z=p.konst(0.0):
                     e.activation(o, i, AF.Exp, bias=z, scale=s))
                p.op('vector', lambda e, o=ot[:, hd], i=ot[:, hd], r=c1[:, 3:4]:
                     e.tensor_scalar(o, i, r, None, ALU.mult))
            p.op('vector', lambda e, o=ot[:, :], a=ot[:, :], b=ow[:, :]:
                 e.tensor_mul(o, a, b))
            p.dma('sync', OMIX[t0:t0 + 128, :], ot[:, :])
        p.emit()
    return nc


def build_router(T, E):
    nc = bass.Bass("TRN2", target_bir_lowering=False)
    KON = nc.dram_tensor("KONST", [128, 16], F32, kind="ExternalInput").ap()
    PR = nc.dram_tensor("PR", [T, E], F32, kind="ExternalInput").ap()
    WS = nc.dram_tensor("WS", [T, E], F32, kind="ExternalOutput").ap()
    SH = nc.dram_tensor("SH", [T, 1], F32, kind="ExternalOutput").ap()
    tch = ceil_div(T, 128)
    p = P(nc)
    with (
        nc.sbuf_tensor("pr", [128, E], F32) as pr,
        nc.sbuf_tensor("eq", [128, E], F32) as eq,
        nc.sbuf_tensor("c1", [128, 8], F32) as c1,
        nc.sbuf_tensor("kt", [128, 16], F32) as kt,
    ):
        p.ktile = kt
        p.ksrc = KON
        for ti in range(tch):
            t0 = ti * 128
            p.dma('sync', pr[:, :], PR[t0:t0 + 128, :])
            p.op('vector', lambda e, o=c1[:, 0:1], i=pr[:, :]:
                 e.reduce_max(o, i, axis=AX.X))
            p.op('vector', lambda e, o=eq[:, :], i=pr[:, :], m=c1[:, 0:1]:
                 e.tensor_scalar(o, i, m, None, ALU.is_ge))
            p.op('vector', lambda e, o=c1[:, 1:2], i=c1[:, 0:1]:
                 e.tensor_scalar(o, i, 1e-8, 1.0 - 1e-8, ALU.max, ALU.min))
            p.op('vector', lambda e, o=c1[:, 2:3], i=c1[:, 1:2]:
                 e.tensor_scalar(o, i, 0.5 + 1e-8, None, ALU.add))
            p.op('vector', lambda e, o=c1[:, 3:4], i=c1[:, 2:3]:
                 e.reciprocal(o, i))
            p.op('vector', lambda e, o=c1[:, 4:5], a=c1[:, 1:2], b=c1[:, 3:4]:
                 e.tensor_mul(o, a, b))
            p.op('vector', lambda e, o=eq[:, :], i=eq[:, :], w=c1[:, 4:5]:
                 e.tensor_scalar(o, i, w, None, ALU.mult))
            p.op('vector', lambda e, o=c1[:, 5:6], i=c1[:, 3:4]:
                 e.tensor_scalar(o, i, 0.5, None, ALU.mult))
            p.dma('sync', WS[t0:t0 + 128, :], eq[:, :])
            p.dma('sync', SH[t0:t0 + 128, :], c1[:, 5:6])
        p.emit()
    return nc


def build_swiglu(K, M, N):
    nc = bass.Bass("TRN2", target_bir_lowering=False)
    KON = nc.dram_tensor("KONST", [128, 16], F32, kind="ExternalInput").ap()
    W1 = nc.dram_tensor("W1", [K, M], F32, kind="ExternalInput").ap()
    W3 = nc.dram_tensor("W3", [K, M], F32, kind="ExternalInput").ap()
    R = nc.dram_tensor("R", [K, N], F32, kind="ExternalInput").ap()
    SC = nc.dram_tensor("SC", [128, N], F32, kind="ExternalInput").ap()
    H = nc.dram_tensor("H", [M, N], F32, kind="ExternalOutput").ap()
    NC = 512
    kch = ceil_div(K, 128)
    mch = ceil_div(M, 128)
    nch = ceil_div(N, NC)
    p = P(nc)
    with (
        nc.sbuf_tensor("w1t", [128, kch * 128], F32) as w1t,
        nc.sbuf_tensor("w3t", [128, kch * 128], F32) as w3t,
        nc.sbuf_tensor("rt", [128, kch * NC], F32) as rt,
        nc.sbuf_tensor("sc", [128, NC], F32) as sc,
        nc.sbuf_tensor("ht", [128, NC], F32) as ht,
        nc.sbuf_tensor("kt", [128, 16], F32) as kt,
        nc.psum_tensor("p1", [128, NC], F32) as p1,
        nc.psum_tensor("p3", [128, NC], F32) as p3,
    ):
        p.ktile = kt
        p.ksrc = KON
        for ni in range(nch):
            n0 = ni * NC
            nw = min(NC, N - n0)
            p.dma('sync', sc[:, 0:nw], SC[:, n0:n0 + nw])
            for j in range(kch):
                k0 = j * 128
                kw = min(128, K - k0)
                p.dma('sync', rt[0:kw, j * NC:j * NC + nw], R[k0:k0 + kw, n0:n0 + nw])
            for mi in range(mch):
                m0 = mi * 128
                mw = min(128, M - m0)
                for j in range(kch):
                    k0 = j * 128
                    kw = min(128, K - k0)
                    p.dma('sync', w1t[0:kw, j * 128:j * 128 + mw], W1[k0:k0 + kw, m0:m0 + mw])
                    p.dma('sync', w3t[0:kw, j * 128:j * 128 + mw], W3[k0:k0 + kw, m0:m0 + mw])
                for j in range(kch):
                    kw = min(128, K - j * 128)
                    p.op('tensor', lambda e, o=p1[0:mw, 0:nw],
                         l=w1t[0:kw, j * 128:j * 128 + mw], r=rt[0:kw, j * NC:j * NC + nw],
                         s=(j == 0), t=(j == kch - 1): e.matmul(o, l, r, start=s, stop=t))
                for j in range(kch):
                    kw = min(128, K - j * 128)
                    p.op('tensor', lambda e, o=p3[0:mw, 0:nw],
                         l=w3t[0:kw, j * 128:j * 128 + mw], r=rt[0:kw, j * NC:j * NC + nw],
                         s=(j == 0), t=(j == kch - 1): e.matmul(o, l, r, start=s, stop=t))
                p.op('scalar', lambda e, o=ht[0:mw, 0:nw], i=p1[0:mw, 0:nw], z=p.konst(0.0):
                     e.activation(o, i, AF.Silu, bias=z))
                p.op('vector', lambda e, o=ht[0:mw, 0:nw], a=ht[0:mw, 0:nw], b=p3[0:mw, 0:nw]:
                     e.tensor_mul(o, a, b))
                p.op('vector', lambda e, o=ht[0:mw, 0:nw], a=ht[0:mw, 0:nw], b=sc[0:mw, 0:nw]:
                     e.tensor_mul(o, a, b))
                p.dma('sync', H[m0:m0 + mw, n0:n0 + nw], ht[0:mw, 0:nw])
        p.emit()
    return nc


def build_moe(K, M, N, E):
    nc = bass.Bass("TRN2", target_bir_lowering=False)
    KON = nc.dram_tensor("KONST", [128, 16], F32, kind="ExternalInput").ap()
    R = nc.dram_tensor("R", [K, N], F32, kind="ExternalInput").ap()
    SCS = nc.dram_tensor("SCS", [1 + E, N], F32, kind="ExternalInput").ap()
    W1S = nc.dram_tensor("W1S", [(1 + E) * K, M], F32, kind="ExternalInput").ap()
    W3S = nc.dram_tensor("W3S", [(1 + E) * K, M], F32, kind="ExternalInput").ap()
    W2S = nc.dram_tensor("W2S", [(1 + E) * M, K], F32, kind="ExternalInput").ap()
    OUT = nc.dram_tensor("OUT", [K, N], F32, kind="ExternalOutput").ap()
    NC = 512
    kch = K // 128
    mch = ceil_div(M, 128)
    nch = ceil_div(N, NC)
    p = P(nc)
    with (
        nc.sbuf_tensor("rt", [128, kch * N], F32) as rt,
        nc.sbuf_tensor("acc", [128, kch * N], F32) as acc,
        nc.sbuf_tensor("w1t", [128, kch * 128], F32) as w1t,
        nc.sbuf_tensor("w3t", [128, kch * 128], F32) as w3t,
        nc.sbuf_tensor("w2t", [128, 128], F32) as w2t,
        nc.sbuf_tensor("sc", [128, N], F32) as sc,
        nc.sbuf_tensor("ht", [128, NC], F32) as ht,
        nc.sbuf_tensor("on", [1, 128], F32) as on,
        nc.sbuf_tensor("kt", [128, 16], F32) as kt,
        nc.psum_tensor("p1", [128, NC], F32) as p1,
        nc.psum_tensor("p3", [128, NC], F32) as p3,
        nc.psum_tensor("pb", [128, NC], F32) as pb,
    ):
        p.ktile = kt
        p.ksrc = KON
        p.op('vector', lambda e: e.memset(on[0:1, 0:128], 0.0))
        p.op('vector', lambda e: e.tensor_scalar(on[0:1, 0:128], on[0:1, 0:128],
                                                 1.0, None, ALU.add))
        for j in range(kch):
            p.dma('sync', rt[:, j * N:(j + 1) * N], R[j * 128:(j + 1) * 128, :])
        p.op('vector', lambda e: e.memset(acc[:, :], 0.0))
        for x in range(1 + E):
            p.dma('sync', sc[0:1, 0:N], SCS[x:x + 1, :])
            for ni2 in range(ceil_div(N, NC)):
                n2 = ni2 * NC
                p.op('tensor', lambda e, o=pb[0:128, 0:NC], l=on[0:1, 0:128],
                     r=sc[0:1, n2:n2 + NC]:
                     e.matmul(o, l, r, start=True, stop=True))
                p.op('vector', lambda e, o=sc[0:128, n2:n2 + NC], i=pb[0:128, 0:NC]:
                     e.tensor_copy(o, i))
            for ni in range(nch):
                n0 = ni * NC
                for mi in range(mch):
                    m0 = mi * 128
                    mw = min(128, M - m0)
                    for j in range(kch):
                        p.dma('sync', w1t[:, j * 128:j * 128 + mw],
                              W1S[x * K + j * 128:x * K + (j + 1) * 128, m0:m0 + mw])
                        p.dma('sync', w3t[:, j * 128:j * 128 + mw],
                              W3S[x * K + j * 128:x * K + (j + 1) * 128, m0:m0 + mw])
                    for j in range(kch):
                        p.op('tensor', lambda e, o=p1[0:mw, 0:NC],
                             l=w1t[:, j * 128:j * 128 + mw],
                             r=rt[:, j * N + n0:j * N + n0 + NC],
                             s=(j == 0), t=(j == kch - 1):
                             e.matmul(o, l, r, start=s, stop=t))
                    for j in range(kch):
                        p.op('tensor', lambda e, o=p3[0:mw, 0:NC],
                             l=w3t[:, j * 128:j * 128 + mw],
                             r=rt[:, j * N + n0:j * N + n0 + NC],
                             s=(j == 0), t=(j == kch - 1):
                             e.matmul(o, l, r, start=s, stop=t))
                    p.op('scalar', lambda e, o=ht[0:mw, 0:NC], i=p1[0:mw, 0:NC],
                         z=p.konst(0.0)[0:mw, 0:1]:
                         e.activation(o, i, AF.Silu, bias=z))
                    p.op('vector', lambda e, o=ht[0:mw, 0:NC], a=ht[0:mw, 0:NC],
                         b=p3[0:mw, 0:NC]: e.tensor_mul(o, a, b))
                    p.op('vector', lambda e, o=ht[0:mw, 0:NC], a=ht[0:mw, 0:NC],
                         b=sc[0:mw, n0:n0 + NC]: e.tensor_mul(o, a, b))
                    for kc2 in range(kch):
                        p.dma('sync', w2t[0:mw, 0:128],
                              W2S[x * M + m0:x * M + m0 + mw, kc2 * 128:(kc2 + 1) * 128])
                        p.op('tensor', lambda e, o=pb[0:128, 0:NC],
                             l=w2t[0:mw, 0:128], r=ht[0:mw, 0:NC],
                             s=True, t=True: e.matmul(o, l, r, start=s, stop=t))
                        p.op('vector', lambda e, o=acc[:, kc2 * N + n0:kc2 * N + n0 + NC],
                             a=acc[:, kc2 * N + n0:kc2 * N + n0 + NC], b=pb[0:128, 0:NC]:
                             e.tensor_add(o, a, b))
        for j in range(kch):
            p.dma('sync', OUT[j * 128:(j + 1) * 128, :], acc[:, j * N:(j + 1) * N])
        p.emit()
    return nc


def build_l0attn(T, Dm, Din, NST, debug=False):
    nc = bass.Bass("TRN2", target_bir_lowering=False)
    KON = nc.dram_tensor("KONST", [128, 16], F32, kind="ExternalInput").ap()
    X = nc.dram_tensor("X", [T, Dm], F32, kind="ExternalInput").ap()
    GB = nc.dram_tensor("GB", [128, 2 * Dm], F32, kind="ExternalInput").ap()
    IDN = nc.dram_tensor("IDN", [128, 128], F32, kind="ExternalInput").ap()
    WIN = nc.dram_tensor("WIN", [Dm, 2 * Din], F32, kind="ExternalInput").ap()
    CW = nc.dram_tensor("CW", [Din, 4], F32, kind="ExternalInput").ap()
    CBi = nc.dram_tensor("CB", [Din, 1], F32, kind="ExternalInput").ap()
    XPW = nc.dram_tensor("XPW", [Din, 96], F32, kind="ExternalInput").ap()
    DTW = nc.dram_tensor("DTW", [64, Din], F32, kind="ExternalInput").ap()
    DTB = nc.dram_tensor("DTB", [Din, 1], F32, kind="ExternalInput").ap()
    ALOG = nc.dram_tensor("ALOG", [Din, NST], F32, kind="ExternalInput").ap()
    DD = nc.dram_tensor("DD", [Din, 1], F32, kind="ExternalInput").ap()
    WOUT = nc.dram_tensor("WOUT", [Din, Dm], F32, kind="ExternalInput").ap()
    XN0 = nc.dram_tensor("XN0", [T, Dm], F32, kind="ExternalOutput").ap()
    XN0T = nc.dram_tensor("XN0T", [Dm, T], F32, kind="ExternalOutput").ap()
    ATT0 = nc.dram_tensor("ATT0", [Dm, T], F32, kind="ExternalOutput").ap()
    if debug:
        DBGU = nc.dram_tensor("DBGU", [Din, T], F32, kind="ExternalOutput").ap()
        DBGD = nc.dram_tensor("DBGD", [Din, T], F32, kind="ExternalOutput").ap()
        DBGY = nc.dram_tensor("DBGY", [Din, T], F32, kind="ExternalOutput").ap()
        DBGXI = nc.dram_tensor("DBGXI", [Dm, T], F32, kind="ExternalOutput").ap()
        DBGUP = nc.dram_tensor("DBGUP", [Din, T], F32, kind="ExternalOutput").ap()

    TC = 256
    nchk = T // TC
    ntile = TC // 128
    dmch = Dm // 128
    dich = Din // 128
    p = P(nc)
    import contextlib
    es = contextlib.ExitStack()
    with es:
        gb = es.enter_context(nc.sbuf_tensor("gb", [128, 2 * Dm], F32))
        idn = es.enter_context(nc.sbuf_tensor("idn", [128, 128], F32))
        xt = es.enter_context(nc.sbuf_tensor("xt", [128, ntile * Dm], F32))
        xn = es.enter_context(nc.sbuf_tensor("xn", [128, ntile * Dm], F32))
        xa = es.enter_context(nc.sbuf_tensor("xa", [128, ntile * Dm], F32))
        sq = es.enter_context(nc.sbuf_tensor("sq", [128, Dm], F32))
        c1 = es.enter_context(nc.sbuf_tensor("c1", [128, 8], F32))
        xint = es.enter_context(nc.sbuf_tensor("xint", [128, dmch * TC], F32))
        xnt = es.enter_context(nc.sbuf_tensor("xnt", [128, dmch * TC], F32))
        wt = es.enter_context(nc.sbuf_tensor("wt", [128, dmch * 128], F32))
        ci = es.enter_context(nc.sbuf_tensor("ci", [128, dich * (3 + TC)], F32))
        cwt = es.enter_context(nc.sbuf_tensor("cwt", [128, dich * 4], F32))
        cbt = es.enter_context(nc.sbuf_tensor("cbt", [128, dich], F32))
        ut = es.enter_context(nc.sbuf_tensor("ut", [128, dich * TC], F32))
        zs = es.enter_context(nc.sbuf_tensor("zs", [128, dich * TC], F32))
        xpw = es.enter_context(nc.sbuf_tensor("xpw", [128, dich * 96], F32))
        dbc = es.enter_context(nc.sbuf_tensor("dbc", [96, TC], F32))
        dtw = es.enter_context(nc.sbuf_tensor("dtw", [64, Din], F32))
        dtb = es.enter_context(nc.sbuf_tensor("dtb", [128, dich], F32))
        dlt = es.enter_context(nc.sbuf_tensor("dlt", [128, dich * TC], F32))
        t1 = es.enter_context(nc.sbuf_tensor("t1", [128, TC], F32))
        t2 = es.enter_context(nc.sbuf_tensor("t2", [128, TC], F32))
        an = es.enter_context(nc.sbuf_tensor("an", [128, dich * NST], F32))
        ddt = es.enter_context(nc.sbuf_tensor("ddt", [128, dich], F32))
        bb = es.enter_context(nc.sbuf_tensor("bb", [128, NST * TC], F32))
        cc = es.enter_context(nc.sbuf_tensor("cc", [128, NST * TC], F32))
        on = es.enter_context(nc.sbuf_tensor("on", [1, 128], F32))
        du = es.enter_context(nc.sbuf_tensor("du", [128, TC], F32))
        da = es.enter_context(nc.sbuf_tensor("da", [128, TC], F32))
        bt = es.enter_context(nc.sbuf_tensor("bt", [128, TC], F32))
        ht = es.enter_context(nc.sbuf_tensor("ht", [128, TC], F32))
        zt = es.enter_context(nc.sbuf_tensor("zt", [128, TC], F32))
        y1 = es.enter_context(nc.sbuf_tensor("y1", [128, TC], F32))
        y2 = es.enter_context(nc.sbuf_tensor("y2", [128, TC], F32))
        st = es.enter_context(nc.sbuf_tensor("st", [128, NST * dich], F32))
        yg = es.enter_context(nc.sbuf_tensor("yg", [128, dich * TC], F32))
        wo = es.enter_context(nc.sbuf_tensor("wo", [128, dich * 128], F32))
        ot = es.enter_context(nc.sbuf_tensor("ot", [128, TC], F32))
        kt = es.enter_context(nc.sbuf_tensor("kt", [128, 16], F32))
        ps = es.enter_context(nc.psum_tensor("ps", [128, TC], F32))
        pt = es.enter_context(nc.psum_tensor("pt", [128, 128], F32))
        psr = es.enter_context(nc.psum_tensor("psr", [1, 512], F32))
        rw = es.enter_context(nc.sbuf_tensor("rw", [1, 512], F32))
        carry = es.enter_context(nc.sbuf_tensor("carry", [128, 48], F32))
        p.ktile = kt
        p.ksrc = KON
        p.dma('sync', gb[:, :], GB[:, :])
        p.dma('sync', idn[:, :], IDN[:, :])
        p.op('vector', lambda e: e.memset(on[0:1, 0:128], 0.0))
        p.op('vector', lambda e: e.tensor_scalar(on[0:1, 0:128], on[0:1, 0:128],
                                                 1.0, None, ALU.add))
        for j in range(dich):
            p.dma('sync', cwt[:, j * 4:(j + 1) * 4], CW[j * 128:(j + 1) * 128, :])
            p.dma('sync', cbt[:, j:j + 1], CBi[j * 128:(j + 1) * 128, :])
            p.dma('sync', dtb[:, j:j + 1], DTB[j * 128:(j + 1) * 128, :])
            p.dma('sync', an[:, j * NST:(j + 1) * NST], ALOG[j * 128:(j + 1) * 128, :])
            p.dma('sync', ddt[:, j:j + 1], DD[j * 128:(j + 1) * 128, :])
            p.op('vector', lambda e, o=ci[:, j * (3 + TC):j * (3 + TC) + 3]:
                 e.memset(o, 0.0))
        p.op('scalar', lambda e, z=p.konst(0.0):
             e.activation(an[:, :], an[:, :], AF.Exp, bias=z))
        p.op('vector', lambda e: e.tensor_scalar(an[:, :], an[:, :], -1.0, None, ALU.mult))
        p.dma('sync', dtw[0:64, :], DTW[:, :])
        for j in range(dich):
            p.dma('sync', xpw[:, j * 96:(j + 1) * 96], XPW[j * 128:(j + 1) * 128, :])
        p.op('vector', lambda e: e.memset(st[:, :], 0.0))

        def lnpass(src, dst, tt):
            p.op('vector', lambda e, o=c1[:, 0:1], i=src:
                 e.reduce_sum(o, i, axis=AX.X))
            p.op('vector', lambda e, o=c1[:, 0:1], i=c1[:, 0:1]:
                 e.tensor_scalar(o, i, 1.0 / Dm, None, ALU.mult))
            p.op('scalar', lambda e, o=sq[:, :], i=src, z=p.konst(0.0):
                 e.activation(o, i, AF.Square, bias=z))
            p.op('vector', lambda e, o=c1[:, 1:2], i=sq[:, :]:
                 e.reduce_sum(o, i, axis=AX.X))
            p.op('vector', lambda e, o=c1[:, 1:2], i=c1[:, 1:2]:
                 e.tensor_scalar(o, i, 1.0 / Dm, None, ALU.mult))
            p.op('vector', lambda e, o=c1[:, 2:3], a=c1[:, 0:1], b=c1[:, 0:1]:
                 e.tensor_mul(o, a, b))
            p.op('vector', lambda e, o=c1[:, 1:2], a=c1[:, 1:2], b=c1[:, 2:3]:
                 e.tensor_sub(o, a, b))
            p.op('scalar', lambda e, o=c1[:, 3:4], i=c1[:, 1:2], b5=p.konst(1e-5):
                 e.activation(o, i, AF.Ln, bias=b5))
            p.op('scalar', lambda e, o=c1[:, 4:5], i=c1[:, 3:4], s=p.konst(-0.5),
                 z=p.konst(0.0):
                 e.activation(o, i, AF.Exp, bias=z, scale=s))
            p.op('vector', lambda e, o=dst, i=src, m=c1[:, 0:1], r=c1[:, 4:5]:
                 e.tensor_scalar(o, i, m, r, ALU.subtract, ALU.mult))

        for ck in range(nchk):
            t0 = ck * TC
            for tt in range(ntile):
                tb = t0 + tt * 128
                xsl = xt[:, tt * Dm:(tt + 1) * Dm]
                p.dma('sync', xsl, X[tb:tb + 128, :])
                lnpass(xsl, xn[:, tt * Dm:(tt + 1) * Dm], tt)
                lnpass(xn[:, tt * Dm:(tt + 1) * Dm], xa[:, tt * Dm:(tt + 1) * Dm], tt)
                p.op('vector', lambda e, o=xa[:, tt * Dm:(tt + 1) * Dm],
                     a=xa[:, tt * Dm:(tt + 1) * Dm], b=gb[:, 0:Dm]:
                     e.tensor_mul(o, a, b))
                p.op('vector', lambda e, o=xa[:, tt * Dm:(tt + 1) * Dm],
                     a=xa[:, tt * Dm:(tt + 1) * Dm], b=gb[:, Dm:2 * Dm]:
                     e.tensor_add(o, a, b))
                p.dma('sync', XN0[tb:tb + 128, :], xn[:, tt * Dm:(tt + 1) * Dm])
                for dj in range(dmch):
                    p.op('tensor', lambda e, o=pt[:, :],
                         i=xa[:, tt * Dm + dj * 128:tt * Dm + (dj + 1) * 128],
                         dd=idn[:, :]: e.transpose(o, i, dd))
                    p.op('vector', lambda e, o=xint[:, dj * TC + tt * 128:dj * TC + (tt + 1) * 128],
                         i=pt[:, :]: e.tensor_copy(o, i))
                    p.op('tensor', lambda e, o=pt[:, :],
                         i=xn[:, tt * Dm + dj * 128:tt * Dm + (dj + 1) * 128],
                         dd=idn[:, :]: e.transpose(o, i, dd))
                    p.op('vector', lambda e, o=xnt[:, dj * TC + tt * 128:dj * TC + (tt + 1) * 128],
                         i=pt[:, :]: e.tensor_copy(o, i))
            for dj in range(dmch):
                p.dma('sync', XN0T[dj * 128:(dj + 1) * 128, t0:t0 + TC],
                      xnt[:, dj * TC:(dj + 1) * TC])
            if ck > 0:
                for j in range(dich):
                    base = j * (3 + TC)
                    p.op('vector', lambda e, o=carry[:, j * 3:(j + 1) * 3],
                         i=ci[:, base + TC:base + TC + 3]: e.tensor_copy(o, i))
                for j in range(dich):
                    base = j * (3 + TC)
                    p.op('vector', lambda e, o=ci[:, base:base + 3],
                         i=carry[:, j * 3:(j + 1) * 3]: e.tensor_copy(o, i))
            for mi in range(2 * dich):
                m0 = mi * 128
                for j in range(dmch):
                    p.dma('sync', wt[:, j * 128:(j + 1) * 128],
                          WIN[j * 128:(j + 1) * 128, m0:m0 + 128])
                for j in range(dmch):
                    p.op('tensor', lambda e, o=ps[:, 0:TC], l=wt[:, j * 128:(j + 1) * 128],
                         r=xint[:, j * TC:(j + 1) * TC], s=(j == 0), t=(j == dmch - 1):
                         e.matmul(o, l, r, start=s, stop=t))
                if mi < dich:
                    p.op('vector', lambda e, o=ci[:, mi * (3 + TC) + 3:mi * (3 + TC) + 3 + TC],
                         i=ps[:, 0:TC]: e.tensor_copy(o, i))
                else:
                    mj = mi - dich
                    p.op('scalar', lambda e, o=zs[:, mj * TC:(mj + 1) * TC], i=ps[:, 0:TC],
                         z=p.konst(0.0): e.activation(o, i, AF.Silu, bias=z))
            for j in range(dich):
                base = j * (3 + TC)
                p.op('vector', lambda e, o=t1[:, 0:TC], i=ci[:, base:base + TC],
                     w=cwt[:, j * 4:j * 4 + 1]:
                     e.tensor_scalar(o, i, w, None, ALU.mult))
                for k in range(1, 4):
                    srcb, dstb = (t1, t2) if k % 2 == 1 else (t2, t1)
                    p.op('vector', lambda e, o=dstb[:, 0:TC], i=ci[:, base + k:base + k + TC],
                         w=cwt[:, j * 4 + k:j * 4 + k + 1], ac=srcb[:, 0:TC]:
                         e.scalar_tensor_tensor(o, i, w, ac, ALU.mult, ALU.add))
                p.op('scalar', lambda e, o=ut[:, j * TC:(j + 1) * TC], i=t2[:, 0:TC],
                     b=cbt[:, j:j + 1]: e.activation(o, i, AF.Silu, bias=b))
            for j in range(dich):
                p.op('tensor', lambda e, o=ps[0:64, 0:TC], l=xpw[:, j * 96:j * 96 + 64],
                     r=ut[:, j * TC:(j + 1) * TC], s=(j == 0), t=(j == dich - 1):
                     e.matmul(o, l, r, start=s, stop=t))
            p.op('vector', lambda e, o=dbc[0:64, :], i=ps[0:64, 0:TC]: e.tensor_copy(o, i))
            for n in range(2 * NST):
                col = 64 + n
                dstt = bb if n < NST else cc
                nn = n if n < NST else n - NST
                for j in range(dich):
                    p.op('tensor', lambda e, o=psr[0:1, 0:TC],
                         l=xpw[:, j * 96 + col:j * 96 + col + 1],
                         r=ut[:, j * TC:(j + 1) * TC], s=(j == 0), t=(j == dich - 1):
                         e.matmul(o, l, r, start=s, stop=t))
                p.op('vector', lambda e, o=rw[0:1, 0:TC], i=psr[0:1, 0:TC]:
                     e.tensor_copy(o, i))
                for half in range(TC // 128):
                    h0 = half * 128
                    p.op('tensor', lambda e, o=pt[:, :], l=on[0:1, 0:128],
                         r=rw[0:1, h0:h0 + 128]:
                         e.matmul(o, l, r, start=True, stop=True))
                    p.op('vector', lambda e, o=dstt[:, nn * TC + h0:nn * TC + h0 + 128],
                         i=pt[:, :]: e.tensor_copy(o, i))
            for mi in range(dich):
                p.op('tensor', lambda e, o=ps[:, 0:TC], l=dtw[0:64, mi * 128:(mi + 1) * 128],
                     r=dbc[0:64, 0:TC]: e.matmul(o, l, r, start=True, stop=True))
                p.op('vector', lambda e, o=t1[:, 0:TC], i=ps[:, 0:TC], b=dtb[:, mi:mi + 1]:
                     e.tensor_scalar(o, i, b, None, ALU.add))
                p.op('scalar', lambda e, o=t2[:, 0:TC], i=t1[:, 0:TC], z=p.konst(0.0):
                     e.activation(o, i, AF.Abs, bias=z))
                p.op('scalar', lambda e, o=t2[:, 0:TC], i=t2[:, 0:TC],
                     s=p.konst(-1.0), z=p.konst(0.0):
                     e.activation(o, i, AF.Exp, bias=z, scale=s))
                p.op('scalar', lambda e, o=t2[:, 0:TC], i=t2[:, 0:TC], b=p.konst(1.0):
                     e.activation(o, i, AF.Ln, bias=b))
                p.op('vector', lambda e, o=t1[:, 0:TC], i=t1[:, 0:TC]:
                     e.tensor_scalar(o, i, 0.0, None, ALU.max))
                p.op('vector', lambda e, o=dlt[:, mi * TC:(mi + 1) * TC],
                     a=t1[:, 0:TC], b=t2[:, 0:TC]: e.tensor_add(o, a, b))
            for j in range(dich):
                p.op('vector', lambda e, o=du[:, 0:TC], a=dlt[:, j * TC:(j + 1) * TC],
                     b=ut[:, j * TC:(j + 1) * TC]: e.tensor_mul(o, a, b))
                ys = [y1, y2]
                for n in range(NST):
                    p.op('scalar', lambda e, o=da[:, 0:TC], i=dlt[:, j * TC:(j + 1) * TC],
                         s=an[:, j * NST + n:j * NST + n + 1], z=p.konst(0.0):
                         e.activation(o, i, AF.Exp, bias=z, scale=s))
                    p.op('vector', lambda e, o=bt[:, 0:TC], a=du[:, 0:TC],
                         b=bb[:, n * TC:(n + 1) * TC]: e.tensor_mul(o, a, b))
                    p.op('vector', lambda e, o=ht[:, 0:TC], d0_=da[:, 0:TC], d1=bt[:, 0:TC],
                         ini=st[:, j * NST + n:j * NST + n + 1]:
                         e.tensor_tensor_scan(o, d0_, d1, ini, ALU.mult, ALU.add))
                    p.op('vector', lambda e, o=st[:, j * NST + n:j * NST + n + 1],
                         i=ht[:, TC - 1:TC]: e.tensor_copy(o, i))
                    p.op('vector', lambda e, o=zt[:, 0:TC], a=ht[:, 0:TC],
                         b=cc[:, n * TC:(n + 1) * TC]: e.tensor_mul(o, a, b))
                    if n == 0:
                        p.op('vector', lambda e, o=ys[0][:, 0:TC], i=zt[:, 0:TC]:
                             e.tensor_copy(o, i))
                    else:
                        srcb, dstb = ys[(n + 1) % 2], ys[n % 2]
                        p.op('vector', lambda e, o=dstb[:, 0:TC], a=srcb[:, 0:TC],
                             b=zt[:, 0:TC]: e.tensor_add(o, a, b))
                fin = ys[(NST - 1) % 2]
                p.op('vector', lambda e, o=fin[:, 0:TC], i=ut[:, j * TC:(j + 1) * TC],
                     w=ddt[:, j:j + 1], ac=fin[:, 0:TC]:
                     e.scalar_tensor_tensor(o, i, w, ac, ALU.mult, ALU.add))
                p.op('vector', lambda e, o=yg[:, j * TC:(j + 1) * TC], a=fin[:, 0:TC],
                     b=zs[:, j * TC:(j + 1) * TC]: e.tensor_mul(o, a, b))
            if debug:
                for j in range(dmch):
                    p.dma('sync', DBGXI[j * 128:(j + 1) * 128, t0:t0 + TC],
                          xint[:, j * TC:(j + 1) * TC])
                for j in range(dich):
                    p.dma('sync', DBGUP[j * 128:(j + 1) * 128, t0:t0 + TC],
                          ci[:, j * (3 + TC) + 3:j * (3 + TC) + 3 + TC])
                for j in range(dich):
                    p.dma('sync', DBGU[j * 128:(j + 1) * 128, t0:t0 + TC],
                          ut[:, j * TC:(j + 1) * TC])
                    p.dma('sync', DBGD[j * 128:(j + 1) * 128, t0:t0 + TC],
                          dlt[:, j * TC:(j + 1) * TC])
                    p.dma('sync', DBGY[j * 128:(j + 1) * 128, t0:t0 + TC],
                          yg[:, j * TC:(j + 1) * TC])
            for mi in range(dmch):
                m0 = mi * 128
                for j in range(dich):
                    p.dma('sync', wo[:, j * 128:(j + 1) * 128],
                          WOUT[j * 128:(j + 1) * 128, m0:m0 + 128])
                for j in range(dich):
                    p.op('tensor', lambda e, o=ps[:, 0:TC], l=wo[:, j * 128:(j + 1) * 128],
                         r=yg[:, j * TC:(j + 1) * TC], s=(j == 0), t=(j == dich - 1):
                         e.matmul(o, l, r, start=s, stop=t))
                p.op('vector', lambda e, o=ot[:, 0:TC], i=ps[:, 0:TC]: e.tensor_copy(o, i))
                p.dma('sync', ATT0[m0:m0 + 128, t0:t0 + TC], ot[:, 0:TC])
        p.emit()
    return nc

def _threefry2x32(k1, k2, c1, c2):
    rotations = [np.array([13, 15, 26, 6], np.uint32),
                 np.array([17, 29, 16, 24], np.uint32)]
    ks = [np.uint32(k1), np.uint32(k2),
          np.uint32(k1) ^ np.uint32(k2) ^ np.uint32(0x1BD11BDA)]
    x = [c1 + ks[0], c2 + ks[1]]

    def rotl(v, r):
        return (v << np.uint32(r)) | (v >> np.uint32(32 - r))

    for i in range(5):
        for r in rotations[i % 2]:
            x[0] = x[0] + x[1]
            x[1] = rotl(x[1], r)
            x[1] = x[1] ^ x[0]
        x[0] = x[0] + ks[(i + 1) % 3]
        x[1] = x[1] + ks[(i + 2) % 3] + np.uint32(i + 1)
    return x


def _keep_mask(seed, p, shape):
    n = int(np.prod(shape))
    with np.errstate(over='ignore'):
        idx = np.arange(n, dtype=np.uint64)
        c1 = (idx >> np.uint64(32)).astype(np.uint32)
        c2 = (idx & np.uint64(0xFFFFFFFF)).astype(np.uint32)
        x0, x1 = _threefry2x32(np.uint32(seed >> 32), np.uint32(seed & 0xFFFFFFFF), c1, c2)
        bits = x0 ^ x1
    fl = ((bits >> np.uint32(9)) | np.uint32(0x3F800000)).view(np.float32) - np.float32(1.0)
    return (fl < np.float32(p)).reshape(shape)


_CACHE = {}


def _prog(name, builder, *args, **kw):
    key = (name,) + tuple(args) + tuple(sorted(kw.items()))
    if key not in _CACHE:
        _CACHE[key] = builder(*args, **kw)
    return _CACHE[key]


def _launch(nc, maps):
    for m in maps:
        m["KONST"] = kinput(nc)
    res = run_bass_kernel_spmd(nc, maps, core_ids=list(range(8)))
    return res.results


def _c(a):
    return np.ascontiguousarray(np.asarray(a, dtype=np.float32))


def kernel(**inputs):
    g = {k: np.asarray(v, dtype=np.float32) for k, v in inputs.items()}
    T, Dm, Din, NH, DK, DV, E, I = 2048, 1024, 2048, 4, 256, 256, 7, 2816
    NST = 16

    x_all = g['L0_attn_x']
    keep = _keep_mask(42, 0.8, (8, T, Dm))
    mask125 = keep.astype(np.float32) * np.float32(1.25)

    GB0 = _c(np.concatenate([np.tile(g['L0_attn_ln_g'], (128, 1)),
                             np.tile(g['L0_attn_ln_b'], (128, 1))], 1))

    IDN = np.eye(128, dtype=np.float32)
    nc = _prog('l0attn', build_l0attn, T, Dm, Din, NST)
    maps = []
    for c in range(8):
        maps.append({
            "X": _c(x_all[c]), "GB": GB0, "IDN": IDN,
            "WIN": _c(g['L0_attn_in_proj_w']), "CW": _c(g['L0_attn_conv_w']),
            "CB": _c(g['L0_attn_conv_b'].reshape(-1, 1)),
            "XPW": _c(g['L0_attn_x_proj_w']),
            "DTW": _c(g['L0_attn_dt_proj_w']),
            "DTB": _c(g['L0_attn_dt_proj_b'].reshape(-1, 1)),
            "ALOG": _c(g['L0_attn_A_log']), "DD": _c(g['L0_attn_D'].reshape(-1, 1)),
            "WOUT": _c(g['L0_attn_out_proj_w'])})
    r = _launch(nc, maps)
    XN0 = [r[c]["XN0"] for c in range(8)]
    XN0T = [r[c]["XN0T"] for c in range(8)]
    ATT0 = [r[c]["ATT0"] for c in range(8)]

    nc = _prog('q1', build_matmul, Dm, 4096, T, act='relu', bias=True, f8=True,
               out_bf16_round=True)
    W = _c(g['L0_ffn_W1'].T)
    B = _c(g['L0_ffn_b1'].reshape(-1, 1))
    r = _launch(nc, [{"W": W, "R": _c(XN0T[c]), "B": B} for c in range(8)])
    H1 = [r[c]["OUT"] for c in range(8)]

    nc = _prog('q2', build_matmul, 4096, Dm, T, bias=True, f8=True,
               out_bf16_round=True, mulT=True)
    W = _c(g['L0_ffn_W2'].T)
    B = _c(g['L0_ffn_b2'].reshape(-1, 1))
    r = _launch(nc, [{"W": W, "R": _c(H1[c]), "B": B,
                      "MT": _c(mask125[c].T)} for c in range(8)])
    FFN0 = [r[c]["OUT"] for c in range(8)]

    nc = _prog('comb4', build_combine, T, Dm, 4)
    r = _launch(nc, [{"X0": _c(x_all[c]), "X1": _c(XN0[c]), "X2": _c(ATT0[c].T),
                      "X3": _c(FFN0[c].T)} for c in range(8)])
    X1 = [r[c]["OUT"] for c in range(8)]

    nc = _prog('lnorm', build_lnorm, T, Dm)
    r = _launch(nc, [{"X": _c(X1[c])} for c in range(8)])
    XN1 = [r[c]["XN"] for c in range(8)]
    XN1T = [_c(XN1[c].T) for c in range(8)]

    nc = _prog('proj1024', build_matmul, Dm, Dm, T)
    QP, KP, VP = [], [], []
    for wname, dst in (('L1_attn_Wq', QP), ('L1_attn_Wk', KP), ('L1_attn_Wv', VP)):
        W = _c(g[wname].T)
        r = _launch(nc, [{"W": W, "R": XN1T[c]} for c in range(8)])
        dst.extend(r[c]["OUT"] for c in range(8))

    nc = _prog('conv1', build_conv, Dm, T, 4, act='silu', bias=False)
    QC, KC, VC = [], [], []
    for wname, src, dst in (('L1_attn_conv_q_w', QP, QC), ('L1_attn_conv_k_w', KP, KC),
                            ('L1_attn_conv_v_w', VP, VC)):
        W = _c(g[wname])
        r = _launch(nc, [{"X": _c(src[c]), "W": W} for c in range(8)])
        dst.extend(r[c]["OUT"] for c in range(8))

    nc = _prog('wb', build_matmul, Dm, NH, T, act='sigmoid')
    W = _c(g['L1_attn_Wb'].T)
    r = _launch(nc, [{"W": W, "R": XN1T[c]} for c in range(8)])
    BETA = [_c(r[c]["OUT"].T) for c in range(8)]

    nc = _prog('l2s', build_l2scale, T, NH, DK)
    r = _launch(nc, [{"QT": _c(QC[c].T), "KT": _c(KC[c].T), "VT": _c(VC[c].T),
                      "BETA": BETA[c]} for c in range(8)])
    QN = [r[c]["QN"] for c in range(8)]
    KN = [r[c]["KN"] for c in range(8)]
    VB = [r[c]["VB"] for c in range(8)]
    KB = [r[c]["KB"] for c in range(8)]

    CH = 32
    nch = T // CH
    sl_m = np.tril(np.ones((CH, CH), np.float32), -1)
    lo_m = np.tril(np.ones((CH, CH), np.float32))
    MSK = _c(np.concatenate([sl_m, sl_m.T, lo_m.T, np.eye(CH, dtype=np.float32)], 1))
    nc = _prog('delta', build_delta, T, DK, DV, CH)
    OD = [np.zeros((T, NH * DV), np.float32) for _ in range(8)]
    for h in range(NH):
        hs = slice(h * DK, (h + 1) * DK)
        maps = []
        for c in range(8):
            maps.append({
                "QF": _c(QN[c][:, hs].T), "KF": _c(KN[c][:, hs].T),
                "KBF": _c(KB[c][:, hs].T),
                "KTC": _c(KN[c][:, hs].reshape(nch, CH, DK)),
                "KBTC": _c(KB[c][:, hs].reshape(nch, CH, DK)),
                "VTC": _c(VB[c][:, hs].reshape(nch, CH, DV)),
                "MSK": MSK})
        r = _launch(nc, maps)
        for c in range(8):
            OD[c][:, hs] = r[c]["O"].reshape(T, DV)

    nc = _prog('fir5', build_conv, Dm, T, 5)
    W = _c(g['L1_attn_fir_short_filt'].reshape(Dm, 5))
    r = _launch(nc, [{"X": _c(VC[c]), "W": W} for c in range(8)])
    FS = [r[c]["OUT"] for c in range(8)]

    nc = _prog('fir64', build_conv, Dm, T, 64)
    W = _c(g['L1_attn_fir_long_filt'].reshape(Dm, 64))
    r = _launch(nc, [{"X": _c(VC[c]), "W": W} for c in range(8)])
    FL = [r[c]["OUT"] for c in range(8)]

    nc = _prog('stats', build_stats, T, NH, DV)
    r = _launch(nc, [{"FS": _c(FS[c].T), "FL": _c(FL[c].T), "OD": _c(OD[c]),
                      "VD": _c(VC[c].T)} for c in range(8)])
    ST = [r[c]["ST"] for c in range(8)]

    nc_g1 = _prog('gate1', build_matmul, Dm + 16, Dm, T, bias=True)
    nc_ge = _prog('gelu', build_gelu, Dm, T)
    nc_g2 = _prog('gate2', build_matmul, Dm, NH, T, bias=True)
    Wg1 = _c(g['L1_attn_gate_W1'].T)
    Wg2 = _c(g['L1_attn_gate_W2'].T)
    Bg2 = _c(g['L1_attn_gate_b2'].reshape(-1, 1))
    Bg1 = _c(g['L1_attn_gate_b1'].reshape(-1, 1))
    LG = [np.zeros((T, NH * 4), np.float32) for _ in range(8)]
    for h in range(NH):
        maps = [{"W": Wg1, "B": Bg1, "R": _c(np.concatenate(
            [XN1T[c], ST[c][:, h * 16:(h + 1) * 16].T], 0))} for c in range(8)]
        r = _launch(nc_g1, maps)
        G1 = [r[c]["OUT"] for c in range(8)]
        r = _launch(nc_ge, [{"X": _c(G1[c])} for c in range(8)])
        H1G = [r[c]["OUT"] for c in range(8)]
        r = _launch(nc_g2, [{"W": Wg2, "R": _c(H1G[c]), "B": Bg2} for c in range(8)])
        for c in range(8):
            LG[c][:, h * 4:(h + 1) * 4] = r[c]["OUT"].T

    nc = _prog('gsoft', build_gate_softmax, T, NH)
    CB = _c(np.tile(g['L1_attn_gate_copy_bias'], (128, 1)))
    LTt = _c(np.tile(g['L1_attn_gate_log_temp'], (128, 1)))
    r = _launch(nc, [{"LG": _c(LG[c]), "CB": CB, "LT": LTt} for c in range(8)])
    WGT = [r[c]["WGT"] for c in range(8)]

    nc = _prog('mix', build_mix, T, NH, DV)
    OWB = _c(np.tile(np.tile(g['L1_attn_o_norm_w'], NH), (128, 1)))
    r = _launch(nc, [{"FS": _c(FS[c].T), "FL": _c(FL[c].T), "OD": _c(OD[c]),
                      "VD": _c(VC[c].T), "WGT": _c(WGT[c]), "OWB": OWB}
                     for c in range(8)])
    OMIX = [r[c]["OMIX"] for c in range(8)]

    nc = _prog('proj1024', build_matmul, Dm, Dm, T)
    W = _c(g['L1_attn_Wo'].T)
    r = _launch(nc, [{"W": W, "R": _c(OMIX[c].T)} for c in range(8)])
    ATT1 = [r[c]["OUT"] for c in range(8)]

    nc = _prog('routerm', build_matmul, Dm, E, T, act='sigmoid', bias=True,
               clip50=True)
    W = _c(g['L1_ffn_router_w'].T)
    B = _c(g['L1_ffn_routing_bias'].reshape(-1, 1))
    r = _launch(nc, [{"W": W, "R": XN1T[c], "B": B} for c in range(8)])
    PRF = [r[c]["OUT"] for c in range(8)]

    nc = _prog('rsel', build_router, T, E)
    r = _launch(nc, [{"PR": _c(PRF[c].T)} for c in range(8)])
    WS = [r[c]["WS"] for c in range(8)]
    SH = [r[c]["SH"] for c in range(8)]

    nc = _prog('moe', build_moe, Dm, I, T, E)
    W1S = _c(np.concatenate([g['L1_ffn_sw1'].T[None]]
                            + [g['L1_ffn_ew1'][e].T[None] for e in range(E)], 0
                            ).reshape(-1, I))
    W3S = _c(np.concatenate([g['L1_ffn_sw3'].T[None]]
                            + [g['L1_ffn_ew3'][e].T[None] for e in range(E)], 0
                            ).reshape(-1, I))
    W2S = _c(np.concatenate([g['L1_ffn_sw2'].T[None]]
                            + [g['L1_ffn_ew2'][e].T[None] for e in range(E)], 0
                            ).reshape(-1, Dm))
    maps = []
    for c in range(8):
        scs = np.concatenate([SH[c].reshape(1, T), WS[c].T], 0)
        maps.append({"R": XN1T[c], "SCS": _c(scs), "W1S": W1S, "W3S": W3S,
                     "W2S": W2S})
    r = _launch(nc, maps)
    ACC = [r[c]["OUT"] for c in range(8)]

    nc = _prog('comb3', build_combine, T, Dm, 3)
    r = _launch(nc, [{"X0": _c(X1[c]), "X1": _c(ATT1[c].T), "X2": _c(ACC[c].T)}
                     for c in range(8)])
    out = np.stack([r[c]["OUT"] for c in range(8)], 0).astype(np.float32)
    return out


# version 04 (final)
# speedup vs baseline: 1.0170x; 1.0170x over previous
import numpy as np
import concourse.bass as bass
import concourse.mybir as mybir
from concourse.bass_utils import run_bass_kernel_spmd

F32 = mybir.dt.float32
BF16 = mybir.dt.bfloat16
F8E5 = mybir.dt.float8e5
AF = mybir.ActivationFunctionType
ALU = mybir.AluOpType
AX = mybir.AxisListType


class P:

    def __init__(self, nc, ktile=None):
        self.nc = nc
        self.ops = []
        self.ktile = ktile
        self.ksrc = None
        self.kmap = {}
        self.kvals = []

    def op(self, eng, fn, inc=1):
        self.ops.append((eng, fn, inc))

    def konst(self, v):
        v = float(v)
        if v not in self.kmap:
            i = len(self.kmap)
            assert i < 16 and self.ktile.shape[1] >= 16
            if i == 0:
                self.dma('sync', self.ktile[:, 0:16], self.ksrc[:, 0:16])
            self.kmap[v] = self.ktile[:, i:i + 1]
            self.kvals.append(v)
        return self.kmap[v]

    def dma(self, queue, out, in_):
        self.op(queue, lambda e: e.dma_start(out=out, in_=in_), 16)

    def emit(self):
        nc = self.nc
        nc._kvals = list(self.kvals)
        befores = []
        tot = 0
        for (_, _, inc) in self.ops:
            befores.append(tot)
            tot += inc
        self.total = tot
        with nc.semaphore("tk") as tk, nc.Block() as block:
            def emit_for(name):
                def inner(eng):
                    last_wait = -1
                    for (en, fn, inc), before in zip(self.ops, befores):
                        if en != name:
                            continue
                        if before > 0 and before > last_wait:
                            eng.wait_ge(tk, before)
                            last_wait = before
                        inst = fn(eng)
                        inst.then_inc(tk, inc)
                    if name == 'sync':
                        eng.wait_ge(tk, tot)
                    return None
                return inner
            used = {en for (en, _, _) in self.ops} | {'sync'}
            if 'sync' in used:
                block.sync(emit_for('sync'))
            if 'tensor' in used:
                block.tensor(emit_for('tensor'))
            if 'vector' in used:
                block.vector(emit_for('vector'))
            if 'scalar' in used:
                block.scalar(emit_for('scalar'))
            if 'gpsimd' in used:
                block.gpsimd(emit_for('gpsimd'))
        return tot


def ceil_div(a, b):
    return (a + b - 1) // b


def run8(nc, in_maps, **kw):
    return run_bass_kernel_spmd(nc, in_maps, core_ids=list(range(8)), **kw)


def build_matmul(K, M, N, act='none', bias=False, mulT=False, addT=False,
                 f8=False, out_bf16_round=False, clip50=False):
    nc = bass.Bass("TRN2", target_bir_lowering=False)
    KON = nc.dram_tensor("KONST", [128, 16], F32, kind="ExternalInput").ap()
    W = nc.dram_tensor("W", [K, M], F32, kind="ExternalInput").ap()
    R = nc.dram_tensor("R", [K, N], F32, kind="ExternalInput").ap()
    OUT = nc.dram_tensor("OUT", [M, N], F32, kind="ExternalOutput").ap()
    Bv = nc.dram_tensor("B", [M, 1], F32, kind="ExternalInput").ap() if bias else None
    MT = nc.dram_tensor("MT", [M, N], F32, kind="ExternalInput").ap() if mulT else None
    AT = nc.dram_tensor("AT", [M, N], F32, kind="ExternalInput").ap() if addT else None

    NC = 512
    kch = ceil_div(K, 128)
    mch = ceil_div(M, 128)
    nch = ceil_div(N, NC)

    p = P(nc)
    with (
        nc.sbuf_tensor("Wt", [128, kch * 128], F32) as Wt,
        nc.sbuf_tensor("Rt", [128, kch * NC], F32) as Rt,
        nc.sbuf_tensor("Wq", [128, kch * 128], F8E5) as Wq,
        nc.sbuf_tensor("Rq", [128, kch * NC], F8E5) as Rq,
        nc.sbuf_tensor("bt", [128, 4], F32) as bt,
        nc.sbuf_tensor("bq", [128, 1], F8E5) as bq,
        nc.sbuf_tensor("ot", [128, NC], F32) as ot,
        nc.sbuf_tensor("t1", [128, NC], F32) as t1,
        nc.sbuf_tensor("t2", [128, NC], F32) as t2,
        nc.sbuf_tensor("mt", [128, NC], F32) as mt,
        nc.sbuf_tensor("rb", [128, NC], BF16) as rb,
        nc.sbuf_tensor("kt", [128, 16], F32) as kt,
        nc.psum_tensor("ps", [128, NC], F32) as ps,
    ):
        p.ktile = kt
        p.ksrc = KON
        for ni in range(nch):
            n0 = ni * NC
            nw = min(NC, N - n0)
            for j in range(kch):
                k0 = j * 128
                kw = min(128, K - k0)
                p.dma('sync', Rt[0:kw, j * NC:j * NC + nw], R[k0:k0 + kw, n0:n0 + nw])
                if f8:
                    p.op('vector', lambda e, a=Rq[0:kw, j * NC:j * NC + nw],
                         b=Rt[0:kw, j * NC:j * NC + nw]: e.tensor_copy(a, b))
            for mi in range(mch):
                m0 = mi * 128
                mw = min(128, M - m0)
                for j in range(kch):
                    k0 = j * 128
                    kw = min(128, K - k0)
                    p.dma('sync', Wt[0:kw, j * 128:j * 128 + mw], W[k0:k0 + kw, m0:m0 + mw])
                    if f8:
                        p.op('vector', lambda e, a=Wq[0:kw, j * 128:j * 128 + mw],
                             b=Wt[0:kw, j * 128:j * 128 + mw]: e.tensor_copy(a, b))
                if bias:
                    p.dma('sync', bt[0:mw, 0:1], Bv[m0:m0 + mw, :])
                    if f8:
                        p.op('vector', lambda e, a=bq[0:mw, 0:1], b=bt[0:mw, 0:1]:
                             e.tensor_copy(a, b))
                        p.op('vector', lambda e, a=bt[0:mw, 0:1], b=bq[0:mw, 0:1]:
                             e.tensor_copy(a, b))
                for j in range(kch):
                    k0 = j * 128
                    kw = min(128, K - k0)
                    lhs = (Wq if f8 else Wt)[0:kw, j * 128:j * 128 + mw]
                    rhs = (Rq if f8 else Rt)[0:kw, j * NC:j * NC + nw]
                    p.op('tensor', lambda e, o=ps[0:mw, 0:nw], l=lhs, r=rhs,
                         s=(j == 0), t=(j == kch - 1):
                         e.matmul(o, l, r, start=s, stop=t))
                src = ps[0:mw, 0:nw]
                dst = ot[0:mw, 0:nw]
                bap = bt[0:mw, 0:1] if bias else p.konst(0.0)[0:mw, 0:1]
                if clip50:
                    p.op('vector', lambda e, o=t1[0:mw, 0:nw], i=src, b=bap:
                         e.tensor_scalar(o, i, b, -50.0, ALU.add, ALU.max))
                    p.op('vector', lambda e, o=t1[0:mw, 0:nw], i=t1[0:mw, 0:nw]:
                         e.tensor_scalar(o, i, 50.0, None, ALU.min))
                    src = t1[0:mw, 0:nw]
                    bap = p.konst(0.0)[0:mw, 0:1]
                if act == 'none':
                    p.op('vector', lambda e, o=dst, i=src, b=bap:
                         e.tensor_scalar(o, i, b, None, ALU.add))
                elif act in ('relu', 'silu', 'sigmoid'):
                    fmap = {'relu': AF.Relu, 'silu': AF.Silu, 'sigmoid': AF.Sigmoid}
                    p.op('scalar', lambda e, o=dst, i=src, b=bap, f=fmap[act]:
                         e.activation(o, i, f, bias=b))
                elif act == 'softplus':
                    p.op('vector', lambda e, o=t1[0:mw, 0:nw], i=src, b=bap:
                         e.tensor_scalar(o, i, b, None, ALU.add))
                    p.op('scalar', lambda e, o=t2[0:mw, 0:nw], i=t1[0:mw, 0:nw],
                         z=p.konst(0.0)[0:mw, 0:1]:
                         e.activation(o, i, AF.Abs, bias=z))
                    p.op('scalar', lambda e, o=t2[0:mw, 0:nw], i=t2[0:mw, 0:nw],
                         s=p.konst(-1.0)[0:mw, 0:1], z=p.konst(0.0)[0:mw, 0:1]:
                         e.activation(o, i, AF.Exp, bias=z, scale=s))
                    p.op('scalar', lambda e, o=t2[0:mw, 0:nw], i=t2[0:mw, 0:nw],
                         b=p.konst(1.0)[0:mw, 0:1]:
                         e.activation(o, i, AF.Ln, bias=b))
                    p.op('vector', lambda e, o=t1[0:mw, 0:nw], i=t1[0:mw, 0:nw]:
                         e.tensor_scalar(o, i, 0.0, None, ALU.max))
                    p.op('vector', lambda e, o=dst, a=t1[0:mw, 0:nw], b=t2[0:mw, 0:nw]:
                         e.tensor_add(o, a, b))
                else:
                    raise ValueError(act)
                if out_bf16_round:
                    p.op('vector', lambda e, o=rb[0:mw, 0:nw], i=dst:
                         e.tensor_copy(o, i))
                    p.op('vector', lambda e, o=dst, i=rb[0:mw, 0:nw]:
                         e.tensor_copy(o, i))
                if mulT:
                    p.dma('sync', mt[0:mw, 0:nw], MT[m0:m0 + mw, n0:n0 + nw])
                    p.op('vector', lambda e, o=dst, a=dst, b=mt[0:mw, 0:nw]:
                         e.tensor_mul(o, a, b))
                if addT:
                    p.dma('sync', mt[0:mw, 0:nw], AT[m0:m0 + mw, n0:n0 + nw])
                    p.op('vector', lambda e, o=dst, a=dst, b=mt[0:mw, 0:nw]:
                         e.tensor_add(o, a, b))
                p.dma('sync', OUT[m0:m0 + mw, n0:n0 + nw], dst)
        p.emit()
    return nc


def build_lnorm0(T, D):
    nc = bass.Bass("TRN2", target_bir_lowering=False)
    KON = nc.dram_tensor("KONST", [128, 16], F32, kind="ExternalInput").ap()
    X = nc.dram_tensor("X", [T, D], F32, kind="ExternalInput").ap()
    GB = nc.dram_tensor("GB", [128, 2 * D], F32, kind="ExternalInput").ap()
    XN0 = nc.dram_tensor("XN0", [T, D], F32, kind="ExternalOutput").ap()
    XIN = nc.dram_tensor("XIN", [T, D], F32, kind="ExternalOutput").ap()
    tch = ceil_div(T, 128)
    p = P(nc)
    with (
        nc.sbuf_tensor("xt", [128, D], F32) as xt,
        nc.sbuf_tensor("gb", [128, 2 * D], F32) as gb,
        nc.sbuf_tensor("sq", [128, D], F32) as sq,
        nc.sbuf_tensor("xn", [128, D], F32) as xn,
        nc.sbuf_tensor("xa", [128, D], F32) as xa,
        nc.sbuf_tensor("c1", [128, 8], F32) as c1,
        nc.sbuf_tensor("kt", [128, 16], F32) as kt,
    ):
        p.ktile = kt
        p.ksrc = KON
        p.dma('sync', gb[:, :], GB[:, :])
        for ti in range(tch):
            t0 = ti * 128
            p.dma('sync', xt[:, :], X[t0:t0 + 128, :])
            for (src, dst) in ((xt, xn), (xn, xa)):
                p.op('vector', lambda e, o=c1[:, 0:1], i=src[:, :]:
                     e.reduce_sum(o, i, axis=AX.X))
                p.op('vector', lambda e, o=c1[:, 0:1], i=c1[:, 0:1]:
                     e.tensor_scalar(o, i, 1.0 / D, None, ALU.mult))
                p.op('scalar', lambda e, o=sq[:, :], i=src[:, :], z=p.konst(0.0):
                     e.activation(o, i, AF.Square, bias=z))
                p.op('vector', lambda e, o=c1[:, 1:2], i=sq[:, :]:
                     e.reduce_sum(o, i, axis=AX.X))
                p.op('vector', lambda e, o=c1[:, 1:2], i=c1[:, 1:2]:
                     e.tensor_scalar(o, i, 1.0 / D, None, ALU.mult))
                p.op('vector', lambda e, o=c1[:, 2:3], a=c1[:, 0:1], b=c1[:, 0:1]:
                     e.tensor_mul(o, a, b))
                p.op('vector', lambda e, o=c1[:, 1:2], a=c1[:, 1:2], b=c1[:, 2:3]:
                     e.tensor_sub(o, a, b))
                p.op('scalar', lambda e, o=c1[:, 3:4], i=c1[:, 1:2], b5=p.konst(1e-5):
                     e.activation(o, i, AF.Ln, bias=b5))
                p.op('scalar', lambda e, o=c1[:, 4:5], i=c1[:, 3:4], s=p.konst(-0.5),
                     z=p.konst(0.0):
                     e.activation(o, i, AF.Exp, bias=z, scale=s))
                p.op('vector', lambda e, o=dst[:, :], i=src[:, :], m=c1[:, 0:1],
                     r=c1[:, 4:5]:
                     e.tensor_scalar(o, i, m, r, ALU.subtract, ALU.mult))
            p.op('vector', lambda e, o=xa[:, :], a=xa[:, :], b=gb[:, 0:D]:
                 e.tensor_mul(o, a, b))
            p.op('vector', lambda e, o=xa[:, :], a=xa[:, :], b=gb[:, D:2 * D]:
                 e.tensor_add(o, a, b))
            p.dma('sync', XN0[t0:t0 + 128, :], xn[:, :])
            p.dma('sync', XIN[t0:t0 + 128, :], xa[:, :])
        p.emit()
    return nc


def build_lnorm(T, D):
    nc = bass.Bass("TRN2", target_bir_lowering=False)
    KON = nc.dram_tensor("KONST", [128, 16], F32, kind="ExternalInput").ap()
    X = nc.dram_tensor("X", [T, D], F32, kind="ExternalInput").ap()
    XN = nc.dram_tensor("XN", [T, D], F32, kind="ExternalOutput").ap()
    tch = ceil_div(T, 128)
    p = P(nc)
    with (
        nc.sbuf_tensor("xt", [128, D], F32) as xt,
        nc.sbuf_tensor("sq", [128, D], F32) as sq,
        nc.sbuf_tensor("xn", [128, D], F32) as xn,
        nc.sbuf_tensor("c1", [128, 8], F32) as c1,
        nc.sbuf_tensor("kt", [128, 16], F32) as kt,
    ):
        p.ktile = kt
        p.ksrc = KON
        for ti in range(tch):
            t0 = ti * 128
            p.dma('sync', xt[:, :], X[t0:t0 + 128, :])
            p.op('vector', lambda e, o=c1[:, 0:1], i=xt[:, :]:
                 e.reduce_sum(o, i, axis=AX.X))
            p.op('vector', lambda e, o=c1[:, 0:1], i=c1[:, 0:1]:
                 e.tensor_scalar(o, i, 1.0 / D, None, ALU.mult))
            p.op('scalar', lambda e, o=sq[:, :], i=xt[:, :], z=p.konst(0.0):
                 e.activation(o, i, AF.Square, bias=z))
            p.op('vector', lambda e, o=c1[:, 1:2], i=sq[:, :]:
                 e.reduce_sum(o, i, axis=AX.X))
            p.op('vector', lambda e, o=c1[:, 1:2], i=c1[:, 1:2]:
                 e.tensor_scalar(o, i, 1.0 / D, None, ALU.mult))
            p.op('vector', lambda e, o=c1[:, 2:3], a=c1[:, 0:1], b=c1[:, 0:1]:
                 e.tensor_mul(o, a, b))
            p.op('vector', lambda e, o=c1[:, 1:2], a=c1[:, 1:2], b=c1[:, 2:3]:
                 e.tensor_sub(o, a, b))
            p.op('scalar', lambda e, o=c1[:, 3:4], i=c1[:, 1:2], b5=p.konst(1e-5):
                 e.activation(o, i, AF.Ln, bias=b5))
            p.op('scalar', lambda e, o=c1[:, 4:5], i=c1[:, 3:4], s=p.konst(-0.5),
                 z=p.konst(0.0):
                 e.activation(o, i, AF.Exp, bias=z, scale=s))
            p.op('vector', lambda e, o=xn[:, :], i=xt[:, :], m=c1[:, 0:1],
                 r=c1[:, 4:5]:
                 e.tensor_scalar(o, i, m, r, ALU.subtract, ALU.mult))
            p.dma('sync', XN[t0:t0 + 128, :], xn[:, :])
        p.emit()
    return nc


def build_conv(C, T, KK, act='none', bias=False):
    nc = bass.Bass("TRN2", target_bir_lowering=False)
    KON = nc.dram_tensor("KONST", [128, 16], F32, kind="ExternalInput").ap()
    X = nc.dram_tensor("X", [C, T], F32, kind="ExternalInput").ap()
    W = nc.dram_tensor("W", [C, KK], F32, kind="ExternalInput").ap()
    Bv = nc.dram_tensor("B", [C, 1], F32, kind="ExternalInput").ap() if bias else None
    OUT = nc.dram_tensor("OUT", [C, T], F32, kind="ExternalOutput").ap()
    cch = ceil_div(C, 128)
    p = P(nc)
    with (
        nc.sbuf_tensor("xt", [128, KK - 1 + T], F32) as xt,
        nc.sbuf_tensor("wt", [128, KK], F32) as wt,
        nc.sbuf_tensor("bt", [128, 1], F32) as bt,
        nc.sbuf_tensor("a1", [128, T], F32) as a1,
        nc.sbuf_tensor("a2", [128, T], F32) as a2,
        nc.sbuf_tensor("kt", [128, 16], F32) as kt,
    ):
        p.ktile = kt
        p.ksrc = KON
        for ci in range(cch):
            c0 = ci * 128
            p.op('vector', lambda e, o=xt[:, 0:KK - 1]: e.memset(o, 0.0))
            p.dma('sync', xt[:, KK - 1:], X[c0:c0 + 128, :])
            p.dma('sync', wt[:, :], W[c0:c0 + 128, :])
            if bias:
                p.dma('sync', bt[:, :], Bv[c0:c0 + 128, :])
            acc = [a1, a2]
            p.op('vector', lambda e, o=a1[:, :], i=xt[:, 0:T], w=wt[:, 0:1]:
                 e.tensor_scalar(o, i, w, None, ALU.mult))
            cur = 0
            for k in range(1, KK):
                src, dst = acc[cur], acc[1 - cur]
                p.op('vector', lambda e, o=dst[:, :], i=xt[:, k:k + T],
                     w=wt[:, k:k + 1], ac=src[:, :]:
                     e.scalar_tensor_tensor(o, i, w, ac, ALU.mult, ALU.add))
                cur = 1 - cur
            fin = acc[cur]
            bap = bt[:, 0:1] if bias else p.konst(0.0)
            if act == 'silu':
                p.op('scalar', lambda e, o=fin[:, :], i=fin[:, :], b=bap:
                     e.activation(o, i, AF.Silu, bias=b))
            elif bias:
                p.op('vector', lambda e, o=fin[:, :], i=fin[:, :], b=bap:
                     e.tensor_scalar(o, i, b, None, ALU.add))
            p.dma('sync', OUT[c0:c0 + 128, :], fin[:, :])
        p.emit()
    return nc


def build_scan(D, T, N):
    nc = bass.Bass("TRN2", target_bir_lowering=False)
    KON = nc.dram_tensor("KONST", [128, 16], F32, kind="ExternalInput").ap()
    DL = nc.dram_tensor("DL", [D, T], F32, kind="ExternalInput").ap()
    U = nc.dram_tensor("U", [D, T], F32, kind="ExternalInput").ap()
    AL = nc.dram_tensor("AL", [D, N], F32, kind="ExternalInput").ap()
    BB = nc.dram_tensor("BB", [N, 128, T], F32, kind="ExternalInput").ap()
    CC = nc.dram_tensor("CC", [N, 128, T], F32, kind="ExternalInput").ap()
    Y = nc.dram_tensor("Y", [D, T], F32, kind="ExternalOutput").ap()
    dch = ceil_div(D, 128)
    TC = 1024
    tch = ceil_div(T, TC)
    p = P(nc)
    with (
        nc.sbuf_tensor("bb", [128, N * TC], F32) as bb,
        nc.sbuf_tensor("cc", [128, N * TC], F32) as cc,
        nc.sbuf_tensor("dl", [128, TC], F32) as dl,
        nc.sbuf_tensor("ut", [128, TC], F32) as ut,
        nc.sbuf_tensor("du", [128, TC], F32) as du,
        nc.sbuf_tensor("da", [128, TC], F32) as da,
        nc.sbuf_tensor("bt", [128, TC], F32) as bt,
        nc.sbuf_tensor("ht", [128, TC], F32) as ht,
        nc.sbuf_tensor("zt", [128, TC], F32) as zt,
        nc.sbuf_tensor("y1", [128, TC], F32) as y1,
        nc.sbuf_tensor("y2", [128, TC], F32) as y2,
        nc.sbuf_tensor("at", [128, N], F32) as at,
        nc.sbuf_tensor("st", [128, N * dch], F32) as st,
        nc.sbuf_tensor("kt", [128, 16], F32) as kt,
    ):
        p.ktile = kt
        p.ksrc = KON
        p.op('vector', lambda e, o=st[:, :]: e.memset(o, 0.0))
        for tc in range(tch):
            t0 = tc * TC
            for n in range(N):
                p.dma('sync', bb[:, n * TC:(n + 1) * TC], BB[n, :, t0:t0 + TC])
                p.dma('sync', cc[:, n * TC:(n + 1) * TC], CC[n, :, t0:t0 + TC])
            for di in range(dch):
                d0 = di * 128
                p.dma('sync', dl[:, :], DL[d0:d0 + 128, t0:t0 + TC])
                p.dma('sync', ut[:, :], U[d0:d0 + 128, t0:t0 + TC])
                p.dma('sync', at[:, :], AL[d0:d0 + 128, :])
                p.op('scalar', lambda e, o=at[:, :], i=at[:, :], z=p.konst(0.0):
                     e.activation(o, i, AF.Exp, bias=z))
                p.op('vector', lambda e, o=at[:, :], i=at[:, :]:
                     e.tensor_scalar(o, i, -1.0, None, ALU.mult))
                p.op('vector', lambda e, o=du[:, :], a=dl[:, :], b=ut[:, :]:
                     e.tensor_mul(o, a, b))
                ys = [y1, y2]
                for n in range(N):
                    p.op('scalar', lambda e, o=da[:, :], i=dl[:, :], s=at[:, n:n + 1],
                         z=p.konst(0.0):
                         e.activation(o, i, AF.Exp, bias=z, scale=s))
                    p.op('vector', lambda e, o=bt[:, :], a=du[:, :],
                         b=bb[:, n * TC:(n + 1) * TC]:
                         e.tensor_mul(o, a, b))
                    p.op('vector', lambda e, o=ht[:, :], d0_=da[:, :], d1=bt[:, :],
                         ini=st[:, di * N + n:di * N + n + 1]:
                         e.tensor_tensor_scan(o, d0_, d1, ini, ALU.mult, ALU.add))
                    p.op('vector', lambda e, o=st[:, di * N + n:di * N + n + 1],
                         i=ht[:, TC - 1:TC]: e.tensor_copy(o, i))
                    p.op('vector', lambda e, o=zt[:, :], a=ht[:, :],
                         b=cc[:, n * TC:(n + 1) * TC]:
                         e.tensor_mul(o, a, b))
                    if n == 0:
                        p.op('vector', lambda e, o=ys[0][:, :], i=zt[:, :]:
                             e.tensor_copy(o, i))
                    else:
                        src, dst = ys[(n + 1) % 2], ys[n % 2]
                        p.op('vector', lambda e, o=dst[:, :], a=src[:, :], b=zt[:, :]:
                             e.tensor_add(o, a, b))
                fin = ys[(N - 1) % 2]
                p.dma('sync', Y[d0:d0 + 128, t0:t0 + TC], fin[:, :])
        p.emit()
    return nc


def build_mamba_out(D, T):
    nc = bass.Bass("TRN2", target_bir_lowering=False)
    KON = nc.dram_tensor("KONST", [128, 16], F32, kind="ExternalInput").ap()
    Yi = nc.dram_tensor("Y", [D, T], F32, kind="ExternalInput").ap()
    U = nc.dram_tensor("U", [D, T], F32, kind="ExternalInput").ap()
    Dv = nc.dram_tensor("DV", [D, 1], F32, kind="ExternalInput").ap()
    Z = nc.dram_tensor("Z", [D, T], F32, kind="ExternalInput").ap()
    OUT = nc.dram_tensor("OUT", [D, T], F32, kind="ExternalOutput").ap()
    dch = ceil_div(D, 128)
    p = P(nc)
    with (
        nc.sbuf_tensor("yt", [128, T], F32) as yt,
        nc.sbuf_tensor("ut", [128, T], F32) as ut,
        nc.sbuf_tensor("zt", [128, T], F32) as zt,
        nc.sbuf_tensor("dv", [128, 1], F32) as dv,
        nc.sbuf_tensor("kt", [128, 16], F32) as kt,
    ):
        p.ktile = kt
        p.ksrc = KON
        for di in range(dch):
            d0 = di * 128
            p.dma('sync', yt[:, :], Yi[d0:d0 + 128, :])
            p.dma('sync', ut[:, :], U[d0:d0 + 128, :])
            p.dma('sync', zt[:, :], Z[d0:d0 + 128, :])
            p.dma('sync', dv[:, :], Dv[d0:d0 + 128, :])
            p.op('vector', lambda e, o=yt[:, :], i=ut[:, :], w=dv[:, 0:1],
                 ac=yt[:, :]:
                 e.scalar_tensor_tensor(o, i, w, ac, ALU.mult, ALU.add))
            p.op('scalar', lambda e, o=zt[:, :], i=zt[:, :], z=p.konst(0.0):
                 e.activation(o, i, AF.Silu, bias=z))
            p.op('vector', lambda e, o=yt[:, :], a=yt[:, :], b=zt[:, :]:
                 e.tensor_mul(o, a, b))
            p.dma('sync', OUT[d0:d0 + 128, :], yt[:, :])
        p.emit()
    return nc


def build_combine(M, N, k):
    nc = bass.Bass("TRN2", target_bir_lowering=False)
    KON = nc.dram_tensor("KONST", [128, 16], F32, kind="ExternalInput").ap()
    Xs = [nc.dram_tensor(f"X{i}", [M, N], F32, kind="ExternalInput").ap()
          for i in range(k)]
    OUT = nc.dram_tensor("OUT", [M, N], F32, kind="ExternalOutput").ap()
    mch = ceil_div(M, 128)
    p = P(nc)
    with (
        nc.sbuf_tensor("t0", [128, N], F32) as t0_,
        nc.sbuf_tensor("t1", [128, N], F32) as t1_,
    ):
        for mi in range(mch):
            m0 = mi * 128
            p.dma('sync', t0_[:, :], Xs[0][m0:m0 + 128, :])
            for i in range(1, k):
                p.dma('sync', t1_[:, :], Xs[i][m0:m0 + 128, :])
                p.op('vector', lambda e, o=t0_[:, :], a=t0_[:, :], b=t1_[:, :]:
                     e.tensor_add(o, a, b))
            p.dma('sync', OUT[m0:m0 + 128, :], t0_[:, :])
        p.emit()
    return nc


def kinput(nc):
    vals = getattr(nc, '_kvals', [])
    a = np.zeros(16, np.float32)
    a[:len(vals)] = vals
    return np.tile(a[None, :], (128, 1))


def build_l2scale(T, NH, DK):
    D = NH * DK
    nc = bass.Bass("TRN2", target_bir_lowering=False)
    KON = nc.dram_tensor("KONST", [128, 16], F32, kind="ExternalInput").ap()
    QT = nc.dram_tensor("QT", [T, D], F32, kind="ExternalInput").ap()
    KT = nc.dram_tensor("KT", [T, D], F32, kind="ExternalInput").ap()
    VT = nc.dram_tensor("VT", [T, D], F32, kind="ExternalInput").ap()
    BETA = nc.dram_tensor("BETA", [T, NH], F32, kind="ExternalInput").ap()
    QN = nc.dram_tensor("QN", [T, D], F32, kind="ExternalOutput").ap()
    KN = nc.dram_tensor("KN", [T, D], F32, kind="ExternalOutput").ap()
    VB = nc.dram_tensor("VB", [T, D], F32, kind="ExternalOutput").ap()
    KB = nc.dram_tensor("KB", [T, D], F32, kind="ExternalOutput").ap()
    tch = ceil_div(T, 128)
    p = P(nc)
    with (
        nc.sbuf_tensor("qt", [128, D], F32) as qt,
        nc.sbuf_tensor("kt2", [128, D], F32) as kt2,
        nc.sbuf_tensor("vt", [128, D], F32) as vt,
        nc.sbuf_tensor("bt", [128, NH], F32) as bt,
        nc.sbuf_tensor("sq", [128, DK], F32) as sq,
        nc.sbuf_tensor("ob", [128, D], F32) as ob,
        nc.sbuf_tensor("c1", [128, 8], F32) as c1,
        nc.sbuf_tensor("kt", [128, 16], F32) as kt,
    ):
        p.ktile = kt
        p.ksrc = KON
        for ti in range(tch):
            t0 = ti * 128
            p.dma('sync', qt[:, :], QT[t0:t0 + 128, :])
            p.dma('sync', kt2[:, :], KT[t0:t0 + 128, :])
            p.dma('sync', vt[:, :], VT[t0:t0 + 128, :])
            p.dma('sync', bt[:, :], BETA[t0:t0 + 128, :])
            for (src, out_, scaled, sout) in ((qt, QN, None, None), (kt2, KN, bt, KB)):
                for h in range(NH):
                    sl = src[:, h * DK:(h + 1) * DK]
                    p.op('scalar', lambda e, o=sq[:, :], i=sl, z=p.konst(0.0):
                         e.activation(o, i, AF.Square, bias=z))
                    p.op('vector', lambda e, o=c1[:, 0:1], i=sq[:, :]:
                         e.reduce_sum(o, i, axis=AX.X))
                    p.op('scalar', lambda e, o=c1[:, 1:2], i=c1[:, 0:1], b=p.konst(1e-6):
                         e.activation(o, i, AF.Ln, bias=b))
                    p.op('scalar', lambda e, o=c1[:, 2:3], i=c1[:, 1:2],
                         s=p.konst(-0.5), z=p.konst(0.0):
                         e.activation(o, i, AF.Exp, bias=z, scale=s))
                    p.op('vector', lambda e, o=ob[:, h * DK:(h + 1) * DK], i=sl,
                         r=c1[:, 2:3]:
                         e.tensor_scalar(o, i, r, None, ALU.mult))
                p.dma('sync', out_[t0:t0 + 128, :], ob[:, :])
                if scaled is not None:
                    for h in range(NH):
                        p.op('vector', lambda e, o=ob[:, h * DK:(h + 1) * DK],
                             i=ob[:, h * DK:(h + 1) * DK], b=scaled[:, h:h + 1]:
                             e.tensor_scalar(o, i, b, None, ALU.mult))
                    p.dma('sync', sout[t0:t0 + 128, :], ob[:, :])
            for h in range(NH):
                p.op('vector', lambda e, o=ob[:, h * DK:(h + 1) * DK],
                     i=vt[:, h * DK:(h + 1) * DK], b=bt[:, h:h + 1]:
                     e.tensor_scalar(o, i, b, None, ALU.mult))
            p.dma('sync', VB[t0:t0 + 128, :], ob[:, :])
        p.emit()
    return nc


def build_delta(T, DK, DV, CH):
    nch = T // CH
    nc = bass.Bass("TRN2", target_bir_lowering=False)
    KON = nc.dram_tensor("KONST", [128, 16], F32, kind="ExternalInput").ap()
    QF = nc.dram_tensor("QF", [DK, T], F32, kind="ExternalInput").ap()
    KF = nc.dram_tensor("KF", [DK, T], F32, kind="ExternalInput").ap()
    KBF = nc.dram_tensor("KBF", [DK, T], F32, kind="ExternalInput").ap()
    KTC = nc.dram_tensor("KTC", [nch, CH, DK], F32, kind="ExternalInput").ap()
    KBTC = nc.dram_tensor("KBTC", [nch, CH, DK], F32, kind="ExternalInput").ap()
    VTC = nc.dram_tensor("VTC", [nch, CH, DV], F32, kind="ExternalInput").ap()
    MSK = nc.dram_tensor("MSK", [CH, 4 * CH], F32, kind="ExternalInput").ap()
    O = nc.dram_tensor("O", [nch, CH, DV], F32, kind="ExternalOutput").ap()
    dkc = DK // 128
    p = P(nc)
    with (
        nc.sbuf_tensor("qf", [128, dkc * T], F32) as qf,
        nc.sbuf_tensor("kf", [128, dkc * T], F32) as kf,
        nc.sbuf_tensor("kbf", [128, dkc * T], F32) as kbf,
        nc.sbuf_tensor("ktc", [CH, DK], F32) as ktc,
        nc.sbuf_tensor("kbtc", [CH, DK], F32) as kbtc,
        nc.sbuf_tensor("vtc", [CH, DV], F32) as vtc,
        nc.sbuf_tensor("msk", [CH, 4 * CH], F32) as msk,
        nc.sbuf_tensor("S0", [128, DV], F32) as S0,
        nc.sbuf_tensor("S1", [128, DV], F32) as S1,
        nc.sbuf_tensor("tw", [CH, 32 * CH], F32) as tw,
        nc.sbuf_tensor("wf", [128, 2 * CH], F32) as wf,
        nc.sbuf_tensor("usb", [CH, DV], F32) as usb,
        nc.sbuf_tensor("u2", [CH, DV], F32) as u2,
        nc.sbuf_tensor("osb", [CH, DV], F32) as osb,
        nc.sbuf_tensor("kt", [128, 16], F32) as kt,
        nc.psum_tensor("ps", [CH, 512], F32) as ps,
        nc.psum_tensor("psS", [128, DV], F32) as psS,
    ):
        p.ktile = kt
        p.ksrc = KON
        p.dma('sync', msk[:, :], MSK[:, :])
        for j in range(dkc):
            p.dma('sync', qf[:, j * T:(j + 1) * T], QF[j * 128:(j + 1) * 128, :])
            p.dma('sync', kf[:, j * T:(j + 1) * T], KF[j * 128:(j + 1) * 128, :])
            p.dma('sync', kbf[:, j * T:(j + 1) * T], KBF[j * 128:(j + 1) * 128, :])
        p.op('vector', lambda e: e.memset(S0[:, :], 0.0))
        p.op('vector', lambda e: e.memset(S1[:, :], 0.0))
        sl = lambda i: tw[:, i * CH:(i + 1) * CH]
        M_ST, M_STT, M_LOWT, M_EYE = (msk[:, 0:CH], msk[:, CH:2 * CH],
                                      msk[:, 2 * CH:3 * CH], msk[:, 3 * CH:4 * CH])
        for ci in range(nch):
            c0 = ci * CH
            p.dma('sync', ktc[:, :], KTC[ci])
            p.dma('sync', kbtc[:, :], KBTC[ci])
            p.dma('sync', vtc[:, :], VTC[ci])
            for (x, y, slot, m) in ((kbf, kf, 0, M_ST), (kf, kbf, 1, M_STT)):
                for j in range(dkc):
                    p.op('tensor', lambda e, o=ps[0:CH, 0:CH],
                         l=x[:, j * T + c0:j * T + c0 + CH],
                         r=y[:, j * T + c0:j * T + c0 + CH],
                         s=(j == 0), t=(j == dkc - 1):
                         e.matmul(o, l, r, start=s, stop=t))
                p.op('vector', lambda e, o=sl(slot), i=ps[0:CH, 0:CH]:
                     e.tensor_scalar(o, i, -1.0, None, ALU.mult))
                p.op('vector', lambda e, o=sl(slot), a=sl(slot), b=m:
                     e.tensor_mul(o, a, b))
            p.op('vector', lambda e, o=sl(4), i=M_EYE: e.tensor_copy(o, i))
            p.op('vector', lambda e, o=sl(5), i=M_EYE: e.tensor_copy(o, i))
            p.op('vector', lambda e, o=sl(2), i=sl(0): e.tensor_copy(o, i))
            p.op('vector', lambda e, o=sl(3), i=sl(1): e.tensor_copy(o, i))
            for lev in range(5):
                p.op('tensor', lambda e, o=ps[0:CH, 0:CH], l=sl(3), r=sl(4):
                     e.matmul(o, l, r, start=True, stop=True))
                p.op('tensor', lambda e, o=ps[0:CH, CH:2 * CH], l=sl(4), r=sl(3):
                     e.matmul(o, l, r, start=True, stop=True))
                if lev < 4:
                    p.op('tensor', lambda e, o=ps[0:CH, 2 * CH:3 * CH], l=sl(3), r=sl(2):
                         e.matmul(o, l, r, start=True, stop=True))
                    p.op('tensor', lambda e, o=ps[0:CH, 3 * CH:4 * CH], l=sl(2), r=sl(3):
                         e.matmul(o, l, r, start=True, stop=True))
                p.op('vector', lambda e, o=sl(4), a=sl(4), b=ps[0:CH, 0:CH]:
                     e.tensor_add(o, a, b))
                p.op('vector', lambda e, o=sl(5), a=sl(5), b=ps[0:CH, CH:2 * CH]:
                     e.tensor_add(o, a, b))
                if lev < 4:
                    p.op('vector', lambda e, o=sl(2), i=ps[0:CH, 2 * CH:3 * CH]:
                         e.tensor_copy(o, i))
                    p.op('vector', lambda e, o=sl(3), i=ps[0:CH, 3 * CH:4 * CH]:
                         e.tensor_copy(o, i))
            p.op('tensor', lambda e, o=ps[0:CH, 0:DV], l=sl(5), r=vtc[:, :]:
                 e.matmul(o, l, r, start=True, stop=True))
            p.op('vector', lambda e, o=usb[:, :], i=ps[0:CH, 0:DV]:
                 e.tensor_copy(o, i))
            for j in range(dkc):
                p.op('tensor', lambda e, o=psS[0:128, 0:CH],
                     l=kbtc[:, j * 128:(j + 1) * 128], r=sl(5):
                     e.matmul(o, l, r, start=True, stop=True))
                p.op('vector', lambda e, o=wf[:, j * CH:(j + 1) * CH],
                     i=psS[0:128, 0:CH]: e.tensor_copy(o, i))
            p.op('tensor', lambda e, o=ps[0:CH, 0:DV], l=wf[:, 0:CH], r=S0[:, :]:
                 e.matmul(o, l, r, start=True, stop=False))
            p.op('tensor', lambda e, o=ps[0:CH, 0:DV], l=wf[:, CH:2 * CH], r=S1[:, :]:
                 e.matmul(o, l, r, start=False, stop=True))
            p.op('vector', lambda e, o=u2[:, :], a=usb[:, :], b=ps[0:CH, 0:DV]:
                 e.tensor_sub(o, a, b))
            for j in range(dkc):
                p.op('tensor', lambda e, o=ps[0:CH, 0:CH],
                     l=kf[:, j * T + c0:j * T + c0 + CH],
                     r=qf[:, j * T + c0:j * T + c0 + CH],
                     s=(j == 0), t=(j == dkc - 1):
                     e.matmul(o, l, r, start=s, stop=t))
            p.op('vector', lambda e, o=sl(6), a=ps[0:CH, 0:CH], b=M_LOWT:
                 e.tensor_mul(o, a, b))
            p.op('tensor', lambda e, o=ps[0:CH, 0:DV],
                 l=qf[:, 0 * T + c0:0 * T + c0 + CH], r=S0[:, :]:
                 e.matmul(o, l, r, start=True, stop=False))
            p.op('tensor', lambda e, o=ps[0:CH, 0:DV],
                 l=qf[:, 1 * T + c0:1 * T + c0 + CH], r=S1[:, :]:
                 e.matmul(o, l, r, start=False, stop=False))
            p.op('tensor', lambda e, o=ps[0:CH, 0:DV], l=sl(6), r=u2[:, :]:
                 e.matmul(o, l, r, start=False, stop=True))
            p.op('vector', lambda e, o=osb[:, :], i=ps[0:CH, 0:DV]:
                 e.tensor_copy(o, i))
            p.dma('sync', O[ci], osb[:, :])
            for (j, S) in ((0, S0), (1, S1)):
                p.op('tensor', lambda e, o=psS[0:128, 0:DV],
                     l=ktc[:, j * 128:(j + 1) * 128], r=u2[:, :]:
                     e.matmul(o, l, r, start=True, stop=True))
                p.op('vector', lambda e, o=S[:, :], a=S[:, :], b=psS[0:128, 0:DV]:
                     e.tensor_add(o, a, b))
        p.emit()
    return nc


def build_stats(T, NH, DV):
    D = NH * DV
    nc = bass.Bass("TRN2", target_bir_lowering=False)
    KON = nc.dram_tensor("KONST", [128, 16], F32, kind="ExternalInput").ap()
    ins = {nm: nc.dram_tensor(nm, [T, D], F32, kind="ExternalInput").ap()
           for nm in ("FS", "FL", "OD", "VD")}
    ST = nc.dram_tensor("ST", [T, NH * 16], F32, kind="ExternalOutput").ap()
    tch = ceil_div(T, 128)
    p = P(nc)
    with (
        nc.sbuf_tensor("xt", [128, D], F32) as xt,
        nc.sbuf_tensor("sq", [128, DV], F32) as sq,
        nc.sbuf_tensor("st", [128, NH * 16], F32) as st,
        nc.sbuf_tensor("c1", [128, 8], F32) as c1,
        nc.sbuf_tensor("kt", [128, 16], F32) as kt,
    ):
        p.ktile = kt
        p.ksrc = KON
        for ti in range(tch):
            t0 = ti * 128
            for w, nm in enumerate(("FS", "FL", "OD", "VD")):
                p.dma('sync', xt[:, :], ins[nm][t0:t0 + 128, :])
                for h in range(NH):
                    slx = xt[:, h * DV:(h + 1) * DV]
                    base = h * 16 + w * 4
                    p.op('vector', lambda e, o=c1[:, 0:1], i=slx:
                         e.reduce_sum(o, i, axis=AX.X))
                    p.op('vector', lambda e, o=st[:, base:base + 1], i=c1[:, 0:1]:
                         e.tensor_scalar(o, i, 1.0 / DV, None, ALU.mult))
                    p.op('scalar', lambda e, o=sq[:, :], i=slx, z=p.konst(0.0):
                         e.activation(o, i, AF.Square, bias=z))
                    p.op('vector', lambda e, o=c1[:, 1:2], i=sq[:, :]:
                         e.reduce_sum(o, i, axis=AX.X))
                    p.op('vector', lambda e, o=c1[:, 2:3], i=c1[:, 1:2]:
                         e.tensor_scalar(o, i, 1.0 / DV, None, ALU.mult))
                    p.op('vector', lambda e, o=c1[:, 3:4], a=st[:, base:base + 1],
                         b=st[:, base:base + 1]: e.tensor_mul(o, a, b))
                    p.op('vector', lambda e, o=st[:, base + 1:base + 2],
                         a=c1[:, 2:3], b=c1[:, 3:4]: e.tensor_sub(o, a, b))
                    p.op('scalar', lambda e, o=sq[:, :], i=slx, z=p.konst(0.0):
                         e.activation(o, i, AF.Abs, bias=z))
                    p.op('vector', lambda e, o=c1[:, 4:5], i=sq[:, :]:
                         e.reduce_sum(o, i, axis=AX.X))
                    p.op('vector', lambda e, o=st[:, base + 2:base + 3], i=c1[:, 4:5]:
                         e.tensor_scalar(o, i, 1.0 / DV, None, ALU.mult))
                    p.op('scalar', lambda e, o=c1[:, 5:6], i=c1[:, 1:2], z=p.konst(0.0):
                         e.activation(o, i, AF.Ln, bias=z))
                    p.op('scalar', lambda e, o=st[:, base + 3:base + 4], i=c1[:, 5:6],
                         s=p.konst(0.5), z=p.konst(0.0):
                         e.activation(o, i, AF.Exp, bias=z, scale=s))
            p.dma('sync', ST[t0:t0 + 128, :], st[:, :])
        p.emit()
    return nc


def build_gelu(M, N):
    nc = bass.Bass("TRN2", target_bir_lowering=False)
    KON = nc.dram_tensor("KONST", [128, 16], F32, kind="ExternalInput").ap()
    X = nc.dram_tensor("X", [M, N], F32, kind="ExternalInput").ap()
    OUT = nc.dram_tensor("OUT", [M, N], F32, kind="ExternalOutput").ap()
    mch = ceil_div(M, 128)
    C = float(np.sqrt(2.0 / np.pi))
    p = P(nc)
    with (
        nc.sbuf_tensor("xt", [128, N], F32) as xt,
        nc.sbuf_tensor("x2", [128, N], F32) as x2,
        nc.sbuf_tensor("tt", [128, N], F32) as tt,
        nc.sbuf_tensor("kt", [128, 16], F32) as kt,
    ):
        p.ktile = kt
        p.ksrc = KON
        for mi in range(mch):
            m0 = mi * 128
            p.dma('sync', xt[:, :], X[m0:m0 + 128, :])
            p.op('scalar', lambda e, o=x2[:, :], i=xt[:, :], z=p.konst(0.0):
                 e.activation(o, i, AF.Square, bias=z))
            p.op('vector', lambda e, o=x2[:, :], i=x2[:, :]:
                 e.tensor_scalar(o, i, C * 0.044715, C, ALU.mult, ALU.add))
            p.op('vector', lambda e, o=tt[:, :], a=x2[:, :], b=xt[:, :]:
                 e.tensor_mul(o, a, b))
            p.op('scalar', lambda e, o=tt[:, :], i=tt[:, :], z=p.konst(0.0):
                 e.activation(o, i, AF.Tanh, bias=z))
            p.op('vector', lambda e, o=tt[:, :], i=tt[:, :]:
                 e.tensor_scalar(o, i, 1.0, 0.5, ALU.add, ALU.mult))
            p.op('vector', lambda e, o=tt[:, :], a=tt[:, :], b=xt[:, :]:
                 e.tensor_mul(o, a, b))
            p.dma('sync', OUT[m0:m0 + 128, :], tt[:, :])
        p.emit()
    return nc


def build_gate_softmax(T, NH):
    DECAY = 1.0 - 1.0 / 3000.0
    EF = 0.08 * DECAY
    nc = bass.Bass("TRN2", target_bir_lowering=False)
    KON = nc.dram_tensor("KONST", [128, 16], F32, kind="ExternalInput").ap()
    LG = nc.dram_tensor("LG", [T, NH * 4], F32, kind="ExternalInput").ap()
    CB = nc.dram_tensor("CB", [128, NH], F32, kind="ExternalInput").ap()
    LTt = nc.dram_tensor("LT", [128, NH], F32, kind="ExternalInput").ap()
    WGT = nc.dram_tensor("WGT", [T, NH * 4], F32, kind="ExternalOutput").ap()
    tch = ceil_div(T, 128)
    p = P(nc)
    with (
        nc.sbuf_tensor("lg", [128, NH * 4], F32) as lg,
        nc.sbuf_tensor("cb", [128, NH], F32) as cb,
        nc.sbuf_tensor("lt", [128, NH], F32) as lt,
        nc.sbuf_tensor("rt", [128, NH], F32) as rt,
        nc.sbuf_tensor("c1", [128, 8], F32) as c1,
        nc.sbuf_tensor("ex", [128, 4], F32) as ex,
        nc.sbuf_tensor("kt", [128, 16], F32) as kt,
    ):
        p.ktile = kt
        p.ksrc = KON
        p.dma('sync', cb[:, :], CB[:, :])
        p.dma('sync', lt[:, :], LTt[:, :])
        p.op('vector', lambda e: e.tensor_scalar(cb[:, :], cb[:, :], DECAY, None, ALU.mult))
        p.op('scalar', lambda e, z=p.konst(0.0):
             e.activation(lt[:, :], lt[:, :], AF.Exp, bias=z))
        p.op('vector', lambda e: e.reciprocal(rt[:, :], lt[:, :]))
        for ti in range(tch):
            t0 = ti * 128
            p.dma('sync', lg[:, :], LG[t0:t0 + 128, :])
            for h in range(NH):
                s4 = lg[:, h * 4:h * 4 + 4]
                p.op('vector', lambda e, o=lg[:, h * 4 + 3:h * 4 + 4], b=cb[:, h:h + 1]:
                     e.tensor_scalar(o, o, b, None, ALU.add))
                p.op('vector', lambda e, o=s4, r=rt[:, h:h + 1]:
                     e.tensor_scalar(o, o, r, None, ALU.mult))
                p.op('vector', lambda e, o=c1[:, 0:1], i=s4:
                     e.reduce_max(o, i, axis=AX.X))
                p.op('vector', lambda e, o=c1[:, 1:2], i=c1[:, 0:1]:
                     e.tensor_scalar(o, i, -1.0, None, ALU.mult))
                p.op('scalar', lambda e, o=ex[:, :], i=s4, b=c1[:, 1:2]:
                     e.activation(o, i, AF.Exp, bias=b))
                p.op('vector', lambda e, o=c1[:, 2:3], i=ex[:, :]:
                     e.reduce_sum(o, i, axis=AX.X))
                p.op('vector', lambda e, o=c1[:, 3:4], i=c1[:, 2:3]:
                     e.reciprocal(o, i))
                p.op('vector', lambda e, o=ex[:, :], r=c1[:, 3:4]:
                     e.tensor_scalar(o, o, r, None, ALU.mult))
                p.op('vector', lambda e, o=s4, i=ex[:, :]:
                     e.tensor_scalar(o, i, 1.0 - 4.0 * EF, EF, ALU.mult, ALU.add))
            p.dma('sync', WGT[t0:t0 + 128, :], lg[:, :])
        p.emit()
    return nc


def build_mix(T, NH, DV):
    D = NH * DV
    nc = bass.Bass("TRN2", target_bir_lowering=False)
    KON = nc.dram_tensor("KONST", [128, 16], F32, kind="ExternalInput").ap()
    ins = {nm: nc.dram_tensor(nm, [T, D], F32, kind="ExternalInput").ap()
           for nm in ("FS", "FL", "OD", "VD")}
    WGT = nc.dram_tensor("WGT", [T, NH * 4], F32, kind="ExternalInput").ap()
    OWB = nc.dram_tensor("OWB", [128, D], F32, kind="ExternalInput").ap()
    OMIX = nc.dram_tensor("OMIX", [T, D], F32, kind="ExternalOutput").ap()
    tch = ceil_div(T, 128)
    p = P(nc)
    with (
        nc.sbuf_tensor("fs", [128, D], F32) as fs,
        nc.sbuf_tensor("fl", [128, D], F32) as fl,
        nc.sbuf_tensor("od", [128, D], F32) as od,
        nc.sbuf_tensor("vd", [128, D], F32) as vd,
        nc.sbuf_tensor("wg", [128, NH * 4], F32) as wg,
        nc.sbuf_tensor("ow", [128, D], F32) as ow,
        nc.sbuf_tensor("ot", [128, D], F32) as ot,
        nc.sbuf_tensor("sq", [128, DV], F32) as sq,
        nc.sbuf_tensor("c1", [128, 8], F32) as c1,
        nc.sbuf_tensor("kt", [128, 16], F32) as kt,
    ):
        p.ktile = kt
        p.ksrc = KON
        p.dma('sync', ow[:, :], OWB[:, :])
        for ti in range(tch):
            t0 = ti * 128
            for tile, nm in ((fs, "FS"), (fl, "FL"), (od, "OD"), (vd, "VD")):
                p.dma('sync', tile[:, :], ins[nm][t0:t0 + 128, :])
            p.dma('sync', wg[:, :], WGT[t0:t0 + 128, :])
            for h in range(NH):
                hd = slice(h * DV, (h + 1) * DV)
                p.op('vector', lambda e, o=ot[:, hd], i=fs[:, hd], w=wg[:, h * 4:h * 4 + 1]:
                     e.tensor_scalar(o, i, w, None, ALU.mult))
                for (tile, ci) in ((fl, 1), (od, 2), (vd, 3)):
                    p.op('vector', lambda e, o=ot[:, hd], i=tile[:, hd],
                         w=wg[:, h * 4 + ci:h * 4 + ci + 1], ac=ot[:, hd]:
                         e.scalar_tensor_tensor(o, i, w, ac, ALU.mult, ALU.add))
                p.op('scalar', lambda e, o=sq[:, :], i=ot[:, hd], z=p.konst(0.0):
                     e.activation(o, i, AF.Square, bias=z))
                p.op('vector', lambda e, o=c1[:, 0:1], i=sq[:, :]:
                     e.reduce_sum(o, i, axis=AX.X))
                p.op('vector', lambda e, o=c1[:, 1:2], i=c1[:, 0:1]:
                     e.tensor_scalar(o, i, 1.0 / DV, None, ALU.mult))
                p.op('scalar', lambda e, o=c1[:, 2:3], i=c1[:, 1:2], b=p.konst(1e-5):
                     e.activation(o, i, AF.Ln, bias=b))
                p.op('scalar', lambda e, o=c1[:, 3:4], i=c1[:, 2:3], s=p.konst(-0.5),
                     z=p.konst(0.0):
                     e.activation(o, i, AF.Exp, bias=z, scale=s))
                p.op('vector', lambda e, o=ot[:, hd], i=ot[:, hd], r=c1[:, 3:4]:
                     e.tensor_scalar(o, i, r, None, ALU.mult))
            p.op('vector', lambda e, o=ot[:, :], a=ot[:, :], b=ow[:, :]:
                 e.tensor_mul(o, a, b))
            p.dma('sync', OMIX[t0:t0 + 128, :], ot[:, :])
        p.emit()
    return nc


def build_router(T, E):
    nc = bass.Bass("TRN2", target_bir_lowering=False)
    KON = nc.dram_tensor("KONST", [128, 16], F32, kind="ExternalInput").ap()
    PR = nc.dram_tensor("PR", [T, E], F32, kind="ExternalInput").ap()
    WS = nc.dram_tensor("WS", [T, E], F32, kind="ExternalOutput").ap()
    SH = nc.dram_tensor("SH", [T, 1], F32, kind="ExternalOutput").ap()
    tch = ceil_div(T, 128)
    p = P(nc)
    with (
        nc.sbuf_tensor("pr", [128, E], F32) as pr,
        nc.sbuf_tensor("eq", [128, E], F32) as eq,
        nc.sbuf_tensor("c1", [128, 8], F32) as c1,
        nc.sbuf_tensor("kt", [128, 16], F32) as kt,
    ):
        p.ktile = kt
        p.ksrc = KON
        for ti in range(tch):
            t0 = ti * 128
            p.dma('sync', pr[:, :], PR[t0:t0 + 128, :])
            p.op('vector', lambda e, o=c1[:, 0:1], i=pr[:, :]:
                 e.reduce_max(o, i, axis=AX.X))
            p.op('vector', lambda e, o=eq[:, :], i=pr[:, :], m=c1[:, 0:1]:
                 e.tensor_scalar(o, i, m, None, ALU.is_ge))
            p.op('vector', lambda e, o=c1[:, 1:2], i=c1[:, 0:1]:
                 e.tensor_scalar(o, i, 1e-8, 1.0 - 1e-8, ALU.max, ALU.min))
            p.op('vector', lambda e, o=c1[:, 2:3], i=c1[:, 1:2]:
                 e.tensor_scalar(o, i, 0.5 + 1e-8, None, ALU.add))
            p.op('vector', lambda e, o=c1[:, 3:4], i=c1[:, 2:3]:
                 e.reciprocal(o, i))
            p.op('vector', lambda e, o=c1[:, 4:5], a=c1[:, 1:2], b=c1[:, 3:4]:
                 e.tensor_mul(o, a, b))
            p.op('vector', lambda e, o=eq[:, :], i=eq[:, :], w=c1[:, 4:5]:
                 e.tensor_scalar(o, i, w, None, ALU.mult))
            p.op('vector', lambda e, o=c1[:, 5:6], i=c1[:, 3:4]:
                 e.tensor_scalar(o, i, 0.5, None, ALU.mult))
            p.dma('sync', WS[t0:t0 + 128, :], eq[:, :])
            p.dma('sync', SH[t0:t0 + 128, :], c1[:, 5:6])
        p.emit()
    return nc


def build_swiglu(K, M, N):
    nc = bass.Bass("TRN2", target_bir_lowering=False)
    KON = nc.dram_tensor("KONST", [128, 16], F32, kind="ExternalInput").ap()
    W1 = nc.dram_tensor("W1", [K, M], F32, kind="ExternalInput").ap()
    W3 = nc.dram_tensor("W3", [K, M], F32, kind="ExternalInput").ap()
    R = nc.dram_tensor("R", [K, N], F32, kind="ExternalInput").ap()
    SC = nc.dram_tensor("SC", [128, N], F32, kind="ExternalInput").ap()
    H = nc.dram_tensor("H", [M, N], F32, kind="ExternalOutput").ap()
    NC = 512
    kch = ceil_div(K, 128)
    mch = ceil_div(M, 128)
    nch = ceil_div(N, NC)
    p = P(nc)
    with (
        nc.sbuf_tensor("w1t", [128, kch * 128], F32) as w1t,
        nc.sbuf_tensor("w3t", [128, kch * 128], F32) as w3t,
        nc.sbuf_tensor("rt", [128, kch * NC], F32) as rt,
        nc.sbuf_tensor("sc", [128, NC], F32) as sc,
        nc.sbuf_tensor("ht", [128, NC], F32) as ht,
        nc.sbuf_tensor("kt", [128, 16], F32) as kt,
        nc.psum_tensor("p1", [128, NC], F32) as p1,
        nc.psum_tensor("p3", [128, NC], F32) as p3,
    ):
        p.ktile = kt
        p.ksrc = KON
        for ni in range(nch):
            n0 = ni * NC
            nw = min(NC, N - n0)
            p.dma('sync', sc[:, 0:nw], SC[:, n0:n0 + nw])
            for j in range(kch):
                k0 = j * 128
                kw = min(128, K - k0)
                p.dma('sync', rt[0:kw, j * NC:j * NC + nw], R[k0:k0 + kw, n0:n0 + nw])
            for mi in range(mch):
                m0 = mi * 128
                mw = min(128, M - m0)
                for j in range(kch):
                    k0 = j * 128
                    kw = min(128, K - k0)
                    p.dma('sync', w1t[0:kw, j * 128:j * 128 + mw], W1[k0:k0 + kw, m0:m0 + mw])
                    p.dma('sync', w3t[0:kw, j * 128:j * 128 + mw], W3[k0:k0 + kw, m0:m0 + mw])
                for j in range(kch):
                    kw = min(128, K - j * 128)
                    p.op('tensor', lambda e, o=p1[0:mw, 0:nw],
                         l=w1t[0:kw, j * 128:j * 128 + mw], r=rt[0:kw, j * NC:j * NC + nw],
                         s=(j == 0), t=(j == kch - 1): e.matmul(o, l, r, start=s, stop=t))
                for j in range(kch):
                    kw = min(128, K - j * 128)
                    p.op('tensor', lambda e, o=p3[0:mw, 0:nw],
                         l=w3t[0:kw, j * 128:j * 128 + mw], r=rt[0:kw, j * NC:j * NC + nw],
                         s=(j == 0), t=(j == kch - 1): e.matmul(o, l, r, start=s, stop=t))
                p.op('scalar', lambda e, o=ht[0:mw, 0:nw], i=p1[0:mw, 0:nw], z=p.konst(0.0):
                     e.activation(o, i, AF.Silu, bias=z))
                p.op('vector', lambda e, o=ht[0:mw, 0:nw], a=ht[0:mw, 0:nw], b=p3[0:mw, 0:nw]:
                     e.tensor_mul(o, a, b))
                p.op('vector', lambda e, o=ht[0:mw, 0:nw], a=ht[0:mw, 0:nw], b=sc[0:mw, 0:nw]:
                     e.tensor_mul(o, a, b))
                p.dma('sync', H[m0:m0 + mw, n0:n0 + nw], ht[0:mw, 0:nw])
        p.emit()
    return nc


def build_moe(K, M, N, E):
    nc = bass.Bass("TRN2", target_bir_lowering=False)
    KON = nc.dram_tensor("KONST", [128, 16], F32, kind="ExternalInput").ap()
    R = nc.dram_tensor("R", [K, N], F32, kind="ExternalInput").ap()
    SCS = nc.dram_tensor("SCS", [1 + E, N], F32, kind="ExternalInput").ap()
    W1S = nc.dram_tensor("W1S", [(1 + E) * K, M], F32, kind="ExternalInput").ap()
    W3S = nc.dram_tensor("W3S", [(1 + E) * K, M], F32, kind="ExternalInput").ap()
    W2S = nc.dram_tensor("W2S", [(1 + E) * M, K], F32, kind="ExternalInput").ap()
    OUT = nc.dram_tensor("OUT", [K, N], F32, kind="ExternalOutput").ap()
    NC = 512
    kch = K // 128
    mch = ceil_div(M, 128)
    nch = ceil_div(N, NC)
    p = P(nc)
    with (
        nc.sbuf_tensor("rt", [128, kch * N], F32) as rt,
        nc.sbuf_tensor("acc", [128, kch * N], F32) as acc,
        nc.sbuf_tensor("w1t", [128, kch * 128], F32) as w1t,
        nc.sbuf_tensor("w3t", [128, kch * 128], F32) as w3t,
        nc.sbuf_tensor("w2t", [128, 128], F32) as w2t,
        nc.sbuf_tensor("sc", [128, N], F32) as sc,
        nc.sbuf_tensor("ht", [128, NC], F32) as ht,
        nc.sbuf_tensor("on", [1, 128], F32) as on,
        nc.sbuf_tensor("kt", [128, 16], F32) as kt,
        nc.psum_tensor("p1", [128, NC], F32) as p1,
        nc.psum_tensor("p3", [128, NC], F32) as p3,
        nc.psum_tensor("pb", [128, NC], F32) as pb,
    ):
        p.ktile = kt
        p.ksrc = KON
        p.op('vector', lambda e: e.memset(on[0:1, 0:128], 0.0))
        p.op('vector', lambda e: e.tensor_scalar(on[0:1, 0:128], on[0:1, 0:128],
                                                 1.0, None, ALU.add))
        for j in range(kch):
            p.dma('sync', rt[:, j * N:(j + 1) * N], R[j * 128:(j + 1) * 128, :])
        p.op('vector', lambda e: e.memset(acc[:, :], 0.0))
        for x in range(1 + E):
            p.dma('sync', sc[0:1, 0:N], SCS[x:x + 1, :])
            for ni2 in range(ceil_div(N, NC)):
                n2 = ni2 * NC
                p.op('tensor', lambda e, o=pb[0:128, 0:NC], l=on[0:1, 0:128],
                     r=sc[0:1, n2:n2 + NC]:
                     e.matmul(o, l, r, start=True, stop=True))
                p.op('vector', lambda e, o=sc[0:128, n2:n2 + NC], i=pb[0:128, 0:NC]:
                     e.tensor_copy(o, i))
            for ni in range(nch):
                n0 = ni * NC
                for mi in range(mch):
                    m0 = mi * 128
                    mw = min(128, M - m0)
                    for j in range(kch):
                        p.dma('sync', w1t[:, j * 128:j * 128 + mw],
                              W1S[x * K + j * 128:x * K + (j + 1) * 128, m0:m0 + mw])
                        p.dma('sync', w3t[:, j * 128:j * 128 + mw],
                              W3S[x * K + j * 128:x * K + (j + 1) * 128, m0:m0 + mw])
                    for j in range(kch):
                        p.op('tensor', lambda e, o=p1[0:mw, 0:NC],
                             l=w1t[:, j * 128:j * 128 + mw],
                             r=rt[:, j * N + n0:j * N + n0 + NC],
                             s=(j == 0), t=(j == kch - 1):
                             e.matmul(o, l, r, start=s, stop=t))
                    for j in range(kch):
                        p.op('tensor', lambda e, o=p3[0:mw, 0:NC],
                             l=w3t[:, j * 128:j * 128 + mw],
                             r=rt[:, j * N + n0:j * N + n0 + NC],
                             s=(j == 0), t=(j == kch - 1):
                             e.matmul(o, l, r, start=s, stop=t))
                    p.op('scalar', lambda e, o=ht[0:mw, 0:NC], i=p1[0:mw, 0:NC],
                         z=p.konst(0.0)[0:mw, 0:1]:
                         e.activation(o, i, AF.Silu, bias=z))
                    p.op('vector', lambda e, o=ht[0:mw, 0:NC], a=ht[0:mw, 0:NC],
                         b=p3[0:mw, 0:NC]: e.tensor_mul(o, a, b))
                    p.op('vector', lambda e, o=ht[0:mw, 0:NC], a=ht[0:mw, 0:NC],
                         b=sc[0:mw, n0:n0 + NC]: e.tensor_mul(o, a, b))
                    for kc2 in range(kch):
                        p.dma('sync', w2t[0:mw, 0:128],
                              W2S[x * M + m0:x * M + m0 + mw, kc2 * 128:(kc2 + 1) * 128])
                        p.op('tensor', lambda e, o=pb[0:128, 0:NC],
                             l=w2t[0:mw, 0:128], r=ht[0:mw, 0:NC],
                             s=True, t=True: e.matmul(o, l, r, start=s, stop=t))
                        p.op('vector', lambda e, o=acc[:, kc2 * N + n0:kc2 * N + n0 + NC],
                             a=acc[:, kc2 * N + n0:kc2 * N + n0 + NC], b=pb[0:128, 0:NC]:
                             e.tensor_add(o, a, b))
        for j in range(kch):
            p.dma('sync', OUT[j * 128:(j + 1) * 128, :], acc[:, j * N:(j + 1) * N])
        p.emit()
    return nc


def build_l0attn(T, Dm, Din, NST, debug=False):
    nc = bass.Bass("TRN2", target_bir_lowering=False)
    KON = nc.dram_tensor("KONST", [128, 16], F32, kind="ExternalInput").ap()
    X = nc.dram_tensor("X", [T, Dm], F32, kind="ExternalInput").ap()
    GB = nc.dram_tensor("GB", [128, 2 * Dm], F32, kind="ExternalInput").ap()
    IDN = nc.dram_tensor("IDN", [128, 128], F32, kind="ExternalInput").ap()
    WIN = nc.dram_tensor("WIN", [Dm, 2 * Din], F32, kind="ExternalInput").ap()
    CW = nc.dram_tensor("CW", [Din, 4], F32, kind="ExternalInput").ap()
    CBi = nc.dram_tensor("CB", [Din, 1], F32, kind="ExternalInput").ap()
    XPW = nc.dram_tensor("XPW", [Din, 96], F32, kind="ExternalInput").ap()
    DTW = nc.dram_tensor("DTW", [64, Din], F32, kind="ExternalInput").ap()
    DTB = nc.dram_tensor("DTB", [Din, 1], F32, kind="ExternalInput").ap()
    ALOG = nc.dram_tensor("ALOG", [Din, NST], F32, kind="ExternalInput").ap()
    DD = nc.dram_tensor("DD", [Din, 1], F32, kind="ExternalInput").ap()
    WOUT = nc.dram_tensor("WOUT", [Din, Dm], F32, kind="ExternalInput").ap()
    XN0 = nc.dram_tensor("XN0", [T, Dm], F32, kind="ExternalOutput").ap()
    XN0T = nc.dram_tensor("XN0T", [Dm, T], F32, kind="ExternalOutput").ap()
    ATT0 = nc.dram_tensor("ATT0", [Dm, T], F32, kind="ExternalOutput").ap()
    if debug:
        DBGU = nc.dram_tensor("DBGU", [Din, T], F32, kind="ExternalOutput").ap()
        DBGD = nc.dram_tensor("DBGD", [Din, T], F32, kind="ExternalOutput").ap()
        DBGY = nc.dram_tensor("DBGY", [Din, T], F32, kind="ExternalOutput").ap()
        DBGXI = nc.dram_tensor("DBGXI", [Dm, T], F32, kind="ExternalOutput").ap()
        DBGUP = nc.dram_tensor("DBGUP", [Din, T], F32, kind="ExternalOutput").ap()

    TC = 256
    nchk = T // TC
    ntile = TC // 128
    dmch = Dm // 128
    dich = Din // 128
    p = P(nc)
    import contextlib
    es = contextlib.ExitStack()
    with es:
        gb = es.enter_context(nc.sbuf_tensor("gb", [128, 2 * Dm], F32))
        idn = es.enter_context(nc.sbuf_tensor("idn", [128, 128], F32))
        xt = es.enter_context(nc.sbuf_tensor("xt", [128, ntile * Dm], F32))
        xn = es.enter_context(nc.sbuf_tensor("xn", [128, ntile * Dm], F32))
        xa = es.enter_context(nc.sbuf_tensor("xa", [128, ntile * Dm], F32))
        sq = es.enter_context(nc.sbuf_tensor("sq", [128, Dm], F32))
        c1 = es.enter_context(nc.sbuf_tensor("c1", [128, 8], F32))
        xint = es.enter_context(nc.sbuf_tensor("xint", [128, dmch * TC], F32))
        xnt = es.enter_context(nc.sbuf_tensor("xnt", [128, dmch * TC], F32))
        wt = es.enter_context(nc.sbuf_tensor("wt", [128, dmch * 128], F32))
        ci = es.enter_context(nc.sbuf_tensor("ci", [128, dich * (3 + TC)], F32))
        cwt = es.enter_context(nc.sbuf_tensor("cwt", [128, dich * 4], F32))
        cbt = es.enter_context(nc.sbuf_tensor("cbt", [128, dich], F32))
        ut = es.enter_context(nc.sbuf_tensor("ut", [128, dich * TC], F32))
        zs = es.enter_context(nc.sbuf_tensor("zs", [128, dich * TC], F32))
        xpw = es.enter_context(nc.sbuf_tensor("xpw", [128, dich * 96], F32))
        dbc = es.enter_context(nc.sbuf_tensor("dbc", [96, TC], F32))
        dtw = es.enter_context(nc.sbuf_tensor("dtw", [64, Din], F32))
        dtb = es.enter_context(nc.sbuf_tensor("dtb", [128, dich], F32))
        dlt = es.enter_context(nc.sbuf_tensor("dlt", [128, dich * TC], F32))
        t1 = es.enter_context(nc.sbuf_tensor("t1", [128, TC], F32))
        t2 = es.enter_context(nc.sbuf_tensor("t2", [128, TC], F32))
        an = es.enter_context(nc.sbuf_tensor("an", [128, dich * NST], F32))
        ddt = es.enter_context(nc.sbuf_tensor("ddt", [128, dich], F32))
        bb = es.enter_context(nc.sbuf_tensor("bb", [128, NST * TC], F32))
        cc = es.enter_context(nc.sbuf_tensor("cc", [128, NST * TC], F32))
        on = es.enter_context(nc.sbuf_tensor("on", [1, 128], F32))
        du = es.enter_context(nc.sbuf_tensor("du", [128, TC], F32))
        da = es.enter_context(nc.sbuf_tensor("da", [128, TC], F32))
        bt = es.enter_context(nc.sbuf_tensor("bt", [128, TC], F32))
        ht = es.enter_context(nc.sbuf_tensor("ht", [128, TC], F32))
        zt = es.enter_context(nc.sbuf_tensor("zt", [128, TC], F32))
        y1 = es.enter_context(nc.sbuf_tensor("y1", [128, TC], F32))
        y2 = es.enter_context(nc.sbuf_tensor("y2", [128, TC], F32))
        st = es.enter_context(nc.sbuf_tensor("st", [128, NST * dich], F32))
        yg = es.enter_context(nc.sbuf_tensor("yg", [128, dich * TC], F32))
        wo = es.enter_context(nc.sbuf_tensor("wo", [128, dich * 128], F32))
        ot = es.enter_context(nc.sbuf_tensor("ot", [128, TC], F32))
        kt = es.enter_context(nc.sbuf_tensor("kt", [128, 16], F32))
        ps = es.enter_context(nc.psum_tensor("ps", [128, TC], F32))
        pt = es.enter_context(nc.psum_tensor("pt", [128, 128], F32))
        psr = es.enter_context(nc.psum_tensor("psr", [1, 512], F32))
        rw = es.enter_context(nc.sbuf_tensor("rw", [1, 512], F32))
        carry = es.enter_context(nc.sbuf_tensor("carry", [128, 48], F32))
        p.ktile = kt
        p.ksrc = KON
        p.dma('sync', gb[:, :], GB[:, :])
        p.dma('sync', idn[:, :], IDN[:, :])
        p.op('vector', lambda e: e.memset(on[0:1, 0:128], 0.0))
        p.op('vector', lambda e: e.tensor_scalar(on[0:1, 0:128], on[0:1, 0:128],
                                                 1.0, None, ALU.add))
        for j in range(dich):
            p.dma('sync', cwt[:, j * 4:(j + 1) * 4], CW[j * 128:(j + 1) * 128, :])
            p.dma('sync', cbt[:, j:j + 1], CBi[j * 128:(j + 1) * 128, :])
            p.dma('sync', dtb[:, j:j + 1], DTB[j * 128:(j + 1) * 128, :])
            p.dma('sync', an[:, j * NST:(j + 1) * NST], ALOG[j * 128:(j + 1) * 128, :])
            p.dma('sync', ddt[:, j:j + 1], DD[j * 128:(j + 1) * 128, :])
            p.op('vector', lambda e, o=ci[:, j * (3 + TC):j * (3 + TC) + 3]:
                 e.memset(o, 0.0))
        p.op('scalar', lambda e, z=p.konst(0.0):
             e.activation(an[:, :], an[:, :], AF.Exp, bias=z))
        p.op('vector', lambda e: e.tensor_scalar(an[:, :], an[:, :], -1.0, None, ALU.mult))
        p.dma('sync', dtw[0:64, :], DTW[:, :])
        for j in range(dich):
            p.dma('sync', xpw[:, j * 96:(j + 1) * 96], XPW[j * 128:(j + 1) * 128, :])
        p.op('vector', lambda e: e.memset(st[:, :], 0.0))

        def lnpass(src, dst, tt):
            p.op('vector', lambda e, o=c1[:, 0:1], i=src:
                 e.reduce_sum(o, i, axis=AX.X))
            p.op('vector', lambda e, o=c1[:, 0:1], i=c1[:, 0:1]:
                 e.tensor_scalar(o, i, 1.0 / Dm, None, ALU.mult))
            p.op('scalar', lambda e, o=sq[:, :], i=src, z=p.konst(0.0):
                 e.activation(o, i, AF.Square, bias=z))
            p.op('vector', lambda e, o=c1[:, 1:2], i=sq[:, :]:
                 e.reduce_sum(o, i, axis=AX.X))
            p.op('vector', lambda e, o=c1[:, 1:2], i=c1[:, 1:2]:
                 e.tensor_scalar(o, i, 1.0 / Dm, None, ALU.mult))
            p.op('vector', lambda e, o=c1[:, 2:3], a=c1[:, 0:1], b=c1[:, 0:1]:
                 e.tensor_mul(o, a, b))
            p.op('vector', lambda e, o=c1[:, 1:2], a=c1[:, 1:2], b=c1[:, 2:3]:
                 e.tensor_sub(o, a, b))
            p.op('scalar', lambda e, o=c1[:, 3:4], i=c1[:, 1:2], b5=p.konst(1e-5):
                 e.activation(o, i, AF.Ln, bias=b5))
            p.op('scalar', lambda e, o=c1[:, 4:5], i=c1[:, 3:4], s=p.konst(-0.5),
                 z=p.konst(0.0):
                 e.activation(o, i, AF.Exp, bias=z, scale=s))
            p.op('vector', lambda e, o=dst, i=src, m=c1[:, 0:1], r=c1[:, 4:5]:
                 e.tensor_scalar(o, i, m, r, ALU.subtract, ALU.mult))

        for ck in range(nchk):
            t0 = ck * TC
            for tt in range(ntile):
                tb = t0 + tt * 128
                xsl = xt[:, tt * Dm:(tt + 1) * Dm]
                p.dma('sync', xsl, X[tb:tb + 128, :])
                lnpass(xsl, xn[:, tt * Dm:(tt + 1) * Dm], tt)
                lnpass(xn[:, tt * Dm:(tt + 1) * Dm], xa[:, tt * Dm:(tt + 1) * Dm], tt)
                p.op('vector', lambda e, o=xa[:, tt * Dm:(tt + 1) * Dm],
                     a=xa[:, tt * Dm:(tt + 1) * Dm], b=gb[:, 0:Dm]:
                     e.tensor_mul(o, a, b))
                p.op('vector', lambda e, o=xa[:, tt * Dm:(tt + 1) * Dm],
                     a=xa[:, tt * Dm:(tt + 1) * Dm], b=gb[:, Dm:2 * Dm]:
                     e.tensor_add(o, a, b))
                p.dma('sync', XN0[tb:tb + 128, :], xn[:, tt * Dm:(tt + 1) * Dm])
                for dj in range(dmch):
                    p.op('tensor', lambda e, o=pt[:, :],
                         i=xa[:, tt * Dm + dj * 128:tt * Dm + (dj + 1) * 128],
                         dd=idn[:, :]: e.transpose(o, i, dd))
                    p.op('vector', lambda e, o=xint[:, dj * TC + tt * 128:dj * TC + (tt + 1) * 128],
                         i=pt[:, :]: e.tensor_copy(o, i))
                    p.op('tensor', lambda e, o=pt[:, :],
                         i=xn[:, tt * Dm + dj * 128:tt * Dm + (dj + 1) * 128],
                         dd=idn[:, :]: e.transpose(o, i, dd))
                    p.op('vector', lambda e, o=xnt[:, dj * TC + tt * 128:dj * TC + (tt + 1) * 128],
                         i=pt[:, :]: e.tensor_copy(o, i))
            for dj in range(dmch):
                p.dma('sync', XN0T[dj * 128:(dj + 1) * 128, t0:t0 + TC],
                      xnt[:, dj * TC:(dj + 1) * TC])
            if ck > 0:
                for j in range(dich):
                    base = j * (3 + TC)
                    p.op('vector', lambda e, o=carry[:, j * 3:(j + 1) * 3],
                         i=ci[:, base + TC:base + TC + 3]: e.tensor_copy(o, i))
                for j in range(dich):
                    base = j * (3 + TC)
                    p.op('vector', lambda e, o=ci[:, base:base + 3],
                         i=carry[:, j * 3:(j + 1) * 3]: e.tensor_copy(o, i))
            for mi in range(2 * dich):
                m0 = mi * 128
                for j in range(dmch):
                    p.dma('sync', wt[:, j * 128:(j + 1) * 128],
                          WIN[j * 128:(j + 1) * 128, m0:m0 + 128])
                for j in range(dmch):
                    p.op('tensor', lambda e, o=ps[:, 0:TC], l=wt[:, j * 128:(j + 1) * 128],
                         r=xint[:, j * TC:(j + 1) * TC], s=(j == 0), t=(j == dmch - 1):
                         e.matmul(o, l, r, start=s, stop=t))
                if mi < dich:
                    p.op('vector', lambda e, o=ci[:, mi * (3 + TC) + 3:mi * (3 + TC) + 3 + TC],
                         i=ps[:, 0:TC]: e.tensor_copy(o, i))
                else:
                    mj = mi - dich
                    p.op('scalar', lambda e, o=zs[:, mj * TC:(mj + 1) * TC], i=ps[:, 0:TC],
                         z=p.konst(0.0): e.activation(o, i, AF.Silu, bias=z))
            for j in range(dich):
                base = j * (3 + TC)
                p.op('vector', lambda e, o=t1[:, 0:TC], i=ci[:, base:base + TC],
                     w=cwt[:, j * 4:j * 4 + 1]:
                     e.tensor_scalar(o, i, w, None, ALU.mult))
                for k in range(1, 4):
                    srcb, dstb = (t1, t2) if k % 2 == 1 else (t2, t1)
                    p.op('vector', lambda e, o=dstb[:, 0:TC], i=ci[:, base + k:base + k + TC],
                         w=cwt[:, j * 4 + k:j * 4 + k + 1], ac=srcb[:, 0:TC]:
                         e.scalar_tensor_tensor(o, i, w, ac, ALU.mult, ALU.add))
                p.op('scalar', lambda e, o=ut[:, j * TC:(j + 1) * TC], i=t2[:, 0:TC],
                     b=cbt[:, j:j + 1]: e.activation(o, i, AF.Silu, bias=b))
            for j in range(dich):
                p.op('tensor', lambda e, o=ps[0:64, 0:TC], l=xpw[:, j * 96:j * 96 + 64],
                     r=ut[:, j * TC:(j + 1) * TC], s=(j == 0), t=(j == dich - 1):
                     e.matmul(o, l, r, start=s, stop=t))
            p.op('vector', lambda e, o=dbc[0:64, :], i=ps[0:64, 0:TC]: e.tensor_copy(o, i))
            for n in range(2 * NST):
                col = 64 + n
                dstt = bb if n < NST else cc
                nn = n if n < NST else n - NST
                for j in range(dich):
                    p.op('tensor', lambda e, o=psr[0:1, 0:TC],
                         l=xpw[:, j * 96 + col:j * 96 + col + 1],
                         r=ut[:, j * TC:(j + 1) * TC], s=(j == 0), t=(j == dich - 1):
                         e.matmul(o, l, r, start=s, stop=t))
                p.op('vector', lambda e, o=rw[0:1, 0:TC], i=psr[0:1, 0:TC]:
                     e.tensor_copy(o, i))
                for half in range(TC // 128):
                    h0 = half * 128
                    p.op('tensor', lambda e, o=pt[:, :], l=on[0:1, 0:128],
                         r=rw[0:1, h0:h0 + 128]:
                         e.matmul(o, l, r, start=True, stop=True))
                    p.op('vector', lambda e, o=dstt[:, nn * TC + h0:nn * TC + h0 + 128],
                         i=pt[:, :]: e.tensor_copy(o, i))
            for mi in range(dich):
                p.op('tensor', lambda e, o=ps[:, 0:TC], l=dtw[0:64, mi * 128:(mi + 1) * 128],
                     r=dbc[0:64, 0:TC]: e.matmul(o, l, r, start=True, stop=True))
                p.op('vector', lambda e, o=t1[:, 0:TC], i=ps[:, 0:TC], b=dtb[:, mi:mi + 1]:
                     e.tensor_scalar(o, i, b, None, ALU.add))
                p.op('scalar', lambda e, o=t2[:, 0:TC], i=t1[:, 0:TC], z=p.konst(0.0):
                     e.activation(o, i, AF.Abs, bias=z))
                p.op('scalar', lambda e, o=t2[:, 0:TC], i=t2[:, 0:TC],
                     s=p.konst(-1.0), z=p.konst(0.0):
                     e.activation(o, i, AF.Exp, bias=z, scale=s))
                p.op('scalar', lambda e, o=t2[:, 0:TC], i=t2[:, 0:TC], b=p.konst(1.0):
                     e.activation(o, i, AF.Ln, bias=b))
                p.op('vector', lambda e, o=t1[:, 0:TC], i=t1[:, 0:TC]:
                     e.tensor_scalar(o, i, 0.0, None, ALU.max))
                p.op('vector', lambda e, o=dlt[:, mi * TC:(mi + 1) * TC],
                     a=t1[:, 0:TC], b=t2[:, 0:TC]: e.tensor_add(o, a, b))
            for j in range(dich):
                p.op('vector', lambda e, o=du[:, 0:TC], a=dlt[:, j * TC:(j + 1) * TC],
                     b=ut[:, j * TC:(j + 1) * TC]: e.tensor_mul(o, a, b))
                ys = [y1, y2]
                for n in range(NST):
                    p.op('scalar', lambda e, o=da[:, 0:TC], i=dlt[:, j * TC:(j + 1) * TC],
                         s=an[:, j * NST + n:j * NST + n + 1], z=p.konst(0.0):
                         e.activation(o, i, AF.Exp, bias=z, scale=s))
                    p.op('vector', lambda e, o=bt[:, 0:TC], a=du[:, 0:TC],
                         b=bb[:, n * TC:(n + 1) * TC]: e.tensor_mul(o, a, b))
                    p.op('vector', lambda e, o=ht[:, 0:TC], d0_=da[:, 0:TC], d1=bt[:, 0:TC],
                         ini=st[:, j * NST + n:j * NST + n + 1]:
                         e.tensor_tensor_scan(o, d0_, d1, ini, ALU.mult, ALU.add))
                    p.op('vector', lambda e, o=st[:, j * NST + n:j * NST + n + 1],
                         i=ht[:, TC - 1:TC]: e.tensor_copy(o, i))
                    p.op('vector', lambda e, o=zt[:, 0:TC], a=ht[:, 0:TC],
                         b=cc[:, n * TC:(n + 1) * TC]: e.tensor_mul(o, a, b))
                    if n == 0:
                        p.op('vector', lambda e, o=ys[0][:, 0:TC], i=zt[:, 0:TC]:
                             e.tensor_copy(o, i))
                    else:
                        srcb, dstb = ys[(n + 1) % 2], ys[n % 2]
                        p.op('vector', lambda e, o=dstb[:, 0:TC], a=srcb[:, 0:TC],
                             b=zt[:, 0:TC]: e.tensor_add(o, a, b))
                fin = ys[(NST - 1) % 2]
                p.op('vector', lambda e, o=fin[:, 0:TC], i=ut[:, j * TC:(j + 1) * TC],
                     w=ddt[:, j:j + 1], ac=fin[:, 0:TC]:
                     e.scalar_tensor_tensor(o, i, w, ac, ALU.mult, ALU.add))
                p.op('vector', lambda e, o=yg[:, j * TC:(j + 1) * TC], a=fin[:, 0:TC],
                     b=zs[:, j * TC:(j + 1) * TC]: e.tensor_mul(o, a, b))
            if debug:
                for j in range(dmch):
                    p.dma('sync', DBGXI[j * 128:(j + 1) * 128, t0:t0 + TC],
                          xint[:, j * TC:(j + 1) * TC])
                for j in range(dich):
                    p.dma('sync', DBGUP[j * 128:(j + 1) * 128, t0:t0 + TC],
                          ci[:, j * (3 + TC) + 3:j * (3 + TC) + 3 + TC])
                for j in range(dich):
                    p.dma('sync', DBGU[j * 128:(j + 1) * 128, t0:t0 + TC],
                          ut[:, j * TC:(j + 1) * TC])
                    p.dma('sync', DBGD[j * 128:(j + 1) * 128, t0:t0 + TC],
                          dlt[:, j * TC:(j + 1) * TC])
                    p.dma('sync', DBGY[j * 128:(j + 1) * 128, t0:t0 + TC],
                          yg[:, j * TC:(j + 1) * TC])
            for mi in range(dmch):
                m0 = mi * 128
                for j in range(dich):
                    p.dma('sync', wo[:, j * 128:(j + 1) * 128],
                          WOUT[j * 128:(j + 1) * 128, m0:m0 + 128])
                for j in range(dich):
                    p.op('tensor', lambda e, o=ps[:, 0:TC], l=wo[:, j * 128:(j + 1) * 128],
                         r=yg[:, j * TC:(j + 1) * TC], s=(j == 0), t=(j == dich - 1):
                         e.matmul(o, l, r, start=s, stop=t))
                p.op('vector', lambda e, o=ot[:, 0:TC], i=ps[:, 0:TC]: e.tensor_copy(o, i))
                p.dma('sync', ATT0[m0:m0 + 128, t0:t0 + TC], ot[:, 0:TC])
        p.emit()
    return nc


def build_gatefuse(T, Dm, NH):
    GIN = Dm + 16
    nc = bass.Bass("TRN2", target_bir_lowering=False)
    KON = nc.dram_tensor("KONST", [128, 16], F32, kind="ExternalInput").ap()
    R = nc.dram_tensor("R", [Dm, T], F32, kind="ExternalInput").ap()
    STT = nc.dram_tensor("STT", [NH * 16, T], F32, kind="ExternalInput").ap()
    WG1 = nc.dram_tensor("WG1", [GIN, Dm], F32, kind="ExternalInput").ap()
    BG1 = nc.dram_tensor("BG1", [Dm, 1], F32, kind="ExternalInput").ap()
    WG2 = nc.dram_tensor("WG2", [Dm, 4], F32, kind="ExternalInput").ap()
    BG2 = nc.dram_tensor("BG2", [4, 1], F32, kind="ExternalInput").ap()
    LG = nc.dram_tensor("LG", [4 * NH, T], F32, kind="ExternalOutput").ap()
    NC = 512
    kch = Dm // 128
    nch = T // NC
    C = float(np.sqrt(2.0 / np.pi))
    p = P(nc)
    import contextlib
    es = contextlib.ExitStack()
    with es:
        rt = es.enter_context(nc.sbuf_tensor("rt", [128, kch * T], F32))
        st2 = es.enter_context(nc.sbuf_tensor("st2", [16, T], F32))
        w1t = es.enter_context(nc.sbuf_tensor("w1t", [128, (kch + 1) * 128], F32))
        w2t = es.enter_context(nc.sbuf_tensor("w2t", [128, kch * 4], F32))
        b1t = es.enter_context(nc.sbuf_tensor("b1t", [128, kch], F32))
        b2t = es.enter_context(nc.sbuf_tensor("b2t", [4, 1], F32))
        hg = es.enter_context(nc.sbuf_tensor("hg", [128, kch * NC], F32))
        x2 = es.enter_context(nc.sbuf_tensor("x2", [128, NC], F32))
        tt = es.enter_context(nc.sbuf_tensor("tt", [128, NC], F32))
        lgo = es.enter_context(nc.sbuf_tensor("lgo", [4, NC], F32))
        kt = es.enter_context(nc.sbuf_tensor("kt", [128, 16], F32))
        ps = es.enter_context(nc.psum_tensor("ps", [128, NC], F32))
        p2 = es.enter_context(nc.psum_tensor("p2", [4, NC], F32))
        p.ktile = kt
        p.ksrc = KON
        for j in range(kch):
            p.dma('sync', rt[:, j * T:(j + 1) * T], R[j * 128:(j + 1) * 128, :])
            p.dma('sync', b1t[:, j:j + 1], BG1[j * 128:(j + 1) * 128, :])
            p.dma('sync', w2t[:, j * 4:(j + 1) * 4], WG2[j * 128:(j + 1) * 128, :])
        p.dma('sync', b2t[:, :], BG2[:, :])
        for h in range(NH):
            p.dma('sync', st2[:, :], STT[h * 16:(h + 1) * 16, :])
            for ni in range(nch):
                n0 = ni * NC
                for mi in range(kch):
                    m0 = mi * 128
                    for j in range(kch):
                        p.dma('sync', w1t[:, j * 128:(j + 1) * 128],
                              WG1[j * 128:(j + 1) * 128, m0:m0 + 128])
                    p.dma('sync', w1t[0:16, kch * 128:kch * 128 + 128],
                          WG1[Dm:GIN, m0:m0 + 128])
                    for j in range(kch):
                        p.op('tensor', lambda e, o=ps[:, 0:NC],
                             l=w1t[:, j * 128:(j + 1) * 128],
                             r=rt[:, j * T + n0:j * T + n0 + NC],
                             s=(j == 0): e.matmul(o, l, r, start=s, stop=False))
                    p.op('tensor', lambda e, o=ps[:, 0:NC],
                         l=w1t[0:16, kch * 128:kch * 128 + 128],
                         r=st2[0:16, n0:n0 + NC]:
                         e.matmul(o, l, r, start=False, stop=True))
                    p.op('vector', lambda e, o=hg[:, mi * NC:(mi + 1) * NC],
                         i=ps[:, 0:NC], b=b1t[:, mi:mi + 1]:
                         e.tensor_scalar(o, i, b, None, ALU.add))
                    xs = hg[:, mi * NC:(mi + 1) * NC]
                    p.op('scalar', lambda e, o=x2[:, 0:NC], i=xs, z=p.konst(0.0):
                         e.activation(o, i, AF.Square, bias=z))
                    p.op('vector', lambda e, o=x2[:, 0:NC], i=x2[:, 0:NC]:
                         e.tensor_scalar(o, i, C * 0.044715, C, ALU.mult, ALU.add))
                    p.op('vector', lambda e, o=tt[:, 0:NC], a=x2[:, 0:NC], b=xs:
                         e.tensor_mul(o, a, b))
                    p.op('scalar', lambda e, o=tt[:, 0:NC], i=tt[:, 0:NC],
                         z=p.konst(0.0): e.activation(o, i, AF.Tanh, bias=z))
                    p.op('vector', lambda e, o=tt[:, 0:NC], i=tt[:, 0:NC]:
                         e.tensor_scalar(o, i, 1.0, 0.5, ALU.add, ALU.mult))
                    p.op('vector', lambda e, o=xs, a=tt[:, 0:NC], b=xs:
                         e.tensor_mul(o, a, b))
                for j in range(kch):
                    p.op('tensor', lambda e, o=p2[0:4, 0:NC], l=w2t[:, j * 4:(j + 1) * 4],
                         r=hg[:, j * NC:(j + 1) * NC], s=(j == 0), t=(j == kch - 1):
                         e.matmul(o, l, r, start=s, stop=t))
                p.op('vector', lambda e, o=lgo[0:4, 0:NC], i=p2[0:4, 0:NC],
                     b=b2t[0:4, 0:1]: e.tensor_scalar(o, i, b, None, ALU.add))
                p.dma('sync', LG[h * 4:(h + 1) * 4, n0:n0 + NC], lgo[0:4, 0:NC])
        p.emit()
    return nc

def _threefry2x32(k1, k2, c1, c2):
    rotations = [np.array([13, 15, 26, 6], np.uint32),
                 np.array([17, 29, 16, 24], np.uint32)]
    ks = [np.uint32(k1), np.uint32(k2),
          np.uint32(k1) ^ np.uint32(k2) ^ np.uint32(0x1BD11BDA)]
    x = [c1 + ks[0], c2 + ks[1]]

    def rotl(v, r):
        return (v << np.uint32(r)) | (v >> np.uint32(32 - r))

    for i in range(5):
        for r in rotations[i % 2]:
            x[0] = x[0] + x[1]
            x[1] = rotl(x[1], r)
            x[1] = x[1] ^ x[0]
        x[0] = x[0] + ks[(i + 1) % 3]
        x[1] = x[1] + ks[(i + 2) % 3] + np.uint32(i + 1)
    return x


def _keep_mask(seed, p, shape):
    n = int(np.prod(shape))
    with np.errstate(over='ignore'):
        idx = np.arange(n, dtype=np.uint64)
        c1 = (idx >> np.uint64(32)).astype(np.uint32)
        c2 = (idx & np.uint64(0xFFFFFFFF)).astype(np.uint32)
        x0, x1 = _threefry2x32(np.uint32(seed >> 32), np.uint32(seed & 0xFFFFFFFF), c1, c2)
        bits = x0 ^ x1
    fl = ((bits >> np.uint32(9)) | np.uint32(0x3F800000)).view(np.float32) - np.float32(1.0)
    return (fl < np.float32(p)).reshape(shape)


_CACHE = {}


def _prog(name, builder, *args, **kw):
    key = (name,) + tuple(args) + tuple(sorted(kw.items()))
    if key not in _CACHE:
        _CACHE[key] = builder(*args, **kw)
    return _CACHE[key]


def _launch(nc, maps):
    for m in maps:
        m["KONST"] = kinput(nc)
    res = run_bass_kernel_spmd(nc, maps, core_ids=list(range(8)))
    return res.results


def _c(a):
    return np.ascontiguousarray(np.asarray(a, dtype=np.float32))


def kernel(**inputs):
    g = {k: np.asarray(v, dtype=np.float32) for k, v in inputs.items()}
    T, Dm, Din, NH, DK, DV, E, I = 2048, 1024, 2048, 4, 256, 256, 7, 2816
    NST = 16

    x_all = g['L0_attn_x']
    keep = _keep_mask(42, 0.8, (8, T, Dm))
    mask125 = keep.astype(np.float32) * np.float32(1.25)

    GB0 = _c(np.concatenate([np.tile(g['L0_attn_ln_g'], (128, 1)),
                             np.tile(g['L0_attn_ln_b'], (128, 1))], 1))

    IDN = np.eye(128, dtype=np.float32)
    nc = _prog('l0attn', build_l0attn, T, Dm, Din, NST)
    maps = []
    for c in range(8):
        maps.append({
            "X": _c(x_all[c]), "GB": GB0, "IDN": IDN,
            "WIN": _c(g['L0_attn_in_proj_w']), "CW": _c(g['L0_attn_conv_w']),
            "CB": _c(g['L0_attn_conv_b'].reshape(-1, 1)),
            "XPW": _c(g['L0_attn_x_proj_w']),
            "DTW": _c(g['L0_attn_dt_proj_w']),
            "DTB": _c(g['L0_attn_dt_proj_b'].reshape(-1, 1)),
            "ALOG": _c(g['L0_attn_A_log']), "DD": _c(g['L0_attn_D'].reshape(-1, 1)),
            "WOUT": _c(g['L0_attn_out_proj_w'])})
    r = _launch(nc, maps)
    XN0 = [r[c]["XN0"] for c in range(8)]
    XN0T = [r[c]["XN0T"] for c in range(8)]
    ATT0 = [r[c]["ATT0"] for c in range(8)]

    nc = _prog('q1', build_matmul, Dm, 4096, T, act='relu', bias=True, f8=True,
               out_bf16_round=True)
    W = _c(g['L0_ffn_W1'].T)
    B = _c(g['L0_ffn_b1'].reshape(-1, 1))
    r = _launch(nc, [{"W": W, "R": _c(XN0T[c]), "B": B} for c in range(8)])
    H1 = [r[c]["OUT"] for c in range(8)]

    nc = _prog('q2', build_matmul, 4096, Dm, T, bias=True, f8=True,
               out_bf16_round=True, mulT=True)
    W = _c(g['L0_ffn_W2'].T)
    B = _c(g['L0_ffn_b2'].reshape(-1, 1))
    r = _launch(nc, [{"W": W, "R": _c(H1[c]), "B": B,
                      "MT": _c(mask125[c].T)} for c in range(8)])
    FFN0 = [r[c]["OUT"] for c in range(8)]

    nc = _prog('comb4', build_combine, T, Dm, 4)
    r = _launch(nc, [{"X0": _c(x_all[c]), "X1": _c(XN0[c]), "X2": _c(ATT0[c].T),
                      "X3": _c(FFN0[c].T)} for c in range(8)])
    X1 = [r[c]["OUT"] for c in range(8)]

    nc = _prog('lnorm', build_lnorm, T, Dm)
    r = _launch(nc, [{"X": _c(X1[c])} for c in range(8)])
    XN1 = [r[c]["XN"] for c in range(8)]
    XN1T = [_c(XN1[c].T) for c in range(8)]

    nc = _prog('proj1024', build_matmul, Dm, Dm, T)
    QP, KP, VP = [], [], []
    for wname, dst in (('L1_attn_Wq', QP), ('L1_attn_Wk', KP), ('L1_attn_Wv', VP)):
        W = _c(g[wname].T)
        r = _launch(nc, [{"W": W, "R": XN1T[c]} for c in range(8)])
        dst.extend(r[c]["OUT"] for c in range(8))

    nc = _prog('conv1', build_conv, Dm, T, 4, act='silu', bias=False)
    QC, KC, VC = [], [], []
    for wname, src, dst in (('L1_attn_conv_q_w', QP, QC), ('L1_attn_conv_k_w', KP, KC),
                            ('L1_attn_conv_v_w', VP, VC)):
        W = _c(g[wname])
        r = _launch(nc, [{"X": _c(src[c]), "W": W} for c in range(8)])
        dst.extend(r[c]["OUT"] for c in range(8))

    nc = _prog('wb', build_matmul, Dm, NH, T, act='sigmoid')
    W = _c(g['L1_attn_Wb'].T)
    r = _launch(nc, [{"W": W, "R": XN1T[c]} for c in range(8)])
    BETA = [_c(r[c]["OUT"].T) for c in range(8)]

    nc = _prog('l2s', build_l2scale, T, NH, DK)
    r = _launch(nc, [{"QT": _c(QC[c].T), "KT": _c(KC[c].T), "VT": _c(VC[c].T),
                      "BETA": BETA[c]} for c in range(8)])
    QN = [r[c]["QN"] for c in range(8)]
    KN = [r[c]["KN"] for c in range(8)]
    VB = [r[c]["VB"] for c in range(8)]
    KB = [r[c]["KB"] for c in range(8)]

    CH = 32
    nch = T // CH
    sl_m = np.tril(np.ones((CH, CH), np.float32), -1)
    lo_m = np.tril(np.ones((CH, CH), np.float32))
    MSK = _c(np.concatenate([sl_m, sl_m.T, lo_m.T, np.eye(CH, dtype=np.float32)], 1))
    nc = _prog('delta', build_delta, T, DK, DV, CH)
    OD = [np.zeros((T, NH * DV), np.float32) for _ in range(8)]
    for h in range(NH):
        hs = slice(h * DK, (h + 1) * DK)
        maps = []
        for c in range(8):
            maps.append({
                "QF": _c(QN[c][:, hs].T), "KF": _c(KN[c][:, hs].T),
                "KBF": _c(KB[c][:, hs].T),
                "KTC": _c(KN[c][:, hs].reshape(nch, CH, DK)),
                "KBTC": _c(KB[c][:, hs].reshape(nch, CH, DK)),
                "VTC": _c(VB[c][:, hs].reshape(nch, CH, DV)),
                "MSK": MSK})
        r = _launch(nc, maps)
        for c in range(8):
            OD[c][:, hs] = r[c]["O"].reshape(T, DV)

    nc = _prog('fir5', build_conv, Dm, T, 5)
    W = _c(g['L1_attn_fir_short_filt'].reshape(Dm, 5))
    r = _launch(nc, [{"X": _c(VC[c]), "W": W} for c in range(8)])
    FS = [r[c]["OUT"] for c in range(8)]

    nc = _prog('fir64', build_conv, Dm, T, 64)
    W = _c(g['L1_attn_fir_long_filt'].reshape(Dm, 64))
    r = _launch(nc, [{"X": _c(VC[c]), "W": W} for c in range(8)])
    FL = [r[c]["OUT"] for c in range(8)]

    nc = _prog('stats', build_stats, T, NH, DV)
    r = _launch(nc, [{"FS": _c(FS[c].T), "FL": _c(FL[c].T), "OD": _c(OD[c]),
                      "VD": _c(VC[c].T)} for c in range(8)])
    ST = [r[c]["ST"] for c in range(8)]

    nc = _prog('gatefuse', build_gatefuse, T, Dm, NH)
    maps = [{"R": XN1T[c], "STT": _c(ST[c].T), "WG1": _c(g['L1_attn_gate_W1'].T),
             "BG1": _c(g['L1_attn_gate_b1'].reshape(-1, 1)),
             "WG2": _c(g['L1_attn_gate_W2'].T),
             "BG2": _c(g['L1_attn_gate_b2'].reshape(-1, 1))} for c in range(8)]
    r = _launch(nc, maps)
    LG = [_c(r[c]["LG"].T) for c in range(8)]

    nc = _prog('gsoft', build_gate_softmax, T, NH)
    CB = _c(np.tile(g['L1_attn_gate_copy_bias'], (128, 1)))
    LTt = _c(np.tile(g['L1_attn_gate_log_temp'], (128, 1)))
    r = _launch(nc, [{"LG": _c(LG[c]), "CB": CB, "LT": LTt} for c in range(8)])
    WGT = [r[c]["WGT"] for c in range(8)]

    nc = _prog('mix', build_mix, T, NH, DV)
    OWB = _c(np.tile(np.tile(g['L1_attn_o_norm_w'], NH), (128, 1)))
    r = _launch(nc, [{"FS": _c(FS[c].T), "FL": _c(FL[c].T), "OD": _c(OD[c]),
                      "VD": _c(VC[c].T), "WGT": _c(WGT[c]), "OWB": OWB}
                     for c in range(8)])
    OMIX = [r[c]["OMIX"] for c in range(8)]

    nc = _prog('proj1024', build_matmul, Dm, Dm, T)
    W = _c(g['L1_attn_Wo'].T)
    r = _launch(nc, [{"W": W, "R": _c(OMIX[c].T)} for c in range(8)])
    ATT1 = [r[c]["OUT"] for c in range(8)]

    nc = _prog('routerm', build_matmul, Dm, E, T, act='sigmoid', bias=True,
               clip50=True)
    W = _c(g['L1_ffn_router_w'].T)
    B = _c(g['L1_ffn_routing_bias'].reshape(-1, 1))
    r = _launch(nc, [{"W": W, "R": XN1T[c], "B": B} for c in range(8)])
    PRF = [r[c]["OUT"] for c in range(8)]

    nc = _prog('rsel', build_router, T, E)
    r = _launch(nc, [{"PR": _c(PRF[c].T)} for c in range(8)])
    WS = [r[c]["WS"] for c in range(8)]
    SH = [r[c]["SH"] for c in range(8)]

    nc = _prog('moe', build_moe, Dm, I, T, E)
    W1S = _c(np.concatenate([g['L1_ffn_sw1'].T[None]]
                            + [g['L1_ffn_ew1'][e].T[None] for e in range(E)], 0
                            ).reshape(-1, I))
    W3S = _c(np.concatenate([g['L1_ffn_sw3'].T[None]]
                            + [g['L1_ffn_ew3'][e].T[None] for e in range(E)], 0
                            ).reshape(-1, I))
    W2S = _c(np.concatenate([g['L1_ffn_sw2'].T[None]]
                            + [g['L1_ffn_ew2'][e].T[None] for e in range(E)], 0
                            ).reshape(-1, Dm))
    maps = []
    for c in range(8):
        scs = np.concatenate([SH[c].reshape(1, T), WS[c].T], 0)
        maps.append({"R": XN1T[c], "SCS": _c(scs), "W1S": W1S, "W3S": W3S,
                     "W2S": W2S})
    r = _launch(nc, maps)
    ACC = [r[c]["OUT"] for c in range(8)]

    nc = _prog('comb3', build_combine, T, Dm, 3)
    r = _launch(nc, [{"X0": _c(X1[c]), "X1": _c(ATT1[c].T), "X2": _c(ACC[c].T)}
                     for c in range(8)])
    out = np.stack([r[c]["OUT"] for c in range(8)], 0).astype(np.float32)
    return out
